# version 02
# baseline (speedup 1.0000x reference)
import numpy as np
from contextlib import ExitStack, contextmanager
import concourse.bass as bass
import concourse.mybir as mybir
import concourse.bass_utils as bass_utils

F32 = mybir.dt.float32
AXX = mybir.AxisListType.X
ALU = mybir.AluOpType
ACTF = mybir.ActivationFunctionType
TS = bass.ts

D = 1024
S = 4096
NT = 32
TILE_B = 128 * D

ACTIVE = (0, 1, 2, 3)
B3_STAGE = 4
B1_STAGE = 5

INPUT_SPECS = {
    "L0_attn_filt": (2048,), "L0_ffn_expert_sel": (16, 1024),
    "L0_ffn_keys": (16, 1024, 256), "L0_ffn_values": (16, 256, 1024),
    "L0_ffn_route_scale": (1,),
    "L1_attn_Wq": (1024, 1024), "L1_attn_Wk": (1024, 1024),
    "L1_attn_Wv": (1024, 1024), "L1_attn_Wo": (1024, 1024),
    "L1_attn_W1": (1024, 512), "L1_attn_b1": (512,),
    "L1_attn_W2": (512, 1024), "L1_attn_b2": (1024,),
    "L1_ffn_gate_w": (2, 1024), "L1_ffn_w1": (2, 1024, 4096),
    "L1_ffn_b1": (2, 4096), "L1_ffn_w2": (2, 4096, 1024),
    "L1_ffn_b2": (2, 1024), "L1_ffn_sw1": (1024, 4096),
    "L1_ffn_sb1": (4096,), "L1_ffn_sw2": (4096, 1024), "L1_ffn_sb2": (1024,),
    "L2_attn_Wq": (1024, 1024), "L2_attn_bq": (1024,),
    "L2_attn_Wk": (1024, 1024), "L2_attn_bk": (1024,),
    "L2_attn_Wv": (1024, 1024), "L2_attn_bv": (1024,),
    "L2_attn_Wo": (1024, 1024), "L2_attn_bo": (1024,),
    "L2_attn_beta": (1,),
    "L2_ffn_gate_w": (16, 1024), "L2_ffn_expert_bias": (16,),
    "L2_ffn_w_gate": (16, 1024, 1024), "L2_ffn_w_up": (16, 1024, 1024),
    "L2_ffn_w_down": (16, 1024, 1024), "L2_ffn_shared_gate_w": (1024, 1024),
    "L2_ffn_shared_up_w": (1024, 1024), "L2_ffn_shared_down_w": (1024, 1024),
    "L3_attn_K": (4096, 1024), "L3_attn_V": (4096, 1024),
    "L3_ffn_weight": (1024, 1024), "L3_ffn_alpha": (1,), "L3_ffn_bias": (1024,),
    "toep": (17, 128, 128),
}


class Arena:
    def __init__(self, ap, ncols):
        self.base_ap = ap
        self.ncols = ncols
        self.off = 0

    def take(self, cols):
        assert self.off + cols <= self.ncols, (self.off, cols, self.ncols)
        v = self.base_ap[:, self.off:self.off + cols]
        self.off += cols
        return v

    def reset(self):
        self.off = 0

    def mark(self):
        return self.off

    def rewind(self, m):
        self.off = m


class P:
    def __init__(self, nc, es):
        self.nc = nc
        self.c = {}
        self.sem = {}
        for name in ("ds", "dg", "tt", "vv", "ss"):
            self.sem[name] = es.enter_context(nc.semaphore(name))
            self.c[name] = 0
        self.lp = None
        self.inputs = {}

    def inp(self, name):
        if name not in self.inputs:
            self.inputs[name] = self.nc.dram_tensor(
                name, list(INPUT_SPECS[name]), F32, kind="ExternalInput")
        return self.inputs[name].ap()

    def dma(self, q, dst, src, slow=False):
        eng = self.nc.sync if q == "ds" else self.nc.gpsimd
        eng.dma_start(dst, src,
                      allow_slow_non_contiguous=slow).then_inc(self.sem[q], 16)
        self.c[q] += 16

    def inc(self, inst, s, n=1):
        inst.then_inc(self.sem[s], n)
        self.c[s] += n
        return inst

    def tgt(self, s, lag=0):
        return self.c[s] - lag

    def w(self, eng, s, lag=0):
        eng.wait_ge(self.sem[s], self.tgt(s, lag))

    def wall(self, s):
        nc = self.nc
        for eng in (nc.tensor, nc.vector, nc.scalar, nc.sync, nc.gpsimd):
            self.w(eng, s)

    def quiesce(self):
        self.wall("ds")
        self.wall("dg")
        self.nc.all_engine_barrier()

    def bar(self):
        self.nc.all_engine_barrier()

    def clear_all(self):
        for s in self.sem:
            self.nc.sync.sem_clear(self.sem[s])
        for s in self.c:
            self.c[s] = 0

    @contextmanager
    def loop(self, n):
        assert self.lp is None
        self.quiesce()
        self.clear_all()
        self.bar()
        ctx = self.nc.Fori(0, n)
        i = ctx.__enter__()
        self.lp = {"i": i}
        yield i
        for s in self.sem:
            self.wall(s)
        self.bar()
        self.clear_all()
        self.bar()
        ctx.__exit__(None, None, None)
        self.lp = None

    def transposes(self, pairs):
        nc = self.nc
        for j, (srcap, dstap) in enumerate(pairs):
            sl = self.psT_slots[j % 2]
            self.w(nc.tensor, "vv", lag=0 if j == 0 else 1)
            self.inc(nc.tensor.transpose(sl, srcap, self.ident), "tt")
            self.w(nc.vector, "tt")
            self.inc(nc.vector.tensor_copy(dstap, sl), "vv")

    def ln(self, xg, xng, sq, n):
        nc = self.nc
        st = self.stats
        x3 = bass.AP(xg.tensor, xg.offset, [xg.ap[0], [D, n], [1, D]])
        xn3 = bass.AP(xng.tensor, xng.offset, [xng.ap[0], [D, n], [1, D]])
        s_sum, s_var = st[:, 0:n], st[:, 8:8 + n]
        s_rs, s_t = st[:, 16:16 + n], st[:, 24:24 + n]
        self.inc(nc.vector.reduce_sum(s_sum, x3, axis=AXX), "vv")
        self.w(nc.vector, "vv")
        self.inc(nc.vector.tensor_scalar_mul(s_t, s_sum, -1.0 / D), "vv")
        self.w(nc.vector, "vv")
        nm_b = bass.AP(st.tensor, s_t.offset, [s_t.ap[0], [1, n], [0, D]])
        self.inc(nc.vector.tensor_tensor(xn3, x3, nm_b, op=ALU.add), "vv")
        self.w(nc.scalar, "vv")
        for t in range(n):
            self.inc(nc.scalar.activation(
                sq[:, t * D:(t + 1) * D], xng[:, t * D:(t + 1) * D],
                ACTF.Square, accum_out=s_var[:, t:t + 1]), "ss")
        self.w(nc.vector, "ss")
        self.inc(nc.vector.tensor_scalar(s_t, s_var, 1.0 / D, 1e-5,
                                         op0=ALU.mult, op1=ALU.add), "vv")
        self.w(nc.scalar, "vv")
        self.inc(nc.scalar.activation(s_var, s_t, ACTF.Ln), "ss")
        self.w(nc.scalar, "ss")
        self.inc(nc.scalar.activation(s_rs, s_var, ACTF.Exp, scale=-0.5), "ss")
        self.w(nc.vector, "ss")
        rs_b = bass.AP(st.tensor, s_rs.offset, [s_rs.ap[0], [1, n], [0, D]])
        self.inc(nc.vector.tensor_tensor(xn3, xn3, rs_b, op=ALU.mult), "vv")


def ap2d(ap, n):
    return bass.AP(ap.tensor, ap.offset, [[n, 1], [1, n]])


def hbm_tiles(t, i, ntile=1):
    if ntile == 1:
        return bass.AP(t.tensor, t.offset + i * TILE_B, [[D, 128], [1, D]])
    return bass.AP(t.tensor, t.offset + i * TILE_B,
                   [[D, 128], [128 * D, ntile], [1, D]])


def fm_cols(t, pc, width, X):
    return bass.AP(t.tensor, t.offset + pc * width,
                   [[X, 128], [128 * X, 8], [1, width]])


def build():
    es = ExitStack()
    nc = bass.Bass("TRN2", target_bir_lowering=False,
                   detect_race_conditions=False)
    p = P(nc, es)

    x0 = nc.dram_tensor("x0", [S, D], F32, kind="ExternalInput").ap()
    out = nc.dram_tensor("out", [S, D], F32, kind="ExternalOutput").ap()
    xa = nc.dram_tensor("xa", [S, D], F32, kind="Internal").ap()
    xb = nc.dram_tensor("xb", [S, D], F32, kind="Internal").ap()
    spill = nc.dram_tensor("spill", [S, D], F32, kind="Internal").ap()
    aux = nc.dram_tensor("aux", [1024, 4096], F32, kind="Internal").ap()
    kT_hbm = nc.dram_tensor("kT_sc", [D, S], F32, kind="Internal").ap()

    ident_h = nc.inline_tensor(np.eye(128, dtype=np.float32), "ident_c")
    onesr_h = nc.inline_tensor(np.ones((1, 128), dtype=np.float32), "onesr_c")
    onesc_h = nc.inline_tensor(np.ones((128, 1), dtype=np.float32), "onesc_c")
    q_ = np.arange(128)
    tri_h = nc.inline_tensor(
        np.where(q_[:, None] >= q_[None, :], 0.0, -3e38).astype(np.float32), "tri_c")
    sm_h = nc.inline_tensor(
        (q_[:, None] <= q_[None, :]).astype(np.float32), "smask_c")

    p.ident = nc.alloc_sbuf_tensor("ident", [128, 128], F32).ap()
    p.onesr = nc.alloc_sbuf_tensor("onesr", [1, 128], F32).ap()
    p.onesc = nc.alloc_sbuf_tensor("onesc", [128, 1], F32).ap()
    p.tri = nc.alloc_sbuf_tensor("tri", [128, 128], F32).ap()
    p.smask = nc.alloc_sbuf_tensor("smask", [128, 128], F32).ap()
    p.stats = nc.alloc_sbuf_tensor("stats", [128, 128], F32).ap()
    p.srow = nc.alloc_sbuf_tensor("srow", [1, 8192], F32).ap()
    p.scol = nc.alloc_sbuf_tensor("scol", [128, 32], F32).ap()
    p.bcols = nc.alloc_sbuf_tensor("bcols", [128, 128], F32).ap()
    big = nc.alloc_sbuf_tensor("big", [128, 44160], F32).ap()
    p.A = Arena(big, 44160)

    p.ps = [nc.alloc_psum_tensor("ps%d" % i, [128, 1024], F32).ap()
            for i in range(4)]
    p.psT_slots = [p.ps[3][:, 0:128], p.ps[3][:, 512:640]]

    p.dma("ds", p.ident, ident_h.ap())
    p.dma("ds", p.onesr, onesr_h.ap())
    p.dma("ds", p.onesc, onesc_h.ap())
    p.dma("ds", p.tri, tri_h.ap())
    p.dma("ds", p.smask, sm_h.ap())
    p.quiesce()

    l2fT = None
    if 2 in ACTIVE:
        l2fT = tuple(
            nc.dram_tensor(nm, shp, F32, kind="Internal").ap()
            for nm, shp in (("l2f_gT", [16 * D, D]), ("l2f_uT", [16 * D, D]),
                            ("l2f_dT", [16 * D, D]), ("l2f_sgT", [D, D]),
                            ("l2f_suT", [D, D]), ("l2f_sdT", [D, D])))
    cur = x0
    free = [xa, xb]
    if 3 in ACTIVE:
        emit_setup_kT(p, kT_hbm)
    if 2 in ACTIVE:
        emit_setup_l2f(p, *l2fT)
    for b in ACTIVE:
        dstb = out if b == max(ACTIVE) else free[0]
        if b == 0:
            emit_b0(p, cur, dstb, spill)
        elif b == 1:
            emit_b1(p, cur, dstb, spill, aux)
        elif b == 2:
            emit_b2(p, cur, dstb, spill, l2fT)
        else:
            emit_b3(p, cur, dstb, kT_hbm)
        if dstb is not out:
            old = cur
            cur = dstb
            free = [free[1], old] if old is not x0 else [free[1], free[0]]
        else:
            cur = dstb
    if cur is not out:
        p.dma("ds", out, cur)
    p.w(nc.sync, "ds")
    p.w(nc.sync, "dg")
    p.bar()
    return nc, es, sorted(p.inputs.keys())


def emit_setup_kT(p, kT_hbm):
    nc = p.nc
    K = p.inp("L3_attn_K")
    A = p.A
    A.reset()
    stage = A.take(1024)
    tbuf = A.take(1024)
    with p.loop(NT):
        i = p.lp["i"]
        p.w(nc.sync, "tt")
        src = bass.AP(K.tensor, K.offset + i * TILE_B, [[D, 128], [1, D]])
        p.dma("ds", stage, src)
        p.w(nc.vector, "ds")
        p.w(nc.tensor, "ds")
        p.transposes([(stage[:, 128 * dc:128 * dc + 128],
                       tbuf[:, 128 * dc:128 * dc + 128]) for dc in range(8)])
        p.w(nc.sync, "vv")
        dst = bass.AP(kT_hbm.tensor, kT_hbm.offset + i * 128,
                      [[S, 128], [128 * S, 8], [1, 128]])
        p.dma("ds", dst, bass.AP(tbuf.tensor, tbuf.offset,
                                 [tbuf.ap[0], [128, 8], [1, 128]]))
    p.quiesce()


def emit_setup_l2f(p, gT, uT, dT, sgT, suT, sdT):
    nc = p.nc
    A = p.A
    A.reset()
    stage = A.take(8192)
    tbuf = A.take(8192)
    wg, wu, wd = p.inp("L2_ffn_w_gate"), p.inp("L2_ffn_w_up"), p.inp("L2_ffn_w_down")
    sg, su, sd = (p.inp("L2_ffn_shared_gate_w"), p.inp("L2_ffn_shared_up_w"),
                  p.inp("L2_ffn_shared_down_w"))

    def one(w_in, w_out, eoff):
        p.w(nc.sync, "tt")
        p.dma("ds", stage, bass.AP(w_in.tensor, eoff,
                                   [[D, 128], [128 * D, 8], [1, D]]))
        p.w(nc.tensor, "ds")
        p.w(nc.vector, "ds")
        pairs = []
        for rc in range(8):
            for cc in range(8):
                pairs.append(
                    (stage[:, rc * D + cc * 128: rc * D + cc * 128 + 128],
                     tbuf[:, cc * D + rc * 128: cc * D + rc * 128 + 128]))
        p.transposes(pairs)
        p.w(nc.sync, "vv")
        p.dma("ds", bass.AP(w_out.tensor, eoff,
                            [[D, 128], [128 * D, 8], [1, D]]), tbuf)

    with p.loop(16):
        eoff = p.lp["i"] * (D * D)
        for w_in, w_out in ((wg, gT), (wu, uT), (wd, dT)):
            one(w_in, w_out, eoff)
    for w_in, w_out in ((sg, sgT), (su, suT), (sd, sdT)):
        one(w_in, w_out, 0)
    p.quiesce()


def emit_b2(p, src, dst, spill, l2fT):
    nc = p.nc
    A = p.A
    A.reset()
    gT, uT, dT, sgT, suT, sdT = l2fT
    Wq, Wk, Wv, Wo = (p.inp("L2_attn_W%s" % s) for s in "qkvo")
    bq, bk, bv, bo = (p.inp("L2_attn_b%s" % s) for s in "qkvo")
    beta = p.inp("L2_attn_beta")
    gw = p.inp("L2_ffn_gate_w")
    ebias = p.inp("L2_ffn_expert_bias")

    wT = {}
    for nm in ("q", "k", "v", "o"):
        wT[nm] = A.take(8192)
    _mk = A.mark()
    stage = A.take(8192)
    for nm, W in (("q", Wq), ("k", Wk), ("v", Wv), ("o", Wo)):
        p.w(nc.sync, "tt")
        p.dma("ds", stage, bass.AP(W.tensor, W.offset, [[D, 128], [128 * D, 8], [1, D]]))
        p.wall("ds")
        pairs = []
        for rc in range(8):
            for cc in range(8):
                pairs.append(
                    (stage[:, rc * D + cc * 128: rc * D + cc * 128 + 128],
                     wT[nm][:, cc * D + rc * 128: cc * D + rc * 128 + 128]))
        p.transposes(pairs)
    p.dma("ds", p.srow[0:1, 0:D], ap2d(bq, D))
    p.dma("ds", p.srow[0:1, D:2 * D], ap2d(bk, D))
    p.dma("ds", p.srow[0:1, 2 * D:3 * D], ap2d(bv, D))
    p.dma("ds", p.srow[0:1, 3 * D:4 * D], ap2d(bo, D))
    p.dma("ds", p.scol[0:1, 6:7], ap2d(beta, 1))
    p.dma("ds", p.srow[0:1, 4 * D:4 * D + 16], ap2d(ebias, 16))
    p.wall("ds")
    p.inc(nc.scalar.activation(p.scol[0:1, 7:8], p.scol[0:1, 6:7], ACTF.Exp), "ss")
    p.w(nc.tensor, "ss")
    p.w(nc.tensor, "vv")
    p.inc(nc.tensor.matmul(p.ps[2][:, 0:1], p.onesr, p.scol[0:1, 7:8],
                           start=True, stop=True), "tt")
    p.inc(nc.tensor.matmul(p.ps[2][:, 4:20], p.onesr, p.srow[0:1, 4 * D:4 * D + 16],
                           start=True, stop=True), "tt")
    p.w(nc.vector, "tt")
    be_col = p.scol[:, 7:8]
    p.inc(nc.vector.tensor_copy(be_col, p.ps[2][:, 0:1]), "vv")
    eb_bc = p.bcols[:, 0:16]
    p.inc(nc.vector.tensor_copy(eb_bc, p.ps[2][:, 4:20]), "vv")
    p.bar()

    A.rewind(_mk)
    xstage = A.take(1024)
    xn = xstage
    xnT = A.take(1024)
    apT = xnT
    pads = A.take(2048)
    qT_pad = A.take(2048)
    kT_pad = A.take(2048)
    v_aug = A.take(1088)
    S_pad = A.take(1088)
    work = A.take(1024)
    num = work
    ap_ = work
    masked = xstage[:, 0:512]
    p.inc(nc.vector.memset(pads, 0.0), "vv")
    p.inc(nc.vector.memset(S_pad, 0.0), "vv")
    p.inc(nc.vector.memset(v_aug, 1.0), "vv")
    p.bar()

    ps_state = p.ps[3][:, 512 + 128:512 + 128 + 272]

    with p.loop(NT):
        c = p.lp["i"]
        coff = c * TILE_B
        p.dma("ds", xstage, bass.AP(src.tensor, src.offset + coff,
                                    [[D, 128], [1, D]]))
        p.w(nc.vector, "ds")
        p.w(nc.scalar, "ds")
        p.ln(xstage, xn, xnT, 1)
        p.w(nc.tensor, "vv")
        p.transposes([(xn[:, dc * 128:(dc + 1) * 128], xnT[:, dc * 128:(dc + 1) * 128])
                      for dc in range(8)])
        pad3 = bass.AP(pads.tensor, pads.offset, [pads.ap[0], [32, 64], [1, 16]])
        for pi, (wname, soff, padT) in enumerate(
                (("q", 0, qT_pad), ("k", D, kT_pad), ("v", 2 * D, None))):
            pd = p.ps[pi % 2]
            p.w(nc.tensor, "ss")
            p.w(nc.tensor, "vv")
            for dh in range(2):
                for dc in range(8):
                    nc.tensor.matmul(
                        pd[:, dh * 512:(dh + 1) * 512],
                        xnT[:, dc * 128:(dc + 1) * 128],
                        wT[wname][:, dc * D + dh * 512: dc * D + (dh + 1) * 512],
                        start=(dc == 0), stop=False)
                mm = nc.tensor.matmul(pd[:, dh * 512:(dh + 1) * 512], p.onesr,
                                      p.srow[0:1, soff + dh * 512: soff + (dh + 1) * 512],
                                      start=False, stop=True)
            p.inc(mm, "tt")
            if padT is not None:
                p.w(nc.scalar, "tt")
                p.w(nc.scalar, "vv")
                p.w(nc.scalar, "tt")
                psrc = bass.AP(pd.tensor, pd.offset, [pd.ap[0], [16, 64], [1, 16]])
                p.inc(nc.scalar.activation(pad3, psrc, ACTF.Exp, scale=be_col), "ss")
                p.w(nc.vector, "ss")
                p.inc(nc.vector.reduce_sum(p.stats[:, 0:64], pad3, axis=AXX), "vv")
                p.w(nc.vector, "vv")
                p.inc(nc.vector.reciprocal(p.stats[:, 0:64], p.stats[:, 0:64]), "vv")
                p.w(nc.vector, "vv")
                r3 = bass.AP(p.stats.tensor, p.stats.offset,
                             [p.stats.ap[0], [1, 64], [0, 16]])
                p.inc(nc.vector.tensor_tensor(pad3, pad3, r3, op=ALU.mult), "vv")
                p.w(nc.tensor, "vv")
                p.transposes([(pads[:, g_ * 128:(g_ + 1) * 128],
                               padT[:, g_ * 128:(g_ + 1) * 128]) for g_ in range(16)])
            else:
                p.w(nc.vector, "tt")
                vsrc = bass.AP(pd.tensor, pd.offset, [pd.ap[0], [16, 64], [1, 16]])
                vdst = bass.AP(v_aug.tensor, v_aug.offset,
                               [v_aug.ap[0], [17, 64], [1, 16]])
                p.inc(nc.vector.tensor_copy(vdst, vsrc), "vv")
                p.w(nc.vector, "vv")
        p.w(nc.tensor, "vv")

        sc_banks = [p.ps[0][:, 0:512], p.ps[0][:, 512:1024],
                    p.ps[1][:, 0:512], p.ps[1][:, 512:1024]]
        for g in range(16):
            p.w(nc.tensor, "vv")
            for s_ in range(4):
                p.inc(nc.tensor.matmul(
                    sc_banks[s_][:, 0:128],
                    kT_pad[32 * s_:32 * s_ + 32, g * 128:(g + 1) * 128],
                    qT_pad[32 * s_:32 * s_ + 32, g * 128:(g + 1) * 128],
                    start=True, stop=True, tile_position=(32 * s_, 0)), "tt")
            p.w(nc.vector, "tt")
            for s_ in range(4):
                p.inc(nc.vector.tensor_tensor(
                    masked[:, s_ * 128:(s_ + 1) * 128], sc_banks[s_][:, 0:128],
                    p.smask, op=ALU.mult), "vv")
            p.w(nc.tensor, "vv")
            pnum = p.ps[2][:, (g % 2) * 512:(g % 2) * 512 + 68]
            for s_ in range(4):
                h = 4 * g + s_
                nc.tensor.matmul(pnum[:, s_ * 17:s_ * 17 + 17],
                                 masked[:, s_ * 128:(s_ + 1) * 128],
                                 v_aug[:, 17 * h:17 * h + 17],
                                 start=True, stop=False)
                mm = nc.tensor.matmul(pnum[:, s_ * 17:s_ * 17 + 17],
                                      qT_pad[:, g * 128:(g + 1) * 128],
                                      S_pad[:, 17 * h:17 * h + 17],
                                      start=False, stop=True)
            p.inc(mm, "tt")
            p.w(nc.vector, "tt")
            nsrc = bass.AP(pnum.tensor, pnum.offset, [pnum.ap[0], [17, 4], [1, 16]])
            ndst = bass.AP(num.tensor, num.offset + g * 64, [num.ap[0], [16, 4], [1, 16]])
            p.inc(nc.vector.tensor_copy(ndst, nsrc), "vv")
            dsrc = bass.AP(pnum.tensor, pnum.offset + 16, [pnum.ap[0], [17, 4], [1, 1]])
            ddst = bass.AP(p.stats.tensor, p.stats.offset + 4 * g,
                           [p.stats.ap[0], [1, 4], [1, 1]])
            p.inc(nc.vector.tensor_copy(ddst, dsrc), "vv")
            for s_ in range(4):
                h = 4 * g + s_
                p.w(nc.tensor, "tt")
                p.inc(nc.tensor.matmul(
                    ps_state[32 * s_:32 * s_ + 16, g * 17:(g + 1) * 17],
                    pads[:, 32 * h:32 * h + 16], v_aug[:, 17 * h:17 * h + 17],
                    start=True, stop=True,
                    tile_position=(0, 32 * s_)), "tt")
        p.w(nc.vector, "tt")
        p.inc(nc.vector.reciprocal(p.stats[:, 64:128], p.stats[:, 0:64]), "vv")
        p.w(nc.vector, "vv")
        n3 = bass.AP(num.tensor, num.offset, [num.ap[0], [16, 64], [1, 16]])
        rb = bass.AP(p.stats.tensor, p.stats.offset + 64,
                     [p.stats.ap[0], [1, 64], [0, 16]])
        p.inc(nc.vector.tensor_tensor(n3, n3, rb, op=ALU.mult), "vv")
        for s_ in range(4):
            sA = ps_state[32 * s_:32 * s_ + 16, :]
            ssrc = bass.AP(sA.tensor, sA.offset, [sA.ap[0], [17, 16], [1, 17]])
            sB = S_pad[32 * s_:32 * s_ + 16, :]
            sdst = bass.AP(sB.tensor, sB.offset + 17 * s_, [sB.ap[0], [68, 16], [1, 17]])
            p.inc(nc.vector.tensor_tensor(sdst, sdst, ssrc, op=ALU.add), "vv")
        p.w(nc.vector, "vv")
        p.w(nc.tensor, "vv")
        p.transposes([(ap_[:, dc * 128:(dc + 1) * 128], apT[:, dc * 128:(dc + 1) * 128])
                      for dc in range(8)])
        p.w(nc.tensor, "vv")
        for dh in range(2):
            pd = p.ps[0][:, dh * 512:(dh + 1) * 512]
            for dc in range(8):
                nc.tensor.matmul(pd, apT[:, dc * 128:(dc + 1) * 128],
                                 wT["o"][:, dc * D + dh * 512: dc * D + (dh + 1) * 512],
                                 start=(dc == 0), stop=False)
            mm = nc.tensor.matmul(pd, p.onesr,
                                  p.srow[0:1, 3 * D + dh * 512: 3 * D + (dh + 1) * 512],
                                  start=False, stop=True)
        p.inc(mm, "tt")
        p.w(nc.vector, "tt")
        p.w(nc.vector, "ds")
        p.inc(nc.vector.tensor_copy(xstage, p.ps[0]), "vv")
        p.w(nc.sync, "vv")
        p.dma("ds", bass.AP(spill.tensor, spill.offset + coff,
                            [[D, 128], [1, D]]), xstage)
    p.quiesce()

    A.rewind(0)
    xg = A.take(4096)
    xng = A.take(4096)
    xnT4 = A.take(4096)
    hgb = A.take(4096)
    hT4 = A.take(4096)
    facc = A.take(4096)
    wbuf = A.take(8192)
    prb = A.take(128)
    rnk = A.take(128)
    wcmb = A.take(128)
    tmp512 = A.take(512)
    astg = A.take(1024)
    xob = A.take(1024)
    gwT16 = A.take(128)
    st16b = A.take(1024)
    p.dma("ds", bass.AP(st16b.tensor, st16b.offset, [[st16b.ap[0][0], 16], [1, 1024]]),
          bass.AP(gw.tensor, gw.offset, [[1024, 16], [1, 1024]]))
    p.wall("ds")
    for dc in range(8):
        sl = p.psT_slots[dc % 2]
        p.w(nc.tensor, "vv", lag=0 if dc == 0 else 1)
        p.inc(nc.tensor.transpose(
            sl[:, 0:16], bass.AP(st16b.tensor, st16b.offset + dc * 128,
                                 [[st16b.ap[0][0], 16], [1, 128]]),
            p.ident[0:16, 0:16]), "tt")
        p.w(nc.vector, "tt")
        p.inc(nc.vector.tensor_copy(gwT16[:, dc * 16:(dc + 1) * 16], sl[:, 0:16]), "vv")
    p.bar()

    with p.loop(8):
        g = p.lp["i"]
        goff = g * (4 * TILE_B)
        toff = [goff if t == 0 else goff + t * TILE_B for t in range(4)]
        p.dma("ds", xg, bass.AP(src.tensor, src.offset + goff,
                                [[D, 128], [128 * D, 4], [1, D]]))
        p.w(nc.vector, "ds")
        p.w(nc.scalar, "ds")
        p.ln(xg, xng, xnT4, 4)
        p.w(nc.tensor, "vv")
        pairs = []
        for t in range(4):
            for dc in range(8):
                pairs.append(
                    (xng[:, t * D + dc * 128: t * D + dc * 128 + 128],
                     xnT4[:, dc * 512 + t * 128: dc * 512 + t * 128 + 128]))
        p.transposes(pairs)
        p.w(nc.tensor, "vv")
        p.w(nc.tensor, "ss")
        for t in range(4):
            for dc in range(8):
                mm = nc.tensor.matmul(
                    p.ps[2][:, t * 16:(t + 1) * 16],
                    xnT4[:, dc * 512 + t * 128: dc * 512 + t * 128 + 128],
                    gwT16[:, dc * 16:(dc + 1) * 16],
                    start=(dc == 0), stop=(dc == 7))
        p.inc(mm, "tt")
        p.w(nc.scalar, "tt")
        p.inc(nc.scalar.activation(prb[:, 0:64], p.ps[2][:, 0:64],
                                   ACTF.Exp, scale=-1.0), "ss")
        p.w(nc.vector, "ss")
        p.inc(nc.vector.tensor_scalar_add(prb[:, 0:64], prb[:, 0:64], 1.0), "vv")
        p.w(nc.vector, "vv")
        p.inc(nc.vector.reciprocal(prb[:, 0:64], prb[:, 0:64]), "vv")
        p.w(nc.vector, "vv")
        p.inc(nc.vector.tensor_tensor(
            bass.AP(rnk.tensor, rnk.offset + 64, [rnk.ap[0], [16, 4], [1, 16]]),
            bass.AP(prb.tensor, prb.offset, [prb.ap[0], [16, 4], [1, 16]]),
            bass.AP(eb_bc.tensor, eb_bc.offset, [eb_bc.ap[0], [0, 4], [1, 16]]),
            op=ALU.add), "vv")
        p.w(nc.vector, "vv")
        sel = rnk[:, 64:128]
        for t in range(4):
            st_ = sel[:, t * 16:(t + 1) * 16]
            rt = rnk[:, t * 16:(t + 1) * 16]
            p.inc(nc.vector.tensor_scalar(rt, st_, st_[:, 0:1], None, op0=ALU.is_lt), "vv")
            p.w(nc.vector, "vv")
            for e in range(1, 16):
                p.inc(nc.vector.scalar_tensor_tensor(
                    rt, st_, st_[:, e:e + 1], rt, op0=ALU.is_lt, op1=ALU.add), "vv")
                p.w(nc.vector, "vv")
        p.inc(nc.vector.tensor_scalar(rnk[:, 0:64], rnk[:, 0:64], 7.5, None,
                                      op0=ALU.is_lt), "vv")
        p.w(nc.vector, "vv")
        p.inc(nc.vector.tensor_mul(wcmb[:, 0:64], prb[:, 0:64], rnk[:, 0:64]), "vv")
        p.w(nc.vector, "vv")
        w3 = bass.AP(wcmb.tensor, wcmb.offset, [wcmb.ap[0], [16, 4], [1, 16]])
        p.inc(nc.vector.reduce_sum(p.stats[:, 0:4], w3, axis=AXX), "vv")
        p.w(nc.vector, "vv")
        p.inc(nc.vector.reciprocal(p.stats[:, 4:8], p.stats[:, 0:4]), "vv")
        p.w(nc.vector, "vv")
        p.inc(nc.vector.tensor_scalar_mul(p.stats[:, 4:8], p.stats[:, 4:8], 2.826), "vv")
        p.w(nc.vector, "vv")
        rr = bass.AP(p.stats.tensor, p.stats.offset + 4, [p.stats.ap[0], [1, 4], [0, 16]])
        p.inc(nc.vector.tensor_tensor(w3, w3, rr, op=ALU.mult), "vv")
        p.w(nc.scalar, "vv")
        p.w(nc.tensor, "vv")
        p.inc(nc.vector.memset(facc, 0.0), "vv")
        p.w(nc.vector, "vv")

        for e in range(17):
            if e < 16:
                srcs = ((gT, e), (uT, e), (dT, e))
            else:
                srcs = ((sgT, 0), (suT, 0), (sdT, 0))
            (g_w, g_e), (u_w, u_e), (d_w, d_e) = srcs
            p.w(nc.sync, "tt")
            p.dma("ds", wbuf, bass.AP(g_w.tensor, g_w.offset + g_e * D * D,
                                      [[D, 128], [128 * D, 8], [1, D]]))
            p.w(nc.tensor, "ds")
            for ic in range(8):
                pd = p.ps[ic % 2][:, 0:512]
                p.w(nc.tensor, "vv", lag=0 if ic == 0 else 1)
                for dc in range(8):
                    mm = nc.tensor.matmul(
                        pd, wbuf[:, dc * D + ic * 128: dc * D + ic * 128 + 128],
                        xnT4[:, dc * 512:(dc + 1) * 512],
                        start=(dc == 0), stop=(dc == 7))
                p.inc(mm, "tt")
                p.w(nc.vector, "tt")
                p.inc(nc.vector.tensor_copy(hgb[:, ic * 512:(ic + 1) * 512], pd), "vv")
            p.w(nc.sync, "tt")
            p.w(nc.sync, "vv")
            p.dma("ds", wbuf, bass.AP(u_w.tensor, u_w.offset + u_e * D * D,
                                      [[D, 128], [128 * D, 8], [1, D]]))
            p.w(nc.tensor, "ds")
            for ic in range(8):
                pd = p.ps[ic % 2][:, 0:512]
                p.w(nc.tensor, "ss", lag=0 if ic == 0 else 1)
                p.w(nc.tensor, "vv", lag=0 if ic == 0 else 1)
                for dc in range(8):
                    mm = nc.tensor.matmul(
                        pd, wbuf[:, dc * D + ic * 128: dc * D + ic * 128 + 128],
                        xnT4[:, dc * 512:(dc + 1) * 512],
                        start=(dc == 0), stop=(dc == 7))
                p.inc(mm, "tt")
                p.w(nc.scalar, "tt")
                p.w(nc.scalar, "vv")
                p.inc(nc.scalar.activation(tmp512, hgb[:, ic * 512:(ic + 1) * 512],
                                           ACTF.Silu), "ss")
                p.w(nc.vector, "ss")
                p.inc(nc.vector.tensor_tensor(hT4[:, ic * 512:(ic + 1) * 512],
                                              tmp512, pd, op=ALU.mult), "vv")
            p.w(nc.sync, "tt")
            p.w(nc.sync, "vv")
            p.dma("ds", wbuf, bass.AP(d_w.tensor, d_w.offset + d_e * D * D,
                                      [[D, 128], [128 * D, 8], [1, D]]))
            p.w(nc.tensor, "ds")
            p.w(nc.tensor, "vv")
            for t in range(4):
                pd2 = p.ps[2]
                for ic in range(8):
                    for dh in range(2):
                        mm = nc.tensor.matmul(
                            pd2[:, dh * 512:(dh + 1) * 512],
                            hT4[:, ic * 512 + t * 128: ic * 512 + t * 128 + 128],
                            wbuf[:, ic * D + dh * 512: ic * D + (dh + 1) * 512],
                            start=(ic == 0), stop=(ic == 7))
                p.inc(mm, "tt")
                p.w(nc.vector, "tt")
                if e < 16:
                    p.inc(nc.vector.scalar_tensor_tensor(
                        facc[:, t * D:(t + 1) * D], pd2,
                        wcmb[:, t * 16 + e: t * 16 + e + 1],
                        facc[:, t * D:(t + 1) * D],
                        op0=ALU.mult, op1=ALU.add), "vv")
                else:
                    p.inc(nc.vector.tensor_add(
                        facc[:, t * D:(t + 1) * D],
                        facc[:, t * D:(t + 1) * D], pd2), "vv")
                p.w(nc.tensor, "vv")
        for t in range(4):
            p.w(nc.sync, "vv")
            p.dma("ds", astg, bass.AP(spill.tensor, spill.offset + toff[t],
                                      [[D, 128], [1, D]]))
            p.w(nc.vector, "ds")
            p.inc(nc.vector.tensor_add(xob, xg[:, t * D:(t + 1) * D],
                                       facc[:, t * D:(t + 1) * D]), "vv")
            p.w(nc.vector, "vv")
            p.inc(nc.vector.tensor_add(xob, xob, astg), "vv")
            p.w(nc.sync, "vv")
            p.dma("ds", bass.AP(dst.tensor, dst.offset + toff[t],
                                [[D, 128], [1, D]]), xob)
    p.quiesce()


def emit_b3(p, src, dst, kT_hbm):
    nc = p.nc
    V = p.inp("L3_attn_V")
    Wt = p.inp("L3_ffn_weight")
    alpha = p.inp("L3_ffn_alpha")
    bias = p.inp("L3_ffn_bias")
    A = p.A
    A.reset()

    if B3_STAGE < 1:
        p.dma("ds", dst, src)
        p.quiesce()
        return
    wefT = A.take(8192)
    _mk = A.mark()
    wq = A.take(8192)
    p.dma("ds", wq, bass.AP(Wt.tensor, Wt.offset, [[D, 128], [128 * D, 8], [1, D]]))
    p.dma("ds", p.srow[:, 0:D], ap2d(bias, D))
    p.dma("ds", p.scol[0:1, 0:1], ap2d(alpha, 1))
    p.wall("ds")
    st = p.stats
    p.inc(nc.vector.reduce_sum(
        st[:, 0:8], bass.AP(wq.tensor, wq.offset, [wq.ap[0], [D, 8], [1, D]]),
        axis=AXX, apply_absolute_value=True), "vv")
    p.w(nc.vector, "vv")
    p.inc(nc.vector.reduce_sum(st[:, 8:9], st[:, 0:8], axis=AXX), "vv")
    p.w(nc.tensor, "vv")
    p.inc(nc.tensor.matmul(p.ps[0][0:1, 0:1], st[:, 8:9], p.onesc,
                           start=True, stop=True), "tt")
    p.w(nc.vector, "tt")
    p.inc(nc.vector.tensor_scalar_mul(st[0:1, 12:13], p.ps[0][0:1, 0:1],
                                      0.7 / (D * D)), "vv")
    p.w(nc.tensor, "vv")
    p.inc(nc.tensor.matmul(p.ps[0][:, 4:5], p.onesr, st[0:1, 12:13],
                           start=True, stop=True), "tt")
    p.inc(nc.tensor.matmul(p.ps[0][:, 5:6], p.onesr, p.scol[0:1, 0:1],
                           start=True, stop=True), "tt")
    p.w(nc.vector, "tt")
    dcol, acol, ndcol = p.scol[:, 1:2], p.scol[:, 2:3], p.scol[:, 3:4]
    p.inc(nc.vector.tensor_copy(dcol, p.ps[0][:, 4:5]), "vv")
    p.inc(nc.vector.tensor_copy(acol, p.ps[0][:, 5:6]), "vv")
    p.w(nc.vector, "vv")
    p.inc(nc.vector.tensor_scalar_mul(ndcol, dcol, -1.0), "vv")
    p.w(nc.vector, "vv")
    tq = A.take(1024)
    for dc in range(8):
        w_ = wq[:, dc * D:(dc + 1) * D]
        p.inc(nc.vector.tensor_scalar(tq, w_, dcol, None, op0=ALU.is_gt), "vv")
        p.inc(nc.vector.tensor_scalar(w_, w_, ndcol, None, op0=ALU.is_lt), "vv")
        p.w(nc.vector, "vv")
        p.inc(nc.vector.tensor_sub(w_, tq, w_), "vv")
        p.w(nc.vector, "vv")
        p.inc(nc.vector.tensor_scalar(w_, w_, acol, None, op0=ALU.mult), "vv")
        p.w(nc.vector, "vv")
    p.w(nc.tensor, "vv")
    pairs = []
    for oc in range(8):
        for dc in range(8):
            pairs.append((wq[:, oc * D + dc * 128: oc * D + dc * 128 + 128],
                          wefT[:, dc * D + oc * 128: dc * D + oc * 128 + 128]))
    p.transposes(pairs)
    p.quiesce()
    A.rewind(_mk)
    if B3_STAGE < 2:
        p.dma("ds", dst, src)
        p.quiesce()
        return

    xg, xng = A.take(4096), A.take(4096)
    xnT = A.take(4096)
    oacc = A.take(4096)
    sq = oacc
    ktb = [A.take(4096), A.take(4096)]
    vtb = [A.take(4096), A.take(4096)]
    ebuf, eT = A.take(512), A.take(512)
    den8 = A.take(32)
    xo = A.take(1024)
    psA = [p.ps[0][:, 0:512], p.ps[0][:, 512:1024]]
    psB = [p.ps[1], p.ps[2]]
    psC = p.ps[1]

    with p.loop(8):
        g = p.lp["i"]
        goff = g * (4 * TILE_B)
        toff = [goff if t == 0 else goff + t * TILE_B for t in range(4)]
        p.w(nc.sync, "vv")
        p.w(nc.sync, "tt")
        p.dma("ds", xg, bass.AP(src.tensor, src.offset + goff,
                                [[D, 128], [128 * D, 4], [1, D]]))
        p.w(nc.vector, "ds")
        p.w(nc.scalar, "ds")
        p.ln(xg, xng, sq, 4)
        p.w(nc.tensor, "vv")
        pairs = []
        for t in range(4):
            for dc in range(8):
                pairs.append(
                    (xng[:, t * D + dc * 128: t * D + dc * 128 + 128],
                     xnT[:, dc * 512 + t * 128: dc * 512 + t * 128 + 128]))
        p.transposes(pairs)
        if B3_STAGE == 2:
            for t in range(4):
                p.w(nc.sync, "vv")
                p.dma("ds", bass.AP(dst.tensor, dst.offset + toff[t],
                                    [[D, 128], [1, D]]), xng[:, t * D:(t + 1) * D])
        if B3_STAGE >= 3:
          p.inc(nc.vector.memset(den8, 0.0), "vv")
          p.w(nc.scalar, "vv")

          for pc in range(8):
            par = pc % 2
            p.w(nc.sync, "tt")
            p.dma("ds", ktb[par], fm_cols(kT_hbm, pc, 512, S))
            p.dma("ds", vtb[par],
                  bass.AP(V.tensor, V.offset + pc * 4 * TILE_B,
                          [[D, 128], [128 * D, 4], [1, D]]))
            p.w(nc.tensor, "ds")
            for t in range(4):
                pa = psA[t % 2]
                p.w(nc.tensor, "ss")
                for dc in range(8):
                    mm = nc.tensor.matmul(
                        pa, xnT[:, dc * 512 + t * 128: dc * 512 + t * 128 + 128],
                        ktb[par][:, dc * 512:(dc + 1) * 512],
                        start=(dc == 0), stop=(dc == 7))
                p.inc(mm, "tt")
                p.w(nc.scalar, "tt")
                p.inc(nc.scalar.activation(
                    ebuf, pa, ACTF.Exp, scale=1.0 / 32.0,
                    accum_out=den8[:, t * 8 + pc: t * 8 + pc + 1]), "ss")
                p.w(nc.tensor, "ss")
                p.transposes([(ebuf[:, j * 128:(j + 1) * 128],
                               eT[:, j * 128:(j + 1) * 128]) for j in range(4)])
                p.w(nc.tensor, "vv")
                pb = psB[t % 2]
                for j in range(4):
                    first = (j == 0)
                    last = (j == 3)
                    nc.tensor.matmul(pb[:, 0:512], eT[:, j * 128:(j + 1) * 128],
                                     vtb[par][:, j * D: j * D + 512],
                                     start=first, stop=last)
                    mm = nc.tensor.matmul(pb[:, 512:1024],
                                          eT[:, j * 128:(j + 1) * 128],
                                          vtb[par][:, j * D + 512: (j + 1) * D],
                                          start=first, stop=last)
                p.inc(mm, "tt")
                p.w(nc.vector, "tt")
                if pc == 0:
                    p.inc(nc.vector.tensor_copy(oacc[:, t * D:(t + 1) * D], pb), "vv")
                else:
                    p.inc(nc.vector.tensor_add(
                        oacc[:, t * D:(t + 1) * D],
                        oacc[:, t * D:(t + 1) * D], pb), "vv")

        if B3_STAGE < 3:
            tail_range = ()
        else:
            tail_range = range(4)
            p.w(nc.vector, "ss")
            d3 = bass.AP(den8.tensor, den8.offset, [den8.ap[0], [8, 4], [1, 8]])
            p.inc(nc.vector.reduce_sum(p.stats[:, 32:36], d3, axis=AXX), "vv")
            p.w(nc.vector, "vv")
            p.inc(nc.vector.reciprocal(p.stats[:, 36:40], p.stats[:, 32:36]), "vv")

        for t in tail_range:
            if B3_STAGE == 3:
                p.w(nc.vector, "vv")
                p.inc(nc.vector.tensor_scalar(xo, oacc[:, t * D:(t + 1) * D],
                                              p.stats[:, 36 + t:37 + t], None,
                                              op0=ALU.mult), "vv")
                p.w(nc.sync, "vv")
                p.dma("ds", bass.AP(dst.tensor, dst.offset + toff[t],
                                    [[D, 128], [1, D]]), xo)
                continue
            p.w(nc.tensor, "vv")
            for dc in range(8):
                o_ = dc * 512 + t * 128
                nc.tensor.matmul(psC[:, 0:512], xnT[:, o_: o_ + 128],
                                 wefT[:, dc * D: dc * D + 512],
                                 start=(dc == 0), stop=False)
                nc.tensor.matmul(psC[:, 512:1024], xnT[:, o_: o_ + 128],
                                 wefT[:, dc * D + 512: (dc + 1) * D],
                                 start=(dc == 0), stop=False)
            nc.tensor.matmul(psC[:, 0:512], p.onesr, p.srow[0:1, 0:512],
                             start=False, stop=True)
            p.inc(nc.tensor.matmul(psC[:, 512:1024], p.onesr,
                                   p.srow[0:1, 512:1024],
                                   start=False, stop=True), "tt")
            p.w(nc.vector, "tt")
            p.w(nc.vector, "dg")
            p.inc(nc.vector.scalar_tensor_tensor(
                xo, oacc[:, t * D:(t + 1) * D], p.stats[:, 36 + t:37 + t],
                xg[:, t * D:(t + 1) * D], op0=ALU.mult, op1=ALU.add), "vv")
            p.w(nc.vector, "vv")
            p.inc(nc.vector.tensor_add(xo, xo, psC), "vv")
            p.w(nc.sync, "vv")
            p.dma("ds", bass.AP(dst.tensor, dst.offset + toff[t],
                                [[D, 128], [1, D]]), xo)
    p.quiesce()


def emit_b0(p, src, dst, spill):
    nc = p.nc
    toep = p.inp("toep")
    sel = p.inp("L0_ffn_expert_sel")
    keys = p.inp("L0_ffn_keys")
    values = p.inp("L0_ffn_values")
    rsc = p.inp("L0_ffn_route_scale")
    A = p.A
    A.reset()

    M = A.take(17 * 128)
    _mk1 = A.mark()
    W17 = A.take(17 * 1024)
    xstage = A.take(1024)
    sq1 = A.take(1024)
    cbuf = A.take(1024)
    p.dma("ds", M, bass.AP(toep.tensor, toep.offset,
                           [[128, 128], [128 * 128, 17], [1, 128]]))
    p.wall("ds")

    def load_ln_to_slot(tile_idx, slot):
        p.w(nc.sync, "vv")
        p.w(nc.sync, "ss")
        p.dma("ds", xstage, hbm_tiles(src, tile_idx))
        p.w(nc.vector, "ds")
        p.w(nc.scalar, "ds")
        p.w(nc.vector, "tt")
        p.ln(xstage, W17[:, slot * D:(slot + 1) * D], sq1, 1)

    for w_ in range(17):
        load_ln_to_slot(w_, w_)
    for bt in range(NT):
        omax = min(16, NT - 1 - bt)
        p.w(nc.tensor, "vv")
        for o in range(omax + 1):
            slot = (bt + o) % 17
            for h in range(2):
                mm = nc.tensor.matmul(
                    p.ps[0][:, h * 512:(h + 1) * 512],
                    M[:, o * 128:(o + 1) * 128],
                    W17[:, slot * D + h * 512: slot * D + (h + 1) * 512],
                    start=(o == 0), stop=(o == omax))
        p.inc(mm, "tt")
        p.w(nc.vector, "tt")
        p.w(nc.vector, "ds", lag=0)
        p.inc(nc.vector.tensor_copy(cbuf, p.ps[0]), "vv")
        p.w(nc.sync, "vv")
        p.dma("ds", hbm_tiles(spill, bt), cbuf)
        if bt <= 14:
            load_ln_to_slot(bt + 17, bt % 17)
    p.quiesce()

    A.rewind(_mk1)
    selT = A.take(128)
    st16 = A.take(1024)
    p.dma("ds", bass.AP(st16.tensor, st16.offset, [[st16.ap[0][0], 16], [1, 1024]]),
          bass.AP(sel.tensor, sel.offset, [[1024, 16], [1, 1024]]))
    p.dma("ds", p.scol[0:1, 4:5], ap2d(rsc, 1))
    p.wall("ds")
    for dc in range(8):
        sl = p.psT_slots[dc % 2]
        p.w(nc.tensor, "vv", lag=0 if dc == 0 else 1)
        p.inc(nc.tensor.transpose(
            sl[:, 0:16], bass.AP(st16.tensor, st16.offset + dc * 128,
                                 [[st16.ap[0][0], 16], [1, 128]]),
            p.ident[0:16, 0:16]), "tt")
        p.w(nc.vector, "tt")
        p.inc(nc.vector.tensor_copy(selT[:, dc * 16:(dc + 1) * 16], sl[:, 0:16]), "vv")
    p.w(nc.tensor, "vv")
    p.inc(nc.tensor.matmul(p.ps[0][:, 0:1], p.onesr, p.scol[0:1, 4:5],
                           start=True, stop=True), "tt")
    p.w(nc.vector, "tt")
    rs_col = p.scol[:, 5:6]
    p.inc(nc.vector.tensor_copy(rs_col, p.ps[0][:, 0:1]), "vv")
    p.bar()

    xg8 = A.take(8192)
    xng8 = A.take(8192)
    facc = xng8
    xnT8 = A.take(8192)
    kb = [A.take(2048), A.take(2048)]
    vb = [A.take(2048), A.take(2048)]
    hT = A.take(2048)
    prb = A.take(128)
    rnk = A.take(128)
    wbuf = A.take(128)
    sstage = A.take(1024)
    xob = A.take(1024)

    for g in range(4):
        p.w(nc.sync, "vv")
        p.w(nc.sync, "ss")
        p.w(nc.sync, "tt")
        p.dma("ds", xg8, hbm_tiles(src, g * 8, 8))
        p.w(nc.vector, "ds")
        p.w(nc.scalar, "ds")
        p.ln(xg8, xng8, xnT8, 8)
        p.w(nc.tensor, "vv")
        pairs = []
        for t in range(8):
            for dc in range(8):
                pairs.append(
                    (xng8[:, t * D + dc * 128: t * D + dc * 128 + 128],
                     xnT8[:, dc * D + t * 128: dc * D + t * 128 + 128]))
        p.transposes(pairs)
        p.w(nc.tensor, "ss")
        p.w(nc.tensor, "vv")
        for t in range(8):
            for dc in range(8):
                mm = nc.tensor.matmul(
                    p.ps[2][:, t * 16:(t + 1) * 16],
                    xnT8[:, dc * D + t * 128: dc * D + t * 128 + 128],
                    selT[:, dc * 16:(dc + 1) * 16],
                    start=(dc == 0), stop=(dc == 7))
        p.inc(mm, "tt")
        p.w(nc.scalar, "tt")
        p.inc(nc.scalar.activation(prb, p.ps[2][:, 0:128], ACTF.Exp, scale=-1.0), "ss")
        p.w(nc.vector, "ss")
        p.inc(nc.vector.tensor_scalar_add(prb, prb, 1.0), "vv")
        p.w(nc.vector, "vv")
        p.inc(nc.vector.reciprocal(prb, prb), "vv")
        p.w(nc.vector, "vv")
        for t in range(8):
            pt = prb[:, t * 16:(t + 1) * 16]
            rt = rnk[:, t * 16:(t + 1) * 16]
            p.inc(nc.vector.tensor_scalar(rt, pt, pt[:, 0:1], None, op0=ALU.is_lt), "vv")
            p.w(nc.vector, "vv")
            for e in range(1, 16):
                p.inc(nc.vector.scalar_tensor_tensor(
                    rt, pt, pt[:, e:e + 1], rt, op0=ALU.is_lt, op1=ALU.add), "vv")
                p.w(nc.vector, "vv")
        p.inc(nc.vector.tensor_scalar(rnk, rnk, 3.5, None, op0=ALU.is_lt), "vv")
        p.w(nc.vector, "vv")
        p.inc(nc.vector.tensor_mul(wbuf, prb, rnk), "vv")
        p.w(nc.vector, "vv")
        wv3 = bass.AP(wbuf.tensor, wbuf.offset, [wbuf.ap[0], [16, 8], [1, 16]])
        p.inc(nc.vector.reduce_sum(p.stats[:, 40:48], wv3, axis=AXX), "vv")
        p.w(nc.vector, "vv")
        p.inc(nc.vector.tensor_scalar_max(p.stats[:, 40:48], p.stats[:, 40:48],
                                          1e-9), "vv")
        p.w(nc.vector, "vv")
        p.inc(nc.vector.reciprocal(p.stats[:, 48:56], p.stats[:, 40:48]), "vv")
        p.w(nc.vector, "vv")
        p.inc(nc.vector.tensor_scalar(p.stats[:, 48:56], p.stats[:, 48:56],
                                      rs_col, None, op0=ALU.mult), "vv")
        p.w(nc.vector, "vv")
        rr_b = bass.AP(p.stats.tensor, p.stats[:, 48:56].offset,
                       [p.stats.ap[0], [1, 8], [0, 16]])
        wb3 = bass.AP(wbuf.tensor, wbuf.offset, [wbuf.ap[0], [16, 8], [1, 16]])
        p.inc(nc.vector.tensor_tensor(wb3, wb3, rr_b, op=ALU.mult), "vv")
        p.w(nc.scalar, "vv")
        p.w(nc.tensor, "vv")

        for e in range(16):
            par = e % 2
            p.w(nc.sync, "tt")
            p.dma("ds", kb[par], bass.AP(keys.tensor, keys.offset + e * D * 256,
                                         [[256, 128], [128 * 256, 8], [1, 256]]))
            p.dma("ds", vb[par], bass.AP(values.tensor, values.offset + e * 256 * D,
                                         [[D, 128], [128 * D, 2], [1, D]]))
            p.w(nc.tensor, "ds")
            p.w(nc.tensor, "ss")
            for hc in range(2):
                for th in range(2):
                    pdst = p.ps[hc][:, th * 512:(th + 1) * 512]
                    for dc in range(8):
                        mm = nc.tensor.matmul(
                            pdst, kb[par][:, dc * 256 + hc * 128: dc * 256 + hc * 128 + 128],
                            xnT8[:, dc * D + th * 512: dc * D + (th + 1) * 512],
                            start=(dc == 0), stop=(dc == 7))
            p.inc(mm, "tt")
            p.w(nc.scalar, "tt")
            p.w(nc.scalar, "vv")
            for hc in range(2):
                ii = nc.scalar.activation(hT[:, hc * D:(hc + 1) * D],
                                          p.ps[hc], ACTF.Relu)
            p.inc(ii, "ss")
            p.w(nc.tensor, "ss")
            for t in range(8):
                for dh in range(2):
                    for hc in range(2):
                        mm = nc.tensor.matmul(
                            p.ps[2][:, dh * 512:(dh + 1) * 512],
                            hT[:, hc * D + t * 128: hc * D + t * 128 + 128],
                            vb[par][:, hc * D + dh * 512: hc * D + (dh + 1) * 512],
                            start=(hc == 0), stop=(hc == 1))
                p.inc(mm, "tt")
                p.w(nc.vector, "tt")
                if e == 0:
                    p.inc(nc.vector.tensor_scalar(
                        facc[:, t * D:(t + 1) * D], p.ps[2],
                        wbuf[:, t * 16 + e: t * 16 + e + 1], None,
                        op0=ALU.mult), "vv")
                else:
                    p.inc(nc.vector.scalar_tensor_tensor(
                        facc[:, t * D:(t + 1) * D], p.ps[2],
                        wbuf[:, t * 16 + e: t * 16 + e + 1],
                        facc[:, t * D:(t + 1) * D],
                        op0=ALU.mult, op1=ALU.add), "vv")
                p.w(nc.tensor, "vv")
        for t in range(8):
            p.w(nc.sync, "vv")
            p.w(nc.sync, "ss")
            p.dma("ds", sstage, hbm_tiles(spill, g * 8 + t))
            p.w(nc.vector, "ds")
            p.inc(nc.vector.tensor_add(xob, xg8[:, t * D:(t + 1) * D],
                                       facc[:, t * D:(t + 1) * D]), "vv")
            p.w(nc.vector, "vv")
            p.inc(nc.vector.tensor_add(xob, xob, sstage), "vv")
            p.w(nc.sync, "vv")
            p.dma("ds", hbm_tiles(dst, g * 8 + t), xob)
    p.quiesce()


def emit_b1(p, src, dst, spill, aux):
    nc = p.nc
    A = p.A
    A.reset()
    Wq, Wk, Wv, Wo = (p.inp("L1_attn_W%s" % s) for s in "qkvo")
    W1, W2 = p.inp("L1_attn_W1"), p.inp("L1_attn_W2")
    b1a, b2a = p.inp("L1_attn_b1"), p.inp("L1_attn_b2")
    gw = p.inp("L1_ffn_gate_w")
    fw1, fb1 = p.inp("L1_ffn_w1"), p.inp("L1_ffn_b1")
    fw2, fb2 = p.inp("L1_ffn_w2"), p.inp("L1_ffn_b2")
    sw1, sb1 = p.inp("L1_ffn_sw1"), p.inp("L1_ffn_sb1")
    sw2, sb2 = p.inp("L1_ffn_sw2"), p.inp("L1_ffn_sb2")
    bcols = p.bcols
    gwT = p.scol[:, 8:24]

    st16 = A.take(1024)
    p.dma("ds", bcols[:, 0:8], bass.AP(b2a.tensor, b2a.offset, [[1, 128], [128, 8]]), slow=True)
    p.dma("ds", bcols[:, 8:12], bass.AP(b1a.tensor, b1a.offset, [[1, 128], [128, 4]]), slow=True)
    p.dma("ds", bcols[:, 16:48], bass.AP(fb1.tensor, fb1.offset, [[1, 128], [128, 32]]), slow=True)
    p.dma("ds", bcols[:, 48:80],
          bass.AP(fb1.tensor, fb1.offset + 4096, [[1, 128], [128, 32]]), slow=True)
    p.dma("ds", bcols[:, 80:112], bass.AP(sb1.tensor, sb1.offset, [[1, 128], [128, 32]]), slow=True)
    p.dma("ds", p.srow[0:1, 0:D], bass.AP(fb2.tensor, fb2.offset, [[D, 1], [1, D]]))
    p.dma("ds", p.srow[0:1, D:2 * D],
          bass.AP(fb2.tensor, fb2.offset + D, [[D, 1], [1, D]]))
    p.dma("ds", p.srow[0:1, 2 * D:3 * D], ap2d(sb2, D))
    p.dma("ds", bass.AP(st16.tensor, st16.offset, [[st16.ap[0][0], 2], [1, 1024]]),
          bass.AP(gw.tensor, gw.offset, [[1024, 2], [1, 1024]]))
    p.wall("ds")
    for dc in range(8):
        sl = p.psT_slots[dc % 2]
        p.w(nc.tensor, "vv", lag=0 if dc == 0 else 1)
        p.inc(nc.tensor.transpose(
            sl[:, 0:2], bass.AP(st16.tensor, st16.offset + dc * 128,
                                [[st16.ap[0][0], 2], [1, 128]]),
            p.ident[0:2, 0:2]), "tt")
        p.w(nc.vector, "tt")
        p.inc(nc.vector.tensor_copy(gwT[:, dc * 2:dc * 2 + 2], sl[:, 0:2]), "vv")
    p.bar()
    if B1_STAGE < 2:
        p.dma("ds", dst, src)
        p.quiesce()
        return

    xg = A.take(4096)
    xng = A.take(4096)
    xnT = A.take(4096)
    xnT0 = A.take(4096)
    xnT1 = A.take(4096)
    hT = A.take(16384)
    w1c = [A.take(1024), A.take(1024)]
    w2c = [A.take(1024), A.take(1024)]
    facc = xnT0
    xnm = A.take(1024)

    mlps = [
        (fw1.tensor, fw1.offset, fw2.tensor, fw2.offset, 16, 0),
        (fw1.tensor, fw1.offset + D * 4096, fw2.tensor, fw2.offset + 4096 * D, 48, 1),
        (sw1.tensor, sw1.offset, sw2.tensor, sw2.offset, 80, 2),
    ]

    for g in range(8):
        p.w(nc.sync, "vv")
        p.w(nc.sync, "ss")
        p.w(nc.sync, "tt")
        p.dma("ds", xg, hbm_tiles(src, g * 4, 4))
        p.w(nc.vector, "ds")
        p.w(nc.scalar, "ds")
        p.ln(xg, xng, xnT, 4)
        p.w(nc.tensor, "vv")
        pairs = []
        for t in range(4):
            for dc in range(8):
                pairs.append(
                    (xng[:, t * D + dc * 128: t * D + dc * 128 + 128],
                     xnT[:, dc * 512 + t * 128: dc * 512 + t * 128 + 128]))
        p.transposes(pairs)
        p.w(nc.tensor, "vv")
        for t in range(4):
            for dc in range(8):
                mm = nc.tensor.matmul(
                    p.ps[2][:, t * 2:t * 2 + 2],
                    xnT[:, dc * 512 + t * 128: dc * 512 + t * 128 + 128],
                    gwT[:, dc * 2:dc * 2 + 2], start=(dc == 0), stop=(dc == 7))
        p.inc(mm, "tt")
        p.w(nc.vector, "tt")
        m01 = p.stats[:, 48:56]
        gsb = p.stats[:, 40:48]
        p.inc(nc.vector.tensor_copy(gsb, p.ps[2][:, 0:8]), "vv")
        p.w(nc.vector, "vv")
        for t in range(4):
            p.inc(nc.vector.tensor_tensor(
                m01[:, 2 * t:2 * t + 1], gsb[:, 2 * t:2 * t + 1],
                gsb[:, 2 * t + 1:2 * t + 2], op=ALU.is_ge), "vv")
            p.w(nc.vector, "vv")
            p.inc(nc.vector.tensor_scalar(
                m01[:, 2 * t + 1:2 * t + 2], m01[:, 2 * t:2 * t + 1],
                -1.0, 1.0, op0=ALU.mult, op1=ALU.add), "vv")
            p.w(nc.vector, "vv")
        for e, xnTe in ((0, xnT0), (1, xnT1)):
            for t in range(4):
                p.inc(nc.vector.tensor_scalar(
                    xnm, xng[:, t * D:(t + 1) * D],
                    m01[:, 2 * t + e:2 * t + e + 1], None, op0=ALU.mult), "vv")
                p.w(nc.tensor, "vv")
                pairs = [(xnm[:, dc * 128:(dc + 1) * 128],
                          xnTe[:, dc * 512 + t * 128: dc * 512 + t * 128 + 128])
                         for dc in range(8)]
                p.transposes(pairs)
                p.w(nc.vector, "vv")

        for w1t, w1o, w2t, w2o, bcol, mi in mlps:
            xin = (xnT0, xnT1, xnT)[mi]
            for hc in range(32):
                par = hc % 2
                p.w(nc.sync, "tt")
                p.dma("ds", w1c[par],
                      bass.AP(w1t, w1o + hc * 128,
                              [[4096, 128], [128 * 4096, 8], [1, 128]]))
                p.w(nc.tensor, "ds")
                p.w(nc.tensor, "ss")
                pdst = p.ps[par][:, 0:512]
                for dc in range(8):
                    mm = nc.tensor.matmul(
                        pdst, w1c[par][:, dc * 128:(dc + 1) * 128],
                        xin[:, dc * 512:(dc + 1) * 512],
                        start=(dc == 0), stop=(dc == 7))
                p.inc(mm, "tt")
                p.w(nc.scalar, "tt")
                p.inc(nc.scalar.activation(
                    hT[:, hc * 512:(hc + 1) * 512], pdst, ACTF.Relu,
                    bias=bcols[:, bcol + hc:bcol + hc + 1]), "ss")
            p.w(nc.tensor, "ss")
            for hc in range(32):
                par = hc % 2
                p.w(nc.sync, "tt", lag=1)
                p.dma("ds", w2c[par],
                      bass.AP(w2t, w2o + hc * 128 * D, [[D, 128], [1, D]]))
                p.w(nc.tensor, "ds")
                for t in range(4):
                    for dh in range(2):
                        mm = nc.tensor.matmul(
                            p.ps[t][:, dh * 512:(dh + 1) * 512],
                            hT[:, hc * 512 + t * 128: hc * 512 + t * 128 + 128],
                            w2c[par][:, dh * 512:(dh + 1) * 512],
                            start=(hc == 0), stop=False)
                p.inc(mm, "tt")
            srow_off = mi * D
            for t in range(4):
                nc.tensor.matmul(p.ps[t][:, 0:512], p.onesr,
                                 p.srow[0:1, srow_off:srow_off + 512],
                                 start=False, stop=True)
                mm = nc.tensor.matmul(p.ps[t][:, 512:1024], p.onesr,
                                      p.srow[0:1, srow_off + 512:srow_off + 1024],
                                      start=False, stop=True)
            p.inc(mm, "tt")
            p.w(nc.vector, "tt")
            for t in range(4):
                if mi == 0:
                    p.inc(nc.vector.tensor_copy(
                        facc[:, t * D:(t + 1) * D], p.ps[t]), "vv")
                else:
                    p.inc(nc.vector.tensor_add(
                        facc[:, t * D:(t + 1) * D],
                        facc[:, t * D:(t + 1) * D], p.ps[t]), "vv")
            p.w(nc.tensor, "vv")
        for t in range(4):
            p.w(nc.sync, "vv")
            p.dma("ds", hbm_tiles(spill, g * 4 + t), facc[:, t * D:(t + 1) * D])
    p.quiesce()
    if B1_STAGE < 3:
        p.dma("ds", dst, src)
        p.quiesce()
        return

    A.reset()
    xg8 = A.take(8192)
    xng8 = A.take(8192)
    xnT8 = A.take(8192)
    compT = xg8
    w1b = xng8[:, 0:4096]
    h1T = xng8[:, 4096:8192]
    compTp = A.take(2048)
    kcT = A.take(2048)
    vc = A.take(2048)
    tmp256 = A.take(256)

    p.dma("ds", xg8, hbm_tiles(src, 0, 8))
    p.w(nc.vector, "ds")
    p.w(nc.scalar, "ds")
    p.ln(xg8, xng8, xnT8, 8)
    p.w(nc.tensor, "vv")
    pairs = []
    for t in range(8):
        for dc in range(8):
            pairs.append(
                (xng8[:, t * D + dc * 128: t * D + dc * 128 + 128],
                 xnT8[:, dc * D + t * 128: dc * D + t * 128 + 128]))
    p.transposes(pairs)
    p.quiesce()
    p.dma("ds", w1b, bass.AP(W1.tensor, W1.offset, [[512, 128], [128 * 512, 8], [1, 512]]))
    p.wall("ds")
    for jc in range(4):
        for th in range(2):
            pdst = p.ps[jc % 2][:, th * 512:(th + 1) * 512]
            p.w(nc.tensor, "ss")
            for dc in range(8):
                mm = nc.tensor.matmul(
                    pdst, w1b[:, dc * 512 + jc * 128: dc * 512 + jc * 128 + 128],
                    xnT8[:, dc * D + th * 512: dc * D + (th + 1) * 512],
                    start=(dc == 0), stop=(dc == 7))
            p.inc(mm, "tt")
            p.w(nc.scalar, "tt")
            p.inc(nc.scalar.activation(
                h1T[:, jc * D + th * 512: jc * D + (th + 1) * 512], pdst,
                ACTF.Relu, bias=bcols[:, 8 + jc:9 + jc]), "ss")
    p.w(nc.sync, "ss")
    p.w(nc.sync, "tt")
    p.dma("ds", w1b, bass.AP(W2.tensor, W2.offset, [[D, 128], [128 * D, 4], [1, D]]))
    p.wall("ds")
    p.w(nc.tensor, "ss")
    for dc in range(8):
        for th in range(2):
            pdst = p.ps[2][:, th * 512:(th + 1) * 512]
            for jc in range(4):
                mm = nc.tensor.matmul(
                    pdst, w1b[:, jc * D + dc * 128: jc * D + dc * 128 + 128],
                    h1T[:, jc * D + th * 512: jc * D + (th + 1) * 512],
                    start=(jc == 0), stop=(jc == 3))
        p.inc(mm, "tt")
        p.w(nc.vector, "tt")
        p.inc(nc.vector.tensor_scalar(
            compT[:, dc * D: dc * D + 512], p.ps[2][:, 0:512],
            bcols[:, dc:dc + 1], None, op0=ALU.add), "vv")
        p.w(nc.vector, "vv")
        p.inc(nc.vector.tensor_scalar(
            compT[:, dc * D + 512: (dc + 1) * D], p.ps[2][:, 512:1024],
            bcols[:, dc:dc + 1], None, op0=ALU.add), "vv")
        p.w(nc.tensor, "vv")
    for dc in range(8):
        a_ = [bass.AP(compT.tensor, compT.offset + dc * D + k, [compT.ap[0], [4, 256]])
              for k in range(4)]
        o_ = compTp[:, dc * 256:(dc + 1) * 256]
        p.inc(nc.vector.tensor_add(o_, a_[0], a_[1]), "vv")
        p.inc(nc.vector.tensor_add(tmp256, a_[2], a_[3]), "vv")
        p.w(nc.vector, "vv")
        p.inc(nc.vector.tensor_add(o_, o_, tmp256), "vv")
        p.w(nc.vector, "vv")
        p.inc(nc.vector.tensor_scalar_mul(o_, o_, 0.25), "vv")
        p.w(nc.vector, "vv")
    p.w(nc.sync, "vv")
    p.w(nc.sync, "ss")
    p.dma("ds", w1b, bass.AP(Wk.tensor, Wk.offset, [[D, 128], [128 * D, 8], [1, 512]]))
    p.dma("ds", h1T, bass.AP(Wk.tensor, Wk.offset + 512,
                             [[D, 128], [128 * D, 8], [1, 512]]))
    p.wall("ds")
    p.w(nc.tensor, "vv")
    for dkc in range(8):
        buf = w1b if dkc < 4 else h1T
        dko = (dkc % 4) * 128
        pdst = p.ps[dkc % 2][:, 0:256]
        p.w(nc.tensor, "vv", lag=0 if dkc == 0 else 1)
        for dc in range(8):
            mm = nc.tensor.matmul(
                pdst, buf[:, dc * 512 + dko: dc * 512 + dko + 128],
                compTp[:, dc * 256:(dc + 1) * 256],
                start=(dc == 0), stop=(dc == 7))
        p.inc(mm, "tt")
        p.w(nc.vector, "tt")
        p.inc(nc.vector.tensor_copy(kcT[:, dkc * 256:(dkc + 1) * 256], pdst), "vv")
    p.w(nc.sync, "vv")
    p.w(nc.sync, "tt")
    p.dma("ds", w1b, bass.AP(Wv.tensor, Wv.offset, [[D, 128], [128 * D, 8], [1, 512]]))
    p.dma("ds", h1T, bass.AP(Wv.tensor, Wv.offset + 512,
                             [[D, 128], [128 * D, 8], [1, 512]]))
    p.wall("ds")
    p.w(nc.tensor, "vv")
    for uc in range(2):
        for dv in range(2):
            buf = w1b if dv == 0 else h1T
            pdst = p.ps[2][:, dv * 512:(dv + 1) * 512]
            for dc in range(8):
                mm = nc.tensor.matmul(
                    pdst, compTp[:, dc * 256 + uc * 128: dc * 256 + uc * 128 + 128],
                    buf[:, dc * 512:(dc + 1) * 512],
                    start=(dc == 0), stop=(dc == 7))
        p.inc(mm, "tt")
        p.w(nc.vector, "tt")
        p.inc(nc.vector.tensor_copy(vc[:, uc * D:(uc + 1) * D], p.ps[2]), "vv")
        p.w(nc.tensor, "vv")
    p.w(nc.sync, "vv")
    p.dma("ds", bass.AP(aux.tensor, aux.offset, [[4096, 128], [1, 2048]]), kcT)
    p.dma("ds", bass.AP(aux.tensor, aux.offset + 2048, [[4096, 128], [1, 2048]]), vc)
    p.quiesce()

    if B1_STAGE < 4:
        p.dma("ds", dst, src)
        p.quiesce()
        return
    kT8 = xg8
    wbb = xng8
    A.rewind(3 * 8192)
    v8 = A.take(8192)
    qTt = A.take(1024)
    attT = A.take(1024)
    probs = A.take(1024)
    pT8 = A.take(1024)
    xstg = A.take(1024)
    fstg = A.take(1024)
    xob = A.take(1024)

    p.dma("ds", wbb, bass.AP(Wk.tensor, Wk.offset, [[D, 128], [128 * D, 8], [1, D]]))
    p.wall("ds")
    for dkc in range(8):
        for th in range(2):
            pdst = p.ps[dkc % 2][:, th * 512:(th + 1) * 512]
            if th == 0:
                p.w(nc.tensor, "vv", lag=0 if dkc == 0 else 1)
            for dc in range(8):
                mm = nc.tensor.matmul(
                    pdst, wbb[:, dc * D + dkc * 128: dc * D + dkc * 128 + 128],
                    xnT8[:, dc * D + th * 512: dc * D + (th + 1) * 512],
                    start=(dc == 0), stop=(dc == 7))
        p.inc(mm, "tt")
        p.w(nc.vector, "tt")
        p.inc(nc.vector.tensor_copy(kT8[:, dkc * D:(dkc + 1) * D],
                                    p.ps[dkc % 2]), "vv")
    p.w(nc.sync, "vv")
    p.w(nc.sync, "tt")
    p.dma("ds", wbb, bass.AP(Wv.tensor, Wv.offset, [[D, 128], [128 * D, 8], [1, D]]))
    p.wall("ds")
    p.w(nc.tensor, "vv")
    for t in range(8):
        for dv in range(2):
            pdst = p.ps[t % 2][:, dv * 512:(dv + 1) * 512]
            if dv == 0:
                p.w(nc.tensor, "vv", lag=0 if t == 0 else 1)
            for dc in range(8):
                mm = nc.tensor.matmul(
                    pdst, xnT8[:, dc * D + t * 128: dc * D + t * 128 + 128],
                    wbb[:, dc * D + dv * 512: dc * D + (dv + 1) * 512],
                    start=(dc == 0), stop=(dc == 7))
        p.inc(mm, "tt")
        p.w(nc.vector, "tt")
        p.inc(nc.vector.tensor_copy(v8[:, t * D:(t + 1) * D], p.ps[t % 2]), "vv")
    p.w(nc.sync, "vv")
    p.w(nc.sync, "tt")
    p.dma("ds", wbb, bass.AP(Wq.tensor, Wq.offset, [[D, 128], [128 * D, 8], [1, D]]))
    p.wall("ds")
    for qt in range(8):
        kw = (qt + 1) * 128
        p.w(nc.tensor, "vv")
        for dqc in range(8):
            pdst = p.ps[2][:, dqc * 128:(dqc + 1) * 128]
            for dc in range(8):
                mm = nc.tensor.matmul(
                    pdst, wbb[:, dc * D + dqc * 128: dc * D + dqc * 128 + 128],
                    xnT8[:, dc * D + qt * 128: dc * D + qt * 128 + 128],
                    start=(dc == 0), stop=(dc == 7))
        p.inc(mm, "tt")
        p.w(nc.vector, "tt")
        p.inc(nc.vector.tensor_copy(qTt, p.ps[2]), "vv")
        p.w(nc.tensor, "vv")
        for h in range(16):
            s_, hc2 = h % 2, h // 2
            lq = qTt[64 * s_:64 * s_ + 64, hc2 * 128:(hc2 + 1) * 128]
            psc = p.ps[0]
            nmm = min(kw, 512)
            mm = nc.tensor.matmul(
                psc[:, 0:nmm], lq,
                kT8[64 * s_:64 * s_ + 64, hc2 * D: hc2 * D + nmm],
                start=True, stop=True)
            if kw > 512:
                mm = nc.tensor.matmul(
                    psc[:, 512:kw], lq,
                    kT8[64 * s_:64 * s_ + 64, hc2 * D + 512: hc2 * D + kw],
                    start=True, stop=True)
            p.inc(mm, "tt")
            p.w(nc.vector, "tt")
            p.inc(nc.vector.tensor_add(psc[:, qt * 128:kw], psc[:, qt * 128:kw],
                                       p.tri), "vv")
            p.w(nc.vector, "vv")
            nm = p.stats[:, 0:1]
            p.inc(nc.vector.reduce_max(nm, psc[:, 0:kw], axis=AXX, negate=True), "vv")
            p.w(nc.vector, "vv")
            p.inc(nc.vector.tensor_scalar_mul(nm, nm, 0.125), "vv")
            p.w(nc.scalar, "vv")
            p.inc(nc.scalar.activation(probs[:, 0:kw], psc[:, 0:kw], ACTF.Exp,
                                       scale=0.125, bias=nm,
                                       accum_out=p.stats[:, 1:2]), "ss")
            p.w(nc.vector, "ss")
            p.inc(nc.vector.reciprocal(p.stats[:, 2:3], p.stats[:, 1:2]), "vv")
            p.w(nc.vector, "vv")
            p.inc(nc.vector.tensor_scalar(probs[:, 0:kw], probs[:, 0:kw],
                                          p.stats[:, 2:3], None, op0=ALU.mult), "vv")
            p.w(nc.tensor, "vv")
            prs = [(probs[:, kt * 128:(kt + 1) * 128], pT8[:, kt * 128:(kt + 1) * 128])
                   for kt in range(qt + 1)]
            p.transposes(prs)
            p.w(nc.tensor, "vv")
            for kt in range(qt + 1):
                mm = nc.tensor.matmul(
                    p.ps[1][64 * s_:64 * s_ + 64, 0:128],
                    v8[:, kt * D + h * 64: kt * D + h * 64 + 64],
                    pT8[:, kt * 128:(kt + 1) * 128],
                    start=(kt == 0), stop=(kt == qt))
            p.inc(mm, "tt")
            if s_ == 1:
                p.w(nc.vector, "tt")
                p.inc(nc.vector.tensor_copy(attT[:, hc2 * 128:(hc2 + 1) * 128],
                                            p.ps[1][:, 0:128]), "vv")
                p.w(nc.tensor, "vv")
        p.w(nc.sync, "vv")
        p.dma("ds", bass.AP(aux.tensor,
                            aux.offset + (128 + 128 * (qt // 4)) * 4096 +
                            (qt % 4) * 1024,
                            [[4096, 128], [1, 1024]]), attT)
    p.quiesce()
    p.dma("ds", wbb, bass.AP(Wo.tensor, Wo.offset, [[D, 128], [128 * D, 8], [1, D]]))
    p.wall("ds")
    for qt in range(8):
        p.w(nc.sync, "vv")
        p.w(nc.sync, "tt")
        p.dma("ds", attT, bass.AP(aux.tensor,
                                  aux.offset + (128 + 128 * (qt // 4)) * 4096 +
                                  (qt % 4) * 1024,
                                  [[4096, 128], [1, 1024]]))
        p.dma("ds", xstg, hbm_tiles(src, qt))
        p.dma("ds", fstg, hbm_tiles(spill, qt))
        p.w(nc.tensor, "ds")
        p.w(nc.tensor, "vv")
        for dh in range(2):
            pdst = p.ps[2][:, dh * 512:(dh + 1) * 512]
            for dvc in range(8):
                mm = nc.tensor.matmul(
                    pdst, attT[:, dvc * 128:(dvc + 1) * 128],
                    wbb[:, dvc * D + dh * 512: dvc * D + (dh + 1) * 512],
                    start=(dvc == 0), stop=(dvc == 7))
        p.inc(mm, "tt")
        p.w(nc.vector, "tt")
        p.w(nc.vector, "ds")
        p.inc(nc.vector.tensor_add(xob, xstg, fstg), "vv")
        p.w(nc.vector, "vv")
        p.inc(nc.vector.tensor_add(xob, xob, p.ps[2]), "vv")
        p.w(nc.sync, "vv")
        p.dma("ds", hbm_tiles(dst, qt), xob)
    p.quiesce()

    if B1_STAGE < 5:
        p.quiesce()
        return
    A.reset()
    wqb = A.take(8192)
    wob = A.take(8192)
    xgc = A.take(4096)
    xngc = A.take(4096)
    xnTc = A.take(4096)
    kcT2 = A.take(2048)
    vc2 = A.take(2048)
    qTg = A.take(4096)
    attTc = A.take(1024)
    probsc = A.take(256)
    pTc = A.take(256)
    fstg2 = A.take(1024)
    xob2 = A.take(1024)
    p.dma("ds", wqb, bass.AP(Wq.tensor, Wq.offset, [[D, 128], [128 * D, 8], [1, D]]))
    p.dma("ds", wob, bass.AP(Wo.tensor, Wo.offset, [[D, 128], [128 * D, 8], [1, D]]))
    p.dma("ds", kcT2, bass.AP(aux.tensor, aux.offset, [[4096, 128], [1, 2048]]))
    p.dma("ds", vc2, bass.AP(aux.tensor, aux.offset + 2048, [[4096, 128], [1, 2048]]))
    p.wall("ds")
    for g in range(2, 8):
        p.w(nc.sync, "vv")
        p.w(nc.sync, "ss")
        p.w(nc.sync, "tt")
        p.dma("ds", xgc, hbm_tiles(src, g * 4, 4))
        p.w(nc.vector, "ds")
        p.w(nc.scalar, "ds")
        p.ln(xgc, xngc, xnTc, 4)
        p.w(nc.tensor, "vv")
        pairs = []
        for t in range(4):
            for dc in range(8):
                pairs.append(
                    (xngc[:, t * D + dc * 128: t * D + dc * 128 + 128],
                     xnTc[:, dc * 512 + t * 128: dc * 512 + t * 128 + 128]))
        p.transposes(pairs)
        p.w(nc.tensor, "vv")
        for dqc in range(8):
            pdst = p.ps[dqc % 2][:, 0:512]
            p.w(nc.tensor, "vv", lag=0 if dqc == 0 else 1)
            for dc in range(8):
                mm = nc.tensor.matmul(
                    pdst, wqb[:, dc * D + dqc * 128: dc * D + dqc * 128 + 128],
                    xnTc[:, dc * 512:(dc + 1) * 512],
                    start=(dc == 0), stop=(dc == 7))
            p.inc(mm, "tt")
            p.w(nc.vector, "tt")
            p.inc(nc.vector.tensor_copy(qTg[:, dqc * 512:(dqc + 1) * 512],
                                        p.ps[dqc % 2][:, 0:512]), "vv")
        p.w(nc.tensor, "vv")
        for t in range(4):
            for h in range(16):
                s_, hc2 = h % 2, h // 2
                lq = qTg[64 * s_:64 * s_ + 64,
                         hc2 * 512 + t * 128: hc2 * 512 + (t + 1) * 128]
                psc = p.ps[0][:, 0:256]
                mm = nc.tensor.matmul(
                    psc, lq, kcT2[64 * s_:64 * s_ + 64, hc2 * 256:(hc2 + 1) * 256],
                    start=True, stop=True)
                p.inc(mm, "tt")
                p.w(nc.vector, "tt")
                nm = p.stats[:, 0:1]
                p.inc(nc.vector.reduce_max(nm, psc, axis=AXX, negate=True), "vv")
                p.w(nc.vector, "vv")
                p.inc(nc.vector.tensor_scalar_mul(nm, nm, 0.125), "vv")
                p.w(nc.scalar, "vv")
                p.inc(nc.scalar.activation(probsc, psc, ACTF.Exp,
                                           scale=0.125, bias=nm,
                                           accum_out=p.stats[:, 1:2]), "ss")
                p.w(nc.vector, "ss")
                p.inc(nc.vector.reciprocal(p.stats[:, 2:3], p.stats[:, 1:2]), "vv")
                p.w(nc.vector, "vv")
                p.inc(nc.vector.tensor_scalar(probsc, probsc, p.stats[:, 2:3],
                                              None, op0=ALU.mult), "vv")
                p.w(nc.tensor, "vv")
                prs = [(probsc[:, u * 128:(u + 1) * 128],
                        pTc[:, u * 128:(u + 1) * 128]) for u in range(2)]
                p.transposes(prs)
                p.w(nc.tensor, "vv")
                for u in range(2):
                    mm = nc.tensor.matmul(
                        p.ps[1][64 * s_:64 * s_ + 64, 0:128],
                        vc2[:, u * D + h * 64: u * D + h * 64 + 64],
                        pTc[:, u * 128:(u + 1) * 128],
                        start=(u == 0), stop=(u == 1))
                p.inc(mm, "tt")
                if s_ == 1:
                    p.w(nc.vector, "tt")
                    p.inc(nc.vector.tensor_copy(
                        attTc[:, hc2 * 128:(hc2 + 1) * 128],
                        p.ps[1][:, 0:128]), "vv")
                    p.w(nc.tensor, "vv")
            p.w(nc.tensor, "vv")
            for dh in range(2):
                pdst = p.ps[2][:, dh * 512:(dh + 1) * 512]
                for dvc in range(8):
                    mm = nc.tensor.matmul(
                        pdst, attTc[:, dvc * 128:(dvc + 1) * 128],
                        wob[:, dvc * D + dh * 512: dvc * D + (dh + 1) * 512],
                        start=(dvc == 0), stop=(dvc == 7))
            p.inc(mm, "tt")
            p.w(nc.sync, "vv")
            p.dma("ds", fstg2, hbm_tiles(spill, g * 4 + t))
            p.w(nc.vector, "tt")
            p.w(nc.vector, "ds")
            p.inc(nc.vector.tensor_add(xob2, xgc[:, t * D:(t + 1) * D], fstg2), "vv")
            p.w(nc.vector, "vv")
            p.inc(nc.vector.tensor_add(xob2, xob2, p.ps[2]), "vv")
            p.w(nc.sync, "vv")
            p.w(nc.tensor, "vv")
            p.dma("ds", hbm_tiles(dst, g * 4 + t), xob2)
    p.quiesce()





def make_toeplitz(filt):
    pad = np.zeros(127 + 2048 + 128, dtype=np.float32)
    pad[127:127 + 2048] = filt
    o = np.arange(17)[:, None, None]
    pq = np.arange(128)
    idx = 127 + 128 * o + pq[None, :, None] - pq[None, None, :]
    return np.ascontiguousarray(pad[idx])


_CACHE = {}


def kernel(**inp):
    key = tuple(ACTIVE)
    if _CACHE.get("key") != key:
        _CACHE["prog"] = build()
        _CACHE["key"] = key
    nc, es, names = _CACHE["prog"]
    x = np.ascontiguousarray(np.asarray(inp["L0_attn_x"], np.float32))
    base = {}
    for n in names:
        if n == "x0":
            continue
        if n == "toep":
            base[n] = make_toeplitz(np.asarray(inp["L0_attn_filt"], np.float32))
        else:
            base[n] = np.ascontiguousarray(np.asarray(inp[n], np.float32))
    in_maps = []
    for c in range(8):
        m = dict(base)
        m["x0"] = x[c]
        in_maps.append(m)
    res = bass_utils.run_bass_kernel_spmd(nc, in_maps, core_ids=list(range(8)))
    return np.stack([res.results[c]["out"] for c in range(8)], axis=0)


# version 03
# speedup vs baseline: 1.1369x; 1.1369x over previous
import numpy as np
from contextlib import ExitStack, contextmanager
import concourse.bass as bass
import concourse.mybir as mybir
import concourse.bass_utils as bass_utils

F32 = mybir.dt.float32
AXX = mybir.AxisListType.X
ALU = mybir.AluOpType
ACTF = mybir.ActivationFunctionType
TS = bass.ts

D = 1024
S = 4096
NT = 32
TILE_B = 128 * D

ACTIVE = (0, 1, 2, 3)
B3_STAGE = 4
B1_STAGE = 5

INPUT_SPECS = {
    "L0_attn_filt": (2048,), "L0_ffn_expert_sel": (16, 1024),
    "L0_ffn_keys": (16, 1024, 256), "L0_ffn_values": (16, 256, 1024),
    "L0_ffn_route_scale": (1,),
    "L1_attn_Wq": (1024, 1024), "L1_attn_Wk": (1024, 1024),
    "L1_attn_Wv": (1024, 1024), "L1_attn_Wo": (1024, 1024),
    "L1_attn_W1": (1024, 512), "L1_attn_b1": (512,),
    "L1_attn_W2": (512, 1024), "L1_attn_b2": (1024,),
    "L1_ffn_gate_w": (2, 1024), "L1_ffn_w1": (2, 1024, 4096),
    "L1_ffn_b1": (2, 4096), "L1_ffn_w2": (2, 4096, 1024),
    "L1_ffn_b2": (2, 1024), "L1_ffn_sw1": (1024, 4096),
    "L1_ffn_sb1": (4096,), "L1_ffn_sw2": (4096, 1024), "L1_ffn_sb2": (1024,),
    "L2_attn_Wq": (1024, 1024), "L2_attn_bq": (1024,),
    "L2_attn_Wk": (1024, 1024), "L2_attn_bk": (1024,),
    "L2_attn_Wv": (1024, 1024), "L2_attn_bv": (1024,),
    "L2_attn_Wo": (1024, 1024), "L2_attn_bo": (1024,),
    "L2_attn_beta": (1,),
    "L2_ffn_gate_w": (16, 1024), "L2_ffn_expert_bias": (16,),
    "L2_ffn_w_gate": (16, 1024, 1024), "L2_ffn_w_up": (16, 1024, 1024),
    "L2_ffn_w_down": (16, 1024, 1024), "L2_ffn_shared_gate_w": (1024, 1024),
    "L2_ffn_shared_up_w": (1024, 1024), "L2_ffn_shared_down_w": (1024, 1024),
    "L3_attn_K": (4096, 1024), "L3_attn_V": (4096, 1024),
    "L3_ffn_weight": (1024, 1024), "L3_ffn_alpha": (1,), "L3_ffn_bias": (1024,),
    "toep": (17, 128, 128),
}


class Arena:
    def __init__(self, ap, ncols):
        self.base_ap = ap
        self.ncols = ncols
        self.off = 0

    def take(self, cols):
        assert self.off + cols <= self.ncols, (self.off, cols, self.ncols)
        v = self.base_ap[:, self.off:self.off + cols]
        self.off += cols
        return v

    def reset(self):
        self.off = 0

    def mark(self):
        return self.off

    def rewind(self, m):
        self.off = m


class P:
    def __init__(self, nc, es):
        self.nc = nc
        self.c = {}
        self.sem = {}
        for name in ("ds", "dg", "tt", "vv", "ss"):
            self.sem[name] = es.enter_context(nc.semaphore(name))
            self.c[name] = 0
        self.lp = None
        self.inputs = {}

    def inp(self, name):
        if name not in self.inputs:
            self.inputs[name] = self.nc.dram_tensor(
                name, list(INPUT_SPECS[name]), F32, kind="ExternalInput")
        return self.inputs[name].ap()

    def dma(self, q, dst, src, slow=False):
        eng = self.nc.sync if q == "ds" else self.nc.gpsimd
        eng.dma_start(dst, src,
                      allow_slow_non_contiguous=slow).then_inc(self.sem[q], 16)
        self.c[q] += 16

    def inc(self, inst, s, n=1):
        inst.then_inc(self.sem[s], n)
        self.c[s] += n
        return inst

    def tgt(self, s, lag=0):
        return self.c[s] - lag

    def w(self, eng, s, lag=0):
        eng.wait_ge(self.sem[s], self.tgt(s, lag))

    def wall(self, s):
        nc = self.nc
        for eng in (nc.tensor, nc.vector, nc.scalar, nc.sync, nc.gpsimd):
            self.w(eng, s)

    def quiesce(self):
        self.wall("ds")
        self.wall("dg")
        self.nc.all_engine_barrier()

    def bar(self):
        self.nc.all_engine_barrier()

    def clear_all(self):
        for s in self.sem:
            self.nc.sync.sem_clear(self.sem[s])
        for s in self.c:
            self.c[s] = 0

    @contextmanager
    def loop(self, n):
        assert self.lp is None
        self.quiesce()
        self.clear_all()
        self.bar()
        ctx = self.nc.Fori(0, n)
        i = ctx.__enter__()
        self.lp = {"i": i}
        yield i
        for s in self.sem:
            self.wall(s)
        self.bar()
        self.clear_all()
        self.bar()
        ctx.__exit__(None, None, None)
        self.lp = None

    def transposes(self, pairs):
        nc = self.nc
        for j, (srcap, dstap) in enumerate(pairs):
            sl = self.psT_slots[j % 2]
            self.w(nc.tensor, "vv", lag=0 if j == 0 else 1)
            self.inc(nc.tensor.transpose(sl, srcap, self.ident), "tt")
            self.w(nc.vector, "tt")
            self.inc(nc.vector.tensor_copy(dstap, sl), "vv")

    def ln(self, xg, xng, sq, n):
        nc = self.nc
        st = self.stats
        x3 = bass.AP(xg.tensor, xg.offset, [xg.ap[0], [D, n], [1, D]])
        xn3 = bass.AP(xng.tensor, xng.offset, [xng.ap[0], [D, n], [1, D]])
        s_sum, s_var = st[:, 0:n], st[:, 8:8 + n]
        s_rs, s_t = st[:, 16:16 + n], st[:, 24:24 + n]
        self.inc(nc.vector.reduce_sum(s_sum, x3, axis=AXX), "vv")
        self.w(nc.vector, "vv")
        self.inc(nc.vector.tensor_scalar_mul(s_t, s_sum, -1.0 / D), "vv")
        self.w(nc.vector, "vv")
        nm_b = bass.AP(st.tensor, s_t.offset, [s_t.ap[0], [1, n], [0, D]])
        self.inc(nc.vector.tensor_tensor(xn3, x3, nm_b, op=ALU.add), "vv")
        self.w(nc.scalar, "vv")
        for t in range(n):
            self.inc(nc.scalar.activation(
                sq[:, t * D:(t + 1) * D], xng[:, t * D:(t + 1) * D],
                ACTF.Square, accum_out=s_var[:, t:t + 1]), "ss")
        self.w(nc.vector, "ss")
        self.inc(nc.vector.tensor_scalar(s_t, s_var, 1.0 / D, 1e-5,
                                         op0=ALU.mult, op1=ALU.add), "vv")
        self.w(nc.scalar, "vv")
        self.inc(nc.scalar.activation(s_var, s_t, ACTF.Ln), "ss")
        self.w(nc.scalar, "ss")
        self.inc(nc.scalar.activation(s_rs, s_var, ACTF.Exp, scale=-0.5), "ss")
        self.w(nc.vector, "ss")
        rs_b = bass.AP(st.tensor, s_rs.offset, [s_rs.ap[0], [1, n], [0, D]])
        self.inc(nc.vector.tensor_tensor(xn3, xn3, rs_b, op=ALU.mult), "vv")


def ap2d(ap, n):
    return bass.AP(ap.tensor, ap.offset, [[n, 1], [1, n]])


def hbm_tiles(t, i, ntile=1):
    if ntile == 1:
        return bass.AP(t.tensor, t.offset + i * TILE_B, [[D, 128], [1, D]])
    return bass.AP(t.tensor, t.offset + i * TILE_B,
                   [[D, 128], [128 * D, ntile], [1, D]])


def fm_cols(t, pc, width, X):
    return bass.AP(t.tensor, t.offset + pc * width,
                   [[X, 128], [128 * X, 8], [1, width]])


def build():
    es = ExitStack()
    nc = bass.Bass("TRN2", target_bir_lowering=False,
                   detect_race_conditions=False)
    p = P(nc, es)

    x0 = nc.dram_tensor("x0", [S, D], F32, kind="ExternalInput").ap()
    out = nc.dram_tensor("out", [S, D], F32, kind="ExternalOutput").ap()
    xa = nc.dram_tensor("xa", [S, D], F32, kind="Internal").ap()
    xb = nc.dram_tensor("xb", [S, D], F32, kind="Internal").ap()
    spill = nc.dram_tensor("spill", [S, D], F32, kind="Internal").ap()
    aux = nc.dram_tensor("aux", [1024, 4096], F32, kind="Internal").ap()
    kT_hbm = nc.dram_tensor("kT_sc", [D, S], F32, kind="Internal").ap()

    ident_h = nc.inline_tensor(np.eye(128, dtype=np.float32), "ident_c")
    onesr_h = nc.inline_tensor(np.ones((1, 128), dtype=np.float32), "onesr_c")
    onesc_h = nc.inline_tensor(np.ones((128, 1), dtype=np.float32), "onesc_c")
    q_ = np.arange(128)
    tri_h = nc.inline_tensor(
        np.where(q_[:, None] >= q_[None, :], 0.0, -3e38).astype(np.float32), "tri_c")
    sm_h = nc.inline_tensor(
        (q_[:, None] <= q_[None, :]).astype(np.float32), "smask_c")

    p.ident = nc.alloc_sbuf_tensor("ident", [128, 128], F32).ap()
    p.onesr = nc.alloc_sbuf_tensor("onesr", [1, 128], F32).ap()
    p.onesc = nc.alloc_sbuf_tensor("onesc", [128, 1], F32).ap()
    p.tri = nc.alloc_sbuf_tensor("tri", [128, 128], F32).ap()
    p.smask = nc.alloc_sbuf_tensor("smask", [128, 128], F32).ap()
    p.stats = nc.alloc_sbuf_tensor("stats", [128, 128], F32).ap()
    p.srow = nc.alloc_sbuf_tensor("srow", [1, 8192], F32).ap()
    p.scol = nc.alloc_sbuf_tensor("scol", [128, 32], F32).ap()
    p.bcols = nc.alloc_sbuf_tensor("bcols", [128, 128], F32).ap()
    big = nc.alloc_sbuf_tensor("big", [128, 44160], F32).ap()
    p.A = Arena(big, 44160)

    p.ps = [nc.alloc_psum_tensor("ps%d" % i, [128, 1024], F32).ap()
            for i in range(4)]
    p.psT_slots = [p.ps[3][:, 0:128], p.ps[3][:, 512:640]]

    p.dma("ds", p.ident, ident_h.ap())
    p.dma("ds", p.onesr, onesr_h.ap())
    p.dma("ds", p.onesc, onesc_h.ap())
    p.dma("ds", p.tri, tri_h.ap())
    p.dma("ds", p.smask, sm_h.ap())
    p.quiesce()

    l2fT = None
    if 2 in ACTIVE:
        l2fT = tuple(
            nc.dram_tensor(nm, shp, F32, kind="Internal").ap()
            for nm, shp in (("l2f_gT", [16 * D, D]), ("l2f_uT", [16 * D, D]),
                            ("l2f_dT", [16 * D, D]), ("l2f_sgT", [D, D]),
                            ("l2f_suT", [D, D]), ("l2f_sdT", [D, D])))
    cur = x0
    free = [xa, xb]
    if 3 in ACTIVE:
        emit_setup_kT(p, kT_hbm)
    if 2 in ACTIVE:
        emit_setup_l2f(p, *l2fT)
    for b in ACTIVE:
        dstb = out if b == max(ACTIVE) else free[0]
        if b == 0:
            emit_b0(p, cur, dstb, spill)
        elif b == 1:
            emit_b1(p, cur, dstb, spill, aux)
        elif b == 2:
            emit_b2(p, cur, dstb, spill, l2fT)
        else:
            emit_b3(p, cur, dstb, kT_hbm)
        if dstb is not out:
            old = cur
            cur = dstb
            free = [free[1], old] if old is not x0 else [free[1], free[0]]
        else:
            cur = dstb
    if cur is not out:
        p.dma("ds", out, cur)
    p.w(nc.sync, "ds")
    p.w(nc.sync, "dg")
    p.bar()
    return nc, es, sorted(p.inputs.keys())


def emit_setup_kT(p, kT_hbm):
    nc = p.nc
    K = p.inp("L3_attn_K")
    A = p.A
    A.reset()
    stage = A.take(1024)
    tbuf = A.take(1024)
    with p.loop(NT):
        i = p.lp["i"]
        p.w(nc.sync, "tt")
        src = bass.AP(K.tensor, K.offset + i * TILE_B, [[D, 128], [1, D]])
        p.dma("ds", stage, src)
        p.w(nc.vector, "ds")
        p.w(nc.tensor, "ds")
        p.transposes([(stage[:, 128 * dc:128 * dc + 128],
                       tbuf[:, 128 * dc:128 * dc + 128]) for dc in range(8)])
        p.w(nc.sync, "vv")
        dst = bass.AP(kT_hbm.tensor, kT_hbm.offset + i * 128,
                      [[S, 128], [128 * S, 8], [1, 128]])
        p.dma("ds", dst, bass.AP(tbuf.tensor, tbuf.offset,
                                 [tbuf.ap[0], [128, 8], [1, 128]]))
    p.quiesce()


def emit_setup_l2f(p, gT, uT, dT, sgT, suT, sdT):
    nc = p.nc
    A = p.A
    A.reset()
    stage = A.take(8192)
    tbuf = A.take(8192)
    wg, wu, wd = p.inp("L2_ffn_w_gate"), p.inp("L2_ffn_w_up"), p.inp("L2_ffn_w_down")
    sg, su, sd = (p.inp("L2_ffn_shared_gate_w"), p.inp("L2_ffn_shared_up_w"),
                  p.inp("L2_ffn_shared_down_w"))

    def one(w_in, w_out, eoff):
        p.w(nc.sync, "tt")
        p.dma("ds", stage, bass.AP(w_in.tensor, eoff,
                                   [[D, 128], [128 * D, 8], [1, D]]))
        p.w(nc.tensor, "ds")
        p.w(nc.vector, "ds")
        pairs = []
        for rc in range(8):
            for cc in range(8):
                pairs.append(
                    (stage[:, rc * D + cc * 128: rc * D + cc * 128 + 128],
                     tbuf[:, cc * D + rc * 128: cc * D + rc * 128 + 128]))
        p.transposes(pairs)
        p.w(nc.sync, "vv")
        p.dma("ds", bass.AP(w_out.tensor, eoff,
                            [[D, 128], [128 * D, 8], [1, D]]), tbuf)

    with p.loop(16):
        eoff = p.lp["i"] * (D * D)
        for w_in, w_out in ((wg, gT), (wu, uT), (wd, dT)):
            one(w_in, w_out, eoff)
    for w_in, w_out in ((sg, sgT), (su, suT), (sd, sdT)):
        one(w_in, w_out, 0)
    p.quiesce()


def emit_b2(p, src, dst, spill, l2fT):
    nc = p.nc
    A = p.A
    A.reset()
    gT, uT, dT, sgT, suT, sdT = l2fT
    Wq, Wk, Wv, Wo = (p.inp("L2_attn_W%s" % s) for s in "qkvo")
    bq, bk, bv, bo = (p.inp("L2_attn_b%s" % s) for s in "qkvo")
    beta = p.inp("L2_attn_beta")
    gw = p.inp("L2_ffn_gate_w")
    ebias = p.inp("L2_ffn_expert_bias")

    wT = {}
    for nm in ("q", "k", "v", "o"):
        wT[nm] = A.take(8192)
    _mk = A.mark()
    stage = A.take(8192)
    for nm, W in (("q", Wq), ("k", Wk), ("v", Wv), ("o", Wo)):
        p.w(nc.sync, "tt")
        p.dma("ds", stage, bass.AP(W.tensor, W.offset, [[D, 128], [128 * D, 8], [1, D]]))
        p.wall("ds")
        pairs = []
        for rc in range(8):
            for cc in range(8):
                pairs.append(
                    (stage[:, rc * D + cc * 128: rc * D + cc * 128 + 128],
                     wT[nm][:, cc * D + rc * 128: cc * D + rc * 128 + 128]))
        p.transposes(pairs)
    p.dma("ds", p.srow[0:1, 0:D], ap2d(bq, D))
    p.dma("ds", p.srow[0:1, D:2 * D], ap2d(bk, D))
    p.dma("ds", p.srow[0:1, 2 * D:3 * D], ap2d(bv, D))
    p.dma("ds", p.srow[0:1, 3 * D:4 * D], ap2d(bo, D))
    p.dma("ds", p.scol[0:1, 6:7], ap2d(beta, 1))
    p.dma("ds", p.srow[0:1, 4 * D:4 * D + 16], ap2d(ebias, 16))
    p.wall("ds")
    p.inc(nc.scalar.activation(p.scol[0:1, 7:8], p.scol[0:1, 6:7], ACTF.Exp), "ss")
    p.w(nc.tensor, "ss")
    p.w(nc.tensor, "vv")
    p.inc(nc.tensor.matmul(p.ps[2][:, 0:1], p.onesr, p.scol[0:1, 7:8],
                           start=True, stop=True), "tt")
    p.inc(nc.tensor.matmul(p.ps[2][:, 4:20], p.onesr, p.srow[0:1, 4 * D:4 * D + 16],
                           start=True, stop=True), "tt")
    p.w(nc.vector, "tt")
    be_col = p.scol[:, 7:8]
    p.inc(nc.vector.tensor_copy(be_col, p.ps[2][:, 0:1]), "vv")
    eb_bc = p.bcols[:, 0:16]
    p.inc(nc.vector.tensor_copy(eb_bc, p.ps[2][:, 4:20]), "vv")
    p.bar()

    A.rewind(_mk)
    xstage = A.take(1024)
    xn = xstage
    xnT = A.take(1024)
    apT = xnT
    pads = A.take(2048)
    qT_pad = A.take(2048)
    kT_pad = A.take(2048)
    v_aug = A.take(1088)
    S_pad = A.take(1088)
    work = A.take(1024)
    num = work
    ap_ = work
    masked = xstage[:, 0:512]
    p.inc(nc.vector.memset(pads, 0.0), "vv")
    p.inc(nc.vector.memset(S_pad, 0.0), "vv")
    p.inc(nc.vector.memset(v_aug, 1.0), "vv")
    p.bar()

    ps_state = p.ps[3][:, 512 + 128:512 + 128 + 272]

    with p.loop(NT):
        c = p.lp["i"]
        coff = c * TILE_B
        p.dma("ds", xstage, bass.AP(src.tensor, src.offset + coff,
                                    [[D, 128], [1, D]]))
        p.w(nc.vector, "ds")
        p.w(nc.scalar, "ds")
        p.ln(xstage, xn, xnT, 1)
        p.w(nc.tensor, "vv")
        p.transposes([(xn[:, dc * 128:(dc + 1) * 128], xnT[:, dc * 128:(dc + 1) * 128])
                      for dc in range(8)])
        pad3 = bass.AP(pads.tensor, pads.offset, [pads.ap[0], [32, 64], [1, 16]])
        for pi, (wname, soff, padT) in enumerate(
                (("q", 0, qT_pad), ("k", D, kT_pad), ("v", 2 * D, None))):
            pd = p.ps[pi % 2]
            p.w(nc.tensor, "ss")
            p.w(nc.tensor, "vv")
            for dh in range(2):
                for dc in range(8):
                    nc.tensor.matmul(
                        pd[:, dh * 512:(dh + 1) * 512],
                        xnT[:, dc * 128:(dc + 1) * 128],
                        wT[wname][:, dc * D + dh * 512: dc * D + (dh + 1) * 512],
                        start=(dc == 0), stop=False)
                mm = nc.tensor.matmul(pd[:, dh * 512:(dh + 1) * 512], p.onesr,
                                      p.srow[0:1, soff + dh * 512: soff + (dh + 1) * 512],
                                      start=False, stop=True)
            p.inc(mm, "tt")
            if padT is not None:
                p.w(nc.scalar, "tt")
                p.w(nc.scalar, "vv")
                p.w(nc.scalar, "tt")
                psrc = bass.AP(pd.tensor, pd.offset, [pd.ap[0], [16, 64], [1, 16]])
                p.inc(nc.scalar.activation(pad3, psrc, ACTF.Exp, scale=be_col), "ss")
                p.w(nc.vector, "ss")
                p.inc(nc.vector.reduce_sum(p.stats[:, 0:64], pad3, axis=AXX), "vv")
                p.w(nc.vector, "vv")
                p.inc(nc.vector.reciprocal(p.stats[:, 0:64], p.stats[:, 0:64]), "vv")
                p.w(nc.vector, "vv")
                r3 = bass.AP(p.stats.tensor, p.stats.offset,
                             [p.stats.ap[0], [1, 64], [0, 16]])
                p.inc(nc.vector.tensor_tensor(pad3, pad3, r3, op=ALU.mult), "vv")
                p.w(nc.tensor, "vv")
                p.transposes([(pads[:, g_ * 128:(g_ + 1) * 128],
                               padT[:, g_ * 128:(g_ + 1) * 128]) for g_ in range(16)])
            else:
                p.w(nc.vector, "tt")
                vsrc = bass.AP(pd.tensor, pd.offset, [pd.ap[0], [16, 64], [1, 16]])
                vdst = bass.AP(v_aug.tensor, v_aug.offset,
                               [v_aug.ap[0], [17, 64], [1, 16]])
                p.inc(nc.vector.tensor_copy(vdst, vsrc), "vv")
                p.w(nc.vector, "vv")
        p.w(nc.tensor, "vv")

        sc_banks = [p.ps[0][:, 0:512], p.ps[0][:, 512:1024],
                    p.ps[1][:, 0:512], p.ps[1][:, 512:1024]]
        for g in range(16):
            p.w(nc.tensor, "vv")
            for s_ in range(4):
                p.inc(nc.tensor.matmul(
                    sc_banks[s_][:, 0:128],
                    kT_pad[32 * s_:32 * s_ + 32, g * 128:(g + 1) * 128],
                    qT_pad[32 * s_:32 * s_ + 32, g * 128:(g + 1) * 128],
                    start=True, stop=True, tile_position=(32 * s_, 0)), "tt")
            p.w(nc.vector, "tt")
            for s_ in range(4):
                p.inc(nc.vector.tensor_tensor(
                    masked[:, s_ * 128:(s_ + 1) * 128], sc_banks[s_][:, 0:128],
                    p.smask, op=ALU.mult), "vv")
            p.w(nc.tensor, "vv")
            pnum = p.ps[2][:, (g % 2) * 512:(g % 2) * 512 + 68]
            for s_ in range(4):
                h = 4 * g + s_
                nc.tensor.matmul(pnum[:, s_ * 17:s_ * 17 + 17],
                                 masked[:, s_ * 128:(s_ + 1) * 128],
                                 v_aug[:, 17 * h:17 * h + 17],
                                 start=True, stop=False)
                mm = nc.tensor.matmul(pnum[:, s_ * 17:s_ * 17 + 17],
                                      qT_pad[:, g * 128:(g + 1) * 128],
                                      S_pad[:, 17 * h:17 * h + 17],
                                      start=False, stop=True)
            p.inc(mm, "tt")
            p.w(nc.vector, "tt")
            nsrc = bass.AP(pnum.tensor, pnum.offset, [pnum.ap[0], [17, 4], [1, 16]])
            ndst = bass.AP(num.tensor, num.offset + g * 64, [num.ap[0], [16, 4], [1, 16]])
            p.inc(nc.vector.tensor_copy(ndst, nsrc), "vv")
            dsrc = bass.AP(pnum.tensor, pnum.offset + 16, [pnum.ap[0], [17, 4], [1, 1]])
            ddst = bass.AP(p.stats.tensor, p.stats.offset + 4 * g,
                           [p.stats.ap[0], [1, 4], [1, 1]])
            p.inc(nc.vector.tensor_copy(ddst, dsrc), "vv")
            for s_ in range(4):
                h = 4 * g + s_
                p.w(nc.tensor, "tt")
                p.inc(nc.tensor.matmul(
                    ps_state[32 * s_:32 * s_ + 16, g * 17:(g + 1) * 17],
                    pads[:, 32 * h:32 * h + 16], v_aug[:, 17 * h:17 * h + 17],
                    start=True, stop=True,
                    tile_position=(0, 32 * s_)), "tt")
        p.w(nc.vector, "tt")
        p.inc(nc.vector.reciprocal(p.stats[:, 64:128], p.stats[:, 0:64]), "vv")
        p.w(nc.vector, "vv")
        n3 = bass.AP(num.tensor, num.offset, [num.ap[0], [16, 64], [1, 16]])
        rb = bass.AP(p.stats.tensor, p.stats.offset + 64,
                     [p.stats.ap[0], [1, 64], [0, 16]])
        p.inc(nc.vector.tensor_tensor(n3, n3, rb, op=ALU.mult), "vv")
        for s_ in range(4):
            sA = ps_state[32 * s_:32 * s_ + 16, :]
            ssrc = bass.AP(sA.tensor, sA.offset, [sA.ap[0], [17, 16], [1, 17]])
            sB = S_pad[32 * s_:32 * s_ + 16, :]
            sdst = bass.AP(sB.tensor, sB.offset + 17 * s_, [sB.ap[0], [68, 16], [1, 17]])
            p.inc(nc.vector.tensor_tensor(sdst, sdst, ssrc, op=ALU.add), "vv")
        p.w(nc.vector, "vv")
        p.w(nc.tensor, "vv")
        p.transposes([(ap_[:, dc * 128:(dc + 1) * 128], apT[:, dc * 128:(dc + 1) * 128])
                      for dc in range(8)])
        p.w(nc.tensor, "vv")
        for dh in range(2):
            pd = p.ps[0][:, dh * 512:(dh + 1) * 512]
            for dc in range(8):
                nc.tensor.matmul(pd, apT[:, dc * 128:(dc + 1) * 128],
                                 wT["o"][:, dc * D + dh * 512: dc * D + (dh + 1) * 512],
                                 start=(dc == 0), stop=False)
            mm = nc.tensor.matmul(pd, p.onesr,
                                  p.srow[0:1, 3 * D + dh * 512: 3 * D + (dh + 1) * 512],
                                  start=False, stop=True)
        p.inc(mm, "tt")
        p.w(nc.vector, "tt")
        p.w(nc.vector, "ds")
        p.inc(nc.vector.tensor_copy(xstage, p.ps[0]), "vv")
        p.w(nc.sync, "vv")
        p.dma("ds", bass.AP(spill.tensor, spill.offset + coff,
                            [[D, 128], [1, D]]), xstage)
    p.quiesce()

    A.rewind(0)
    xg = A.take(4096)
    xng = A.take(4096)
    xnT4 = A.take(4096)
    hT4 = A.take(4096)
    facc = A.take(4096)
    wbuf_pair = [A.take(8192), A.take(8192)]
    hgb = xng
    prb = A.take(128)
    rnk = A.take(128)
    wcmb = A.take(128)
    tmp512 = A.take(512)
    astg = A.take(1024)
    xob = A.take(1024)
    gwT16 = A.take(128)
    st16b = A.take(1024)
    p.dma("ds", bass.AP(st16b.tensor, st16b.offset, [[st16b.ap[0][0], 16], [1, 1024]]),
          bass.AP(gw.tensor, gw.offset, [[1024, 16], [1, 1024]]))
    p.wall("ds")
    for dc in range(8):
        sl = p.psT_slots[dc % 2]
        p.w(nc.tensor, "vv", lag=0 if dc == 0 else 1)
        p.inc(nc.tensor.transpose(
            sl[:, 0:16], bass.AP(st16b.tensor, st16b.offset + dc * 128,
                                 [[st16b.ap[0][0], 16], [1, 128]]),
            p.ident[0:16, 0:16]), "tt")
        p.w(nc.vector, "tt")
        p.inc(nc.vector.tensor_copy(gwT16[:, dc * 16:(dc + 1) * 16], sl[:, 0:16]), "vv")
    p.bar()

    with p.loop(8):
        g = p.lp["i"]
        goff = g * (4 * TILE_B)
        toff = [goff if t == 0 else goff + t * TILE_B for t in range(4)]
        p.dma("ds", xg, bass.AP(src.tensor, src.offset + goff,
                                [[D, 128], [128 * D, 4], [1, D]]))
        p.w(nc.vector, "ds")
        p.w(nc.scalar, "ds")
        p.ln(xg, xng, xnT4, 4)
        p.w(nc.tensor, "vv")
        pairs = []
        for t in range(4):
            for dc in range(8):
                pairs.append(
                    (xng[:, t * D + dc * 128: t * D + dc * 128 + 128],
                     xnT4[:, dc * 512 + t * 128: dc * 512 + t * 128 + 128]))
        p.transposes(pairs)
        p.w(nc.tensor, "vv")
        p.w(nc.tensor, "ss")
        for t in range(4):
            for dc in range(8):
                mm = nc.tensor.matmul(
                    p.ps[2][:, t * 16:(t + 1) * 16],
                    xnT4[:, dc * 512 + t * 128: dc * 512 + t * 128 + 128],
                    gwT16[:, dc * 16:(dc + 1) * 16],
                    start=(dc == 0), stop=(dc == 7))
        p.inc(mm, "tt")
        p.w(nc.scalar, "tt")
        p.inc(nc.scalar.activation(prb[:, 0:64], p.ps[2][:, 0:64],
                                   ACTF.Exp, scale=-1.0), "ss")
        p.w(nc.vector, "ss")
        p.inc(nc.vector.tensor_scalar_add(prb[:, 0:64], prb[:, 0:64], 1.0), "vv")
        p.w(nc.vector, "vv")
        p.inc(nc.vector.reciprocal(prb[:, 0:64], prb[:, 0:64]), "vv")
        p.w(nc.vector, "vv")
        p.inc(nc.vector.tensor_tensor(
            bass.AP(rnk.tensor, rnk.offset + 64, [rnk.ap[0], [16, 4], [1, 16]]),
            bass.AP(prb.tensor, prb.offset, [prb.ap[0], [16, 4], [1, 16]]),
            bass.AP(eb_bc.tensor, eb_bc.offset, [eb_bc.ap[0], [0, 4], [1, 16]]),
            op=ALU.add), "vv")
        p.w(nc.vector, "vv")
        sel = rnk[:, 64:128]
        for t in range(4):
            st_ = sel[:, t * 16:(t + 1) * 16]
            rt = rnk[:, t * 16:(t + 1) * 16]
            p.inc(nc.vector.tensor_scalar(rt, st_, st_[:, 0:1], None, op0=ALU.is_lt), "vv")
            p.w(nc.vector, "vv")
            for e in range(1, 16):
                p.inc(nc.vector.scalar_tensor_tensor(
                    rt, st_, st_[:, e:e + 1], rt, op0=ALU.is_lt, op1=ALU.add), "vv")
                p.w(nc.vector, "vv")
        p.inc(nc.vector.tensor_scalar(rnk[:, 0:64], rnk[:, 0:64], 7.5, None,
                                      op0=ALU.is_lt), "vv")
        p.w(nc.vector, "vv")
        p.inc(nc.vector.tensor_mul(wcmb[:, 0:64], prb[:, 0:64], rnk[:, 0:64]), "vv")
        p.w(nc.vector, "vv")
        w3 = bass.AP(wcmb.tensor, wcmb.offset, [wcmb.ap[0], [16, 4], [1, 16]])
        p.inc(nc.vector.reduce_sum(p.stats[:, 0:4], w3, axis=AXX), "vv")
        p.w(nc.vector, "vv")
        p.inc(nc.vector.reciprocal(p.stats[:, 4:8], p.stats[:, 0:4]), "vv")
        p.w(nc.vector, "vv")
        p.inc(nc.vector.tensor_scalar_mul(p.stats[:, 4:8], p.stats[:, 4:8], 2.826), "vv")
        p.w(nc.vector, "vv")
        rr = bass.AP(p.stats.tensor, p.stats.offset + 4, [p.stats.ap[0], [1, 4], [0, 16]])
        p.inc(nc.vector.tensor_tensor(w3, w3, rr, op=ALU.mult), "vv")
        p.w(nc.scalar, "vv")
        p.w(nc.tensor, "vv")
        p.inc(nc.vector.memset(facc, 0.0), "vv")
        p.w(nc.vector, "vv")

        marks = []
        mi = 0
        for e in range(17):
            if e < 16:
                srcs = ((gT, e), (uT, e), (dT, e))
            else:
                srcs = ((sgT, 0), (suT, 0), (sdT, 0))
            (g_w, g_e), (u_w, u_e), (d_w, d_e) = srcs
            wb = wbuf_pair[mi % 2]
            if mi >= 2:
                nc.sync.wait_ge(p.sem["tt"], marks[mi - 2])
            p.dma("ds", wb, bass.AP(g_w.tensor, g_w.offset + g_e * D * D,
                                    [[D, 128], [128 * D, 8], [1, D]]))
            p.w(nc.tensor, "ds")
            for ic in range(8):
                pd = p.ps[ic % 2][:, 0:512]
                p.w(nc.tensor, "vv", lag=0 if ic == 0 else 1)
                for dc in range(8):
                    mm = nc.tensor.matmul(
                        pd, wb[:, dc * D + ic * 128: dc * D + ic * 128 + 128],
                        xnT4[:, dc * 512:(dc + 1) * 512],
                        start=(dc == 0), stop=(dc == 7))
                p.inc(mm, "tt")
                p.w(nc.vector, "tt")
                p.inc(nc.vector.tensor_copy(hgb[:, ic * 512:(ic + 1) * 512], pd), "vv")
            marks.append(p.c["tt"])
            mi += 1
            wb = wbuf_pair[mi % 2]
            if mi >= 2:
                nc.sync.wait_ge(p.sem["tt"], marks[mi - 2])
            p.dma("ds", wb, bass.AP(u_w.tensor, u_w.offset + u_e * D * D,
                                    [[D, 128], [128 * D, 8], [1, D]]))
            p.w(nc.tensor, "ds")
            for ic in range(8):
                pd = p.ps[ic % 2][:, 0:512]
                p.w(nc.tensor, "ss", lag=0 if ic == 0 else 1)
                p.w(nc.tensor, "vv", lag=0 if ic == 0 else 1)
                for dc in range(8):
                    mm = nc.tensor.matmul(
                        pd, wb[:, dc * D + ic * 128: dc * D + ic * 128 + 128],
                        xnT4[:, dc * 512:(dc + 1) * 512],
                        start=(dc == 0), stop=(dc == 7))
                p.inc(mm, "tt")
                p.w(nc.scalar, "tt")
                p.w(nc.scalar, "vv")
                p.inc(nc.scalar.activation(tmp512, hgb[:, ic * 512:(ic + 1) * 512],
                                           ACTF.Silu), "ss")
                p.w(nc.vector, "ss")
                p.inc(nc.vector.tensor_tensor(hT4[:, ic * 512:(ic + 1) * 512],
                                              tmp512, pd, op=ALU.mult), "vv")
            marks.append(p.c["tt"])
            mi += 1
            wb = wbuf_pair[mi % 2]
            if mi >= 2:
                nc.sync.wait_ge(p.sem["tt"], marks[mi - 2])
            p.dma("ds", wb, bass.AP(d_w.tensor, d_w.offset + d_e * D * D,
                                    [[D, 128], [128 * D, 8], [1, D]]))
            p.w(nc.tensor, "ds")
            p.w(nc.tensor, "vv")
            for t in range(4):
                pd2 = p.ps[2]
                for ic in range(8):
                    for dh in range(2):
                        mm = nc.tensor.matmul(
                            pd2[:, dh * 512:(dh + 1) * 512],
                            hT4[:, ic * 512 + t * 128: ic * 512 + t * 128 + 128],
                            wb[:, ic * D + dh * 512: ic * D + (dh + 1) * 512],
                            start=(ic == 0), stop=(ic == 7))
                p.inc(mm, "tt")
                p.w(nc.vector, "tt")
                if e < 16:
                    p.inc(nc.vector.scalar_tensor_tensor(
                        facc[:, t * D:(t + 1) * D], pd2,
                        wcmb[:, t * 16 + e: t * 16 + e + 1],
                        facc[:, t * D:(t + 1) * D],
                        op0=ALU.mult, op1=ALU.add), "vv")
                else:
                    p.inc(nc.vector.tensor_add(
                        facc[:, t * D:(t + 1) * D],
                        facc[:, t * D:(t + 1) * D], pd2), "vv")
                p.w(nc.tensor, "vv")
            marks.append(p.c["tt"])
            mi += 1
        for t in range(4):
            p.w(nc.sync, "vv")
            p.dma("ds", astg, bass.AP(spill.tensor, spill.offset + toff[t],
                                      [[D, 128], [1, D]]))
            p.w(nc.vector, "ds")
            p.inc(nc.vector.tensor_add(xob, xg[:, t * D:(t + 1) * D],
                                       facc[:, t * D:(t + 1) * D]), "vv")
            p.w(nc.vector, "vv")
            p.inc(nc.vector.tensor_add(xob, xob, astg), "vv")
            p.w(nc.sync, "vv")
            p.dma("ds", bass.AP(dst.tensor, dst.offset + toff[t],
                                [[D, 128], [1, D]]), xob)
    p.quiesce()


def emit_b3(p, src, dst, kT_hbm):
    nc = p.nc
    V = p.inp("L3_attn_V")
    Wt = p.inp("L3_ffn_weight")
    alpha = p.inp("L3_ffn_alpha")
    bias = p.inp("L3_ffn_bias")
    A = p.A
    A.reset()

    if B3_STAGE < 1:
        p.dma("ds", dst, src)
        p.quiesce()
        return
    wefT = A.take(8192)
    _mk = A.mark()
    wq = A.take(8192)
    p.dma("ds", wq, bass.AP(Wt.tensor, Wt.offset, [[D, 128], [128 * D, 8], [1, D]]))
    p.dma("ds", p.srow[:, 0:D], ap2d(bias, D))
    p.dma("ds", p.scol[0:1, 0:1], ap2d(alpha, 1))
    p.wall("ds")
    st = p.stats
    p.inc(nc.vector.reduce_sum(
        st[:, 0:8], bass.AP(wq.tensor, wq.offset, [wq.ap[0], [D, 8], [1, D]]),
        axis=AXX, apply_absolute_value=True), "vv")
    p.w(nc.vector, "vv")
    p.inc(nc.vector.reduce_sum(st[:, 8:9], st[:, 0:8], axis=AXX), "vv")
    p.w(nc.tensor, "vv")
    p.inc(nc.tensor.matmul(p.ps[0][0:1, 0:1], st[:, 8:9], p.onesc,
                           start=True, stop=True), "tt")
    p.w(nc.vector, "tt")
    p.inc(nc.vector.tensor_scalar_mul(st[0:1, 12:13], p.ps[0][0:1, 0:1],
                                      0.7 / (D * D)), "vv")
    p.w(nc.tensor, "vv")
    p.inc(nc.tensor.matmul(p.ps[0][:, 4:5], p.onesr, st[0:1, 12:13],
                           start=True, stop=True), "tt")
    p.inc(nc.tensor.matmul(p.ps[0][:, 5:6], p.onesr, p.scol[0:1, 0:1],
                           start=True, stop=True), "tt")
    p.w(nc.vector, "tt")
    dcol, acol, ndcol = p.scol[:, 1:2], p.scol[:, 2:3], p.scol[:, 3:4]
    p.inc(nc.vector.tensor_copy(dcol, p.ps[0][:, 4:5]), "vv")
    p.inc(nc.vector.tensor_copy(acol, p.ps[0][:, 5:6]), "vv")
    p.w(nc.vector, "vv")
    p.inc(nc.vector.tensor_scalar_mul(ndcol, dcol, -1.0), "vv")
    p.w(nc.vector, "vv")
    tq = A.take(1024)
    for dc in range(8):
        w_ = wq[:, dc * D:(dc + 1) * D]
        p.inc(nc.vector.tensor_scalar(tq, w_, dcol, None, op0=ALU.is_gt), "vv")
        p.inc(nc.vector.tensor_scalar(w_, w_, ndcol, None, op0=ALU.is_lt), "vv")
        p.w(nc.vector, "vv")
        p.inc(nc.vector.tensor_sub(w_, tq, w_), "vv")
        p.w(nc.vector, "vv")
        p.inc(nc.vector.tensor_scalar(w_, w_, acol, None, op0=ALU.mult), "vv")
        p.w(nc.vector, "vv")
    p.w(nc.tensor, "vv")
    pairs = []
    for oc in range(8):
        for dc in range(8):
            pairs.append((wq[:, oc * D + dc * 128: oc * D + dc * 128 + 128],
                          wefT[:, dc * D + oc * 128: dc * D + oc * 128 + 128]))
    p.transposes(pairs)
    p.quiesce()
    A.rewind(_mk)
    if B3_STAGE < 2:
        p.dma("ds", dst, src)
        p.quiesce()
        return

    xg, xng = A.take(4096), A.take(4096)
    xnT = A.take(4096)
    oacc = A.take(4096)
    sq = oacc
    ktb = [A.take(4096), A.take(4096)]
    vtb = [A.take(4096), A.take(4096)]
    ebuf, eT = A.take(512), A.take(512)
    den8 = A.take(32)
    xo = A.take(1024)
    psA = [p.ps[0][:, 0:512], p.ps[0][:, 512:1024]]
    psB = [p.ps[1], p.ps[2]]
    psC = p.ps[1]

    with p.loop(8):
        g = p.lp["i"]
        goff = g * (4 * TILE_B)
        toff = [goff if t == 0 else goff + t * TILE_B for t in range(4)]
        p.w(nc.sync, "vv")
        p.w(nc.sync, "tt")
        p.dma("ds", xg, bass.AP(src.tensor, src.offset + goff,
                                [[D, 128], [128 * D, 4], [1, D]]))
        p.w(nc.vector, "ds")
        p.w(nc.scalar, "ds")
        p.ln(xg, xng, sq, 4)
        p.w(nc.tensor, "vv")
        pairs = []
        for t in range(4):
            for dc in range(8):
                pairs.append(
                    (xng[:, t * D + dc * 128: t * D + dc * 128 + 128],
                     xnT[:, dc * 512 + t * 128: dc * 512 + t * 128 + 128]))
        p.transposes(pairs)
        if B3_STAGE == 2:
            for t in range(4):
                p.w(nc.sync, "vv")
                p.dma("ds", bass.AP(dst.tensor, dst.offset + toff[t],
                                    [[D, 128], [1, D]]), xng[:, t * D:(t + 1) * D])
        if B3_STAGE >= 3:
          p.inc(nc.vector.memset(den8, 0.0), "vv")
          p.w(nc.scalar, "vv")

          for pc in range(8):
            par = pc % 2
            p.w(nc.sync, "tt", lag=24)
            p.dma("ds", ktb[par], fm_cols(kT_hbm, pc, 512, S))
            p.dma("ds", vtb[par],
                  bass.AP(V.tensor, V.offset + pc * 4 * TILE_B,
                          [[D, 128], [128 * D, 4], [1, D]]))
            p.w(nc.tensor, "ds")
            for t in range(4):
                pa = psA[t % 2]
                p.w(nc.tensor, "ss")
                for dc in range(8):
                    mm = nc.tensor.matmul(
                        pa, xnT[:, dc * 512 + t * 128: dc * 512 + t * 128 + 128],
                        ktb[par][:, dc * 512:(dc + 1) * 512],
                        start=(dc == 0), stop=(dc == 7))
                p.inc(mm, "tt")
                p.w(nc.scalar, "tt")
                p.inc(nc.scalar.activation(
                    ebuf, pa, ACTF.Exp, scale=1.0 / 32.0,
                    accum_out=den8[:, t * 8 + pc: t * 8 + pc + 1]), "ss")
                p.w(nc.tensor, "ss")
                p.transposes([(ebuf[:, j * 128:(j + 1) * 128],
                               eT[:, j * 128:(j + 1) * 128]) for j in range(4)])
                p.w(nc.tensor, "vv")
                pb = psB[t % 2]
                for j in range(4):
                    first = (j == 0)
                    last = (j == 3)
                    nc.tensor.matmul(pb[:, 0:512], eT[:, j * 128:(j + 1) * 128],
                                     vtb[par][:, j * D: j * D + 512],
                                     start=first, stop=last)
                    mm = nc.tensor.matmul(pb[:, 512:1024],
                                          eT[:, j * 128:(j + 1) * 128],
                                          vtb[par][:, j * D + 512: (j + 1) * D],
                                          start=first, stop=last)
                p.inc(mm, "tt")
                p.w(nc.vector, "tt")
                if pc == 0:
                    p.inc(nc.vector.tensor_copy(oacc[:, t * D:(t + 1) * D], pb), "vv")
                else:
                    p.inc(nc.vector.tensor_add(
                        oacc[:, t * D:(t + 1) * D],
                        oacc[:, t * D:(t + 1) * D], pb), "vv")

        if B3_STAGE < 3:
            tail_range = ()
        else:
            tail_range = range(4)
            p.w(nc.vector, "ss")
            d3 = bass.AP(den8.tensor, den8.offset, [den8.ap[0], [8, 4], [1, 8]])
            p.inc(nc.vector.reduce_sum(p.stats[:, 32:36], d3, axis=AXX), "vv")
            p.w(nc.vector, "vv")
            p.inc(nc.vector.reciprocal(p.stats[:, 36:40], p.stats[:, 32:36]), "vv")

        for t in tail_range:
            if B3_STAGE == 3:
                p.w(nc.vector, "vv")
                p.inc(nc.vector.tensor_scalar(xo, oacc[:, t * D:(t + 1) * D],
                                              p.stats[:, 36 + t:37 + t], None,
                                              op0=ALU.mult), "vv")
                p.w(nc.sync, "vv")
                p.dma("ds", bass.AP(dst.tensor, dst.offset + toff[t],
                                    [[D, 128], [1, D]]), xo)
                continue
            p.w(nc.tensor, "vv")
            for dc in range(8):
                o_ = dc * 512 + t * 128
                nc.tensor.matmul(psC[:, 0:512], xnT[:, o_: o_ + 128],
                                 wefT[:, dc * D: dc * D + 512],
                                 start=(dc == 0), stop=False)
                nc.tensor.matmul(psC[:, 512:1024], xnT[:, o_: o_ + 128],
                                 wefT[:, dc * D + 512: (dc + 1) * D],
                                 start=(dc == 0), stop=False)
            nc.tensor.matmul(psC[:, 0:512], p.onesr, p.srow[0:1, 0:512],
                             start=False, stop=True)
            p.inc(nc.tensor.matmul(psC[:, 512:1024], p.onesr,
                                   p.srow[0:1, 512:1024],
                                   start=False, stop=True), "tt")
            p.w(nc.vector, "tt")
            p.w(nc.vector, "dg")
            p.inc(nc.vector.scalar_tensor_tensor(
                xo, oacc[:, t * D:(t + 1) * D], p.stats[:, 36 + t:37 + t],
                xg[:, t * D:(t + 1) * D], op0=ALU.mult, op1=ALU.add), "vv")
            p.w(nc.vector, "vv")
            p.inc(nc.vector.tensor_add(xo, xo, psC), "vv")
            p.w(nc.sync, "vv")
            p.dma("ds", bass.AP(dst.tensor, dst.offset + toff[t],
                                [[D, 128], [1, D]]), xo)
    p.quiesce()


def emit_b0(p, src, dst, spill):
    nc = p.nc
    toep = p.inp("toep")
    sel = p.inp("L0_ffn_expert_sel")
    keys = p.inp("L0_ffn_keys")
    values = p.inp("L0_ffn_values")
    rsc = p.inp("L0_ffn_route_scale")
    A = p.A
    A.reset()

    M = A.take(17 * 128)
    _mk1 = A.mark()
    W17 = A.take(17 * 1024)
    xstage = A.take(1024)
    sq1 = A.take(1024)
    cbuf = A.take(1024)
    p.dma("ds", M, bass.AP(toep.tensor, toep.offset,
                           [[128, 128], [128 * 128, 17], [1, 128]]))
    p.wall("ds")

    def load_ln_to_slot(tile_idx, slot):
        p.w(nc.sync, "vv")
        p.w(nc.sync, "ss")
        p.dma("ds", xstage, hbm_tiles(src, tile_idx))
        p.w(nc.vector, "ds")
        p.w(nc.scalar, "ds")
        p.w(nc.vector, "tt")
        p.ln(xstage, W17[:, slot * D:(slot + 1) * D], sq1, 1)

    for w_ in range(17):
        load_ln_to_slot(w_, w_)
    for bt in range(NT):
        omax = min(16, NT - 1 - bt)
        p.w(nc.tensor, "vv")
        for o in range(omax + 1):
            slot = (bt + o) % 17
            for h in range(2):
                mm = nc.tensor.matmul(
                    p.ps[0][:, h * 512:(h + 1) * 512],
                    M[:, o * 128:(o + 1) * 128],
                    W17[:, slot * D + h * 512: slot * D + (h + 1) * 512],
                    start=(o == 0), stop=(o == omax))
        p.inc(mm, "tt")
        p.w(nc.vector, "tt")
        p.w(nc.vector, "ds", lag=0)
        p.inc(nc.vector.tensor_copy(cbuf, p.ps[0]), "vv")
        p.w(nc.sync, "vv")
        p.dma("ds", hbm_tiles(spill, bt), cbuf)
        if bt <= 14:
            load_ln_to_slot(bt + 17, bt % 17)
    p.quiesce()

    A.rewind(_mk1)
    selT = A.take(128)
    st16 = A.take(1024)
    p.dma("ds", bass.AP(st16.tensor, st16.offset, [[st16.ap[0][0], 16], [1, 1024]]),
          bass.AP(sel.tensor, sel.offset, [[1024, 16], [1, 1024]]))
    p.dma("ds", p.scol[0:1, 4:5], ap2d(rsc, 1))
    p.wall("ds")
    for dc in range(8):
        sl = p.psT_slots[dc % 2]
        p.w(nc.tensor, "vv", lag=0 if dc == 0 else 1)
        p.inc(nc.tensor.transpose(
            sl[:, 0:16], bass.AP(st16.tensor, st16.offset + dc * 128,
                                 [[st16.ap[0][0], 16], [1, 128]]),
            p.ident[0:16, 0:16]), "tt")
        p.w(nc.vector, "tt")
        p.inc(nc.vector.tensor_copy(selT[:, dc * 16:(dc + 1) * 16], sl[:, 0:16]), "vv")
    p.w(nc.tensor, "vv")
    p.inc(nc.tensor.matmul(p.ps[0][:, 0:1], p.onesr, p.scol[0:1, 4:5],
                           start=True, stop=True), "tt")
    p.w(nc.vector, "tt")
    rs_col = p.scol[:, 5:6]
    p.inc(nc.vector.tensor_copy(rs_col, p.ps[0][:, 0:1]), "vv")
    p.bar()

    xg8 = A.take(8192)
    xng8 = A.take(8192)
    facc = xng8
    xnT8 = A.take(8192)
    kb = [A.take(2048), A.take(2048)]
    vb = [A.take(2048), A.take(2048)]
    hT = A.take(2048)
    prb = A.take(128)
    rnk = A.take(128)
    wbuf = A.take(128)
    sstage = A.take(1024)
    xob = A.take(1024)

    for g in range(4):
        p.w(nc.sync, "vv")
        p.w(nc.sync, "ss")
        p.w(nc.sync, "tt")
        p.dma("ds", xg8, hbm_tiles(src, g * 8, 8))
        p.w(nc.vector, "ds")
        p.w(nc.scalar, "ds")
        p.ln(xg8, xng8, xnT8, 8)
        p.w(nc.tensor, "vv")
        pairs = []
        for t in range(8):
            for dc in range(8):
                pairs.append(
                    (xng8[:, t * D + dc * 128: t * D + dc * 128 + 128],
                     xnT8[:, dc * D + t * 128: dc * D + t * 128 + 128]))
        p.transposes(pairs)
        p.w(nc.tensor, "ss")
        p.w(nc.tensor, "vv")
        for t in range(8):
            for dc in range(8):
                mm = nc.tensor.matmul(
                    p.ps[2][:, t * 16:(t + 1) * 16],
                    xnT8[:, dc * D + t * 128: dc * D + t * 128 + 128],
                    selT[:, dc * 16:(dc + 1) * 16],
                    start=(dc == 0), stop=(dc == 7))
        p.inc(mm, "tt")
        p.w(nc.scalar, "tt")
        p.inc(nc.scalar.activation(prb, p.ps[2][:, 0:128], ACTF.Exp, scale=-1.0), "ss")
        p.w(nc.vector, "ss")
        p.inc(nc.vector.tensor_scalar_add(prb, prb, 1.0), "vv")
        p.w(nc.vector, "vv")
        p.inc(nc.vector.reciprocal(prb, prb), "vv")
        p.w(nc.vector, "vv")
        for t in range(8):
            pt = prb[:, t * 16:(t + 1) * 16]
            rt = rnk[:, t * 16:(t + 1) * 16]
            p.inc(nc.vector.tensor_scalar(rt, pt, pt[:, 0:1], None, op0=ALU.is_lt), "vv")
            p.w(nc.vector, "vv")
            for e in range(1, 16):
                p.inc(nc.vector.scalar_tensor_tensor(
                    rt, pt, pt[:, e:e + 1], rt, op0=ALU.is_lt, op1=ALU.add), "vv")
                p.w(nc.vector, "vv")
        p.inc(nc.vector.tensor_scalar(rnk, rnk, 3.5, None, op0=ALU.is_lt), "vv")
        p.w(nc.vector, "vv")
        p.inc(nc.vector.tensor_mul(wbuf, prb, rnk), "vv")
        p.w(nc.vector, "vv")
        wv3 = bass.AP(wbuf.tensor, wbuf.offset, [wbuf.ap[0], [16, 8], [1, 16]])
        p.inc(nc.vector.reduce_sum(p.stats[:, 40:48], wv3, axis=AXX), "vv")
        p.w(nc.vector, "vv")
        p.inc(nc.vector.tensor_scalar_max(p.stats[:, 40:48], p.stats[:, 40:48],
                                          1e-9), "vv")
        p.w(nc.vector, "vv")
        p.inc(nc.vector.reciprocal(p.stats[:, 48:56], p.stats[:, 40:48]), "vv")
        p.w(nc.vector, "vv")
        p.inc(nc.vector.tensor_scalar(p.stats[:, 48:56], p.stats[:, 48:56],
                                      rs_col, None, op0=ALU.mult), "vv")
        p.w(nc.vector, "vv")
        rr_b = bass.AP(p.stats.tensor, p.stats[:, 48:56].offset,
                       [p.stats.ap[0], [1, 8], [0, 16]])
        wb3 = bass.AP(wbuf.tensor, wbuf.offset, [wbuf.ap[0], [16, 8], [1, 16]])
        p.inc(nc.vector.tensor_tensor(wb3, wb3, rr_b, op=ALU.mult), "vv")
        p.w(nc.scalar, "vv")
        p.w(nc.tensor, "vv")

        for e in range(16):
            par = e % 2
            p.w(nc.sync, "tt", lag=9)
            p.dma("ds", kb[par], bass.AP(keys.tensor, keys.offset + e * D * 256,
                                         [[256, 128], [128 * 256, 8], [1, 256]]))
            p.dma("ds", vb[par], bass.AP(values.tensor, values.offset + e * 256 * D,
                                         [[D, 128], [128 * D, 2], [1, D]]))
            p.w(nc.tensor, "ds")
            p.w(nc.tensor, "ss")
            for hc in range(2):
                for th in range(2):
                    pdst = p.ps[hc][:, th * 512:(th + 1) * 512]
                    for dc in range(8):
                        mm = nc.tensor.matmul(
                            pdst, kb[par][:, dc * 256 + hc * 128: dc * 256 + hc * 128 + 128],
                            xnT8[:, dc * D + th * 512: dc * D + (th + 1) * 512],
                            start=(dc == 0), stop=(dc == 7))
            p.inc(mm, "tt")
            p.w(nc.scalar, "tt")
            p.w(nc.scalar, "vv")
            for hc in range(2):
                ii = nc.scalar.activation(hT[:, hc * D:(hc + 1) * D],
                                          p.ps[hc], ACTF.Relu)
            p.inc(ii, "ss")
            p.w(nc.tensor, "ss")
            for t in range(8):
                for dh in range(2):
                    for hc in range(2):
                        mm = nc.tensor.matmul(
                            p.ps[2][:, dh * 512:(dh + 1) * 512],
                            hT[:, hc * D + t * 128: hc * D + t * 128 + 128],
                            vb[par][:, hc * D + dh * 512: hc * D + (dh + 1) * 512],
                            start=(hc == 0), stop=(hc == 1))
                p.inc(mm, "tt")
                p.w(nc.vector, "tt")
                if e == 0:
                    p.inc(nc.vector.tensor_scalar(
                        facc[:, t * D:(t + 1) * D], p.ps[2],
                        wbuf[:, t * 16 + e: t * 16 + e + 1], None,
                        op0=ALU.mult), "vv")
                else:
                    p.inc(nc.vector.scalar_tensor_tensor(
                        facc[:, t * D:(t + 1) * D], p.ps[2],
                        wbuf[:, t * 16 + e: t * 16 + e + 1],
                        facc[:, t * D:(t + 1) * D],
                        op0=ALU.mult, op1=ALU.add), "vv")
                p.w(nc.tensor, "vv")
        for t in range(8):
            p.w(nc.sync, "vv")
            p.w(nc.sync, "ss")
            p.dma("ds", sstage, hbm_tiles(spill, g * 8 + t))
            p.w(nc.vector, "ds")
            p.inc(nc.vector.tensor_add(xob, xg8[:, t * D:(t + 1) * D],
                                       facc[:, t * D:(t + 1) * D]), "vv")
            p.w(nc.vector, "vv")
            p.inc(nc.vector.tensor_add(xob, xob, sstage), "vv")
            p.w(nc.sync, "vv")
            p.dma("ds", hbm_tiles(dst, g * 8 + t), xob)
    p.quiesce()


def emit_b1(p, src, dst, spill, aux):
    nc = p.nc
    A = p.A
    A.reset()
    Wq, Wk, Wv, Wo = (p.inp("L1_attn_W%s" % s) for s in "qkvo")
    W1, W2 = p.inp("L1_attn_W1"), p.inp("L1_attn_W2")
    b1a, b2a = p.inp("L1_attn_b1"), p.inp("L1_attn_b2")
    gw = p.inp("L1_ffn_gate_w")
    fw1, fb1 = p.inp("L1_ffn_w1"), p.inp("L1_ffn_b1")
    fw2, fb2 = p.inp("L1_ffn_w2"), p.inp("L1_ffn_b2")
    sw1, sb1 = p.inp("L1_ffn_sw1"), p.inp("L1_ffn_sb1")
    sw2, sb2 = p.inp("L1_ffn_sw2"), p.inp("L1_ffn_sb2")
    bcols = p.bcols
    gwT = p.scol[:, 8:24]

    st16 = A.take(1024)
    p.dma("ds", bcols[:, 0:8], bass.AP(b2a.tensor, b2a.offset, [[1, 128], [128, 8]]), slow=True)
    p.dma("ds", bcols[:, 8:12], bass.AP(b1a.tensor, b1a.offset, [[1, 128], [128, 4]]), slow=True)
    p.dma("ds", bcols[:, 16:48], bass.AP(fb1.tensor, fb1.offset, [[1, 128], [128, 32]]), slow=True)
    p.dma("ds", bcols[:, 48:80],
          bass.AP(fb1.tensor, fb1.offset + 4096, [[1, 128], [128, 32]]), slow=True)
    p.dma("ds", bcols[:, 80:112], bass.AP(sb1.tensor, sb1.offset, [[1, 128], [128, 32]]), slow=True)
    p.dma("ds", p.srow[0:1, 0:D], bass.AP(fb2.tensor, fb2.offset, [[D, 1], [1, D]]))
    p.dma("ds", p.srow[0:1, D:2 * D],
          bass.AP(fb2.tensor, fb2.offset + D, [[D, 1], [1, D]]))
    p.dma("ds", p.srow[0:1, 2 * D:3 * D], ap2d(sb2, D))
    p.dma("ds", bass.AP(st16.tensor, st16.offset, [[st16.ap[0][0], 2], [1, 1024]]),
          bass.AP(gw.tensor, gw.offset, [[1024, 2], [1, 1024]]))
    p.wall("ds")
    for dc in range(8):
        sl = p.psT_slots[dc % 2]
        p.w(nc.tensor, "vv", lag=0 if dc == 0 else 1)
        p.inc(nc.tensor.transpose(
            sl[:, 0:2], bass.AP(st16.tensor, st16.offset + dc * 128,
                                [[st16.ap[0][0], 2], [1, 128]]),
            p.ident[0:2, 0:2]), "tt")
        p.w(nc.vector, "tt")
        p.inc(nc.vector.tensor_copy(gwT[:, dc * 2:dc * 2 + 2], sl[:, 0:2]), "vv")
    p.bar()
    if B1_STAGE < 2:
        p.dma("ds", dst, src)
        p.quiesce()
        return

    xg = A.take(4096)
    xng = A.take(4096)
    xnT = A.take(4096)
    xnT0 = A.take(4096)
    xnT1 = A.take(4096)
    hT = A.take(16384)
    w1c = [A.take(1024), A.take(1024)]
    w2c = [A.take(1024), A.take(1024)]
    facc = xnT0
    xnm = A.take(1024)

    mlps = [
        (fw1.tensor, fw1.offset, fw2.tensor, fw2.offset, 16, 0),
        (fw1.tensor, fw1.offset + D * 4096, fw2.tensor, fw2.offset + 4096 * D, 48, 1),
        (sw1.tensor, sw1.offset, sw2.tensor, sw2.offset, 80, 2),
    ]

    for g in range(8):
        p.w(nc.sync, "vv")
        p.w(nc.sync, "ss")
        p.w(nc.sync, "tt")
        p.dma("ds", xg, hbm_tiles(src, g * 4, 4))
        p.w(nc.vector, "ds")
        p.w(nc.scalar, "ds")
        p.ln(xg, xng, xnT, 4)
        p.w(nc.tensor, "vv")
        pairs = []
        for t in range(4):
            for dc in range(8):
                pairs.append(
                    (xng[:, t * D + dc * 128: t * D + dc * 128 + 128],
                     xnT[:, dc * 512 + t * 128: dc * 512 + t * 128 + 128]))
        p.transposes(pairs)
        p.w(nc.tensor, "vv")
        for t in range(4):
            for dc in range(8):
                mm = nc.tensor.matmul(
                    p.ps[2][:, t * 2:t * 2 + 2],
                    xnT[:, dc * 512 + t * 128: dc * 512 + t * 128 + 128],
                    gwT[:, dc * 2:dc * 2 + 2], start=(dc == 0), stop=(dc == 7))
        p.inc(mm, "tt")
        p.w(nc.vector, "tt")
        m01 = p.stats[:, 48:56]
        gsb = p.stats[:, 40:48]
        p.inc(nc.vector.tensor_copy(gsb, p.ps[2][:, 0:8]), "vv")
        p.w(nc.vector, "vv")
        for t in range(4):
            p.inc(nc.vector.tensor_tensor(
                m01[:, 2 * t:2 * t + 1], gsb[:, 2 * t:2 * t + 1],
                gsb[:, 2 * t + 1:2 * t + 2], op=ALU.is_ge), "vv")
            p.w(nc.vector, "vv")
            p.inc(nc.vector.tensor_scalar(
                m01[:, 2 * t + 1:2 * t + 2], m01[:, 2 * t:2 * t + 1],
                -1.0, 1.0, op0=ALU.mult, op1=ALU.add), "vv")
            p.w(nc.vector, "vv")
        for e, xnTe in ((0, xnT0), (1, xnT1)):
            for t in range(4):
                p.inc(nc.vector.tensor_scalar(
                    xnm, xng[:, t * D:(t + 1) * D],
                    m01[:, 2 * t + e:2 * t + e + 1], None, op0=ALU.mult), "vv")
                p.w(nc.tensor, "vv")
                pairs = [(xnm[:, dc * 128:(dc + 1) * 128],
                          xnTe[:, dc * 512 + t * 128: dc * 512 + t * 128 + 128])
                         for dc in range(8)]
                p.transposes(pairs)
                p.w(nc.vector, "vv")

        for w1t, w1o, w2t, w2o, bcol, mi in mlps:
            xin = (xnT0, xnT1, xnT)[mi]
            for hc in range(32):
                par = hc % 2
                p.w(nc.sync, "tt", lag=1)
                p.dma("ds", w1c[par],
                      bass.AP(w1t, w1o + hc * 128,
                              [[4096, 128], [128 * 4096, 8], [1, 128]]))
                p.w(nc.tensor, "ds")
                p.w(nc.tensor, "ss")
                pdst = p.ps[par][:, 0:512]
                for dc in range(8):
                    mm = nc.tensor.matmul(
                        pdst, w1c[par][:, dc * 128:(dc + 1) * 128],
                        xin[:, dc * 512:(dc + 1) * 512],
                        start=(dc == 0), stop=(dc == 7))
                p.inc(mm, "tt")
                p.w(nc.scalar, "tt")
                p.inc(nc.scalar.activation(
                    hT[:, hc * 512:(hc + 1) * 512], pdst, ACTF.Relu,
                    bias=bcols[:, bcol + hc:bcol + hc + 1]), "ss")
            p.w(nc.tensor, "ss")
            for hc in range(32):
                par = hc % 2
                p.w(nc.sync, "tt", lag=1)
                p.dma("ds", w2c[par],
                      bass.AP(w2t, w2o + hc * 128 * D, [[D, 128], [1, D]]))
                p.w(nc.tensor, "ds")
                for t in range(4):
                    for dh in range(2):
                        mm = nc.tensor.matmul(
                            p.ps[t][:, dh * 512:(dh + 1) * 512],
                            hT[:, hc * 512 + t * 128: hc * 512 + t * 128 + 128],
                            w2c[par][:, dh * 512:(dh + 1) * 512],
                            start=(hc == 0), stop=False)
                p.inc(mm, "tt")
            srow_off = mi * D
            for t in range(4):
                nc.tensor.matmul(p.ps[t][:, 0:512], p.onesr,
                                 p.srow[0:1, srow_off:srow_off + 512],
                                 start=False, stop=True)
                mm = nc.tensor.matmul(p.ps[t][:, 512:1024], p.onesr,
                                      p.srow[0:1, srow_off + 512:srow_off + 1024],
                                      start=False, stop=True)
            p.inc(mm, "tt")
            p.w(nc.vector, "tt")
            for t in range(4):
                if mi == 0:
                    p.inc(nc.vector.tensor_copy(
                        facc[:, t * D:(t + 1) * D], p.ps[t]), "vv")
                else:
                    p.inc(nc.vector.tensor_add(
                        facc[:, t * D:(t + 1) * D],
                        facc[:, t * D:(t + 1) * D], p.ps[t]), "vv")
            p.w(nc.tensor, "vv")
        for t in range(4):
            p.w(nc.sync, "vv")
            p.dma("ds", hbm_tiles(spill, g * 4 + t), facc[:, t * D:(t + 1) * D])
    p.quiesce()
    if B1_STAGE < 3:
        p.dma("ds", dst, src)
        p.quiesce()
        return

    A.reset()
    xg8 = A.take(8192)
    xng8 = A.take(8192)
    xnT8 = A.take(8192)
    compT = xg8
    w1b = xng8[:, 0:4096]
    h1T = xng8[:, 4096:8192]
    compTp = A.take(2048)
    kcT = A.take(2048)
    vc = A.take(2048)
    tmp256 = A.take(256)

    p.dma("ds", xg8, hbm_tiles(src, 0, 8))
    p.w(nc.vector, "ds")
    p.w(nc.scalar, "ds")
    p.ln(xg8, xng8, xnT8, 8)
    p.w(nc.tensor, "vv")
    pairs = []
    for t in range(8):
        for dc in range(8):
            pairs.append(
                (xng8[:, t * D + dc * 128: t * D + dc * 128 + 128],
                 xnT8[:, dc * D + t * 128: dc * D + t * 128 + 128]))
    p.transposes(pairs)
    p.quiesce()
    p.dma("ds", w1b, bass.AP(W1.tensor, W1.offset, [[512, 128], [128 * 512, 8], [1, 512]]))
    p.wall("ds")
    for jc in range(4):
        for th in range(2):
            pdst = p.ps[jc % 2][:, th * 512:(th + 1) * 512]
            p.w(nc.tensor, "ss")
            for dc in range(8):
                mm = nc.tensor.matmul(
                    pdst, w1b[:, dc * 512 + jc * 128: dc * 512 + jc * 128 + 128],
                    xnT8[:, dc * D + th * 512: dc * D + (th + 1) * 512],
                    start=(dc == 0), stop=(dc == 7))
            p.inc(mm, "tt")
            p.w(nc.scalar, "tt")
            p.inc(nc.scalar.activation(
                h1T[:, jc * D + th * 512: jc * D + (th + 1) * 512], pdst,
                ACTF.Relu, bias=bcols[:, 8 + jc:9 + jc]), "ss")
    p.w(nc.sync, "ss")
    p.w(nc.sync, "tt")
    p.dma("ds", w1b, bass.AP(W2.tensor, W2.offset, [[D, 128], [128 * D, 4], [1, D]]))
    p.wall("ds")
    p.w(nc.tensor, "ss")
    for dc in range(8):
        for th in range(2):
            pdst = p.ps[2][:, th * 512:(th + 1) * 512]
            for jc in range(4):
                mm = nc.tensor.matmul(
                    pdst, w1b[:, jc * D + dc * 128: jc * D + dc * 128 + 128],
                    h1T[:, jc * D + th * 512: jc * D + (th + 1) * 512],
                    start=(jc == 0), stop=(jc == 3))
        p.inc(mm, "tt")
        p.w(nc.vector, "tt")
        p.inc(nc.vector.tensor_scalar(
            compT[:, dc * D: dc * D + 512], p.ps[2][:, 0:512],
            bcols[:, dc:dc + 1], None, op0=ALU.add), "vv")
        p.w(nc.vector, "vv")
        p.inc(nc.vector.tensor_scalar(
            compT[:, dc * D + 512: (dc + 1) * D], p.ps[2][:, 512:1024],
            bcols[:, dc:dc + 1], None, op0=ALU.add), "vv")
        p.w(nc.tensor, "vv")
    for dc in range(8):
        a_ = [bass.AP(compT.tensor, compT.offset + dc * D + k, [compT.ap[0], [4, 256]])
              for k in range(4)]
        o_ = compTp[:, dc * 256:(dc + 1) * 256]
        p.inc(nc.vector.tensor_add(o_, a_[0], a_[1]), "vv")
        p.inc(nc.vector.tensor_add(tmp256, a_[2], a_[3]), "vv")
        p.w(nc.vector, "vv")
        p.inc(nc.vector.tensor_add(o_, o_, tmp256), "vv")
        p.w(nc.vector, "vv")
        p.inc(nc.vector.tensor_scalar_mul(o_, o_, 0.25), "vv")
        p.w(nc.vector, "vv")
    p.w(nc.sync, "vv")
    p.w(nc.sync, "ss")
    p.dma("ds", w1b, bass.AP(Wk.tensor, Wk.offset, [[D, 128], [128 * D, 8], [1, 512]]))
    p.dma("ds", h1T, bass.AP(Wk.tensor, Wk.offset + 512,
                             [[D, 128], [128 * D, 8], [1, 512]]))
    p.wall("ds")
    p.w(nc.tensor, "vv")
    for dkc in range(8):
        buf = w1b if dkc < 4 else h1T
        dko = (dkc % 4) * 128
        pdst = p.ps[dkc % 2][:, 0:256]
        p.w(nc.tensor, "vv", lag=0 if dkc == 0 else 1)
        for dc in range(8):
            mm = nc.tensor.matmul(
                pdst, buf[:, dc * 512 + dko: dc * 512 + dko + 128],
                compTp[:, dc * 256:(dc + 1) * 256],
                start=(dc == 0), stop=(dc == 7))
        p.inc(mm, "tt")
        p.w(nc.vector, "tt")
        p.inc(nc.vector.tensor_copy(kcT[:, dkc * 256:(dkc + 1) * 256], pdst), "vv")
    p.w(nc.sync, "vv")
    p.w(nc.sync, "tt")
    p.dma("ds", w1b, bass.AP(Wv.tensor, Wv.offset, [[D, 128], [128 * D, 8], [1, 512]]))
    p.dma("ds", h1T, bass.AP(Wv.tensor, Wv.offset + 512,
                             [[D, 128], [128 * D, 8], [1, 512]]))
    p.wall("ds")
    p.w(nc.tensor, "vv")
    for uc in range(2):
        for dv in range(2):
            buf = w1b if dv == 0 else h1T
            pdst = p.ps[2][:, dv * 512:(dv + 1) * 512]
            for dc in range(8):
                mm = nc.tensor.matmul(
                    pdst, compTp[:, dc * 256 + uc * 128: dc * 256 + uc * 128 + 128],
                    buf[:, dc * 512:(dc + 1) * 512],
                    start=(dc == 0), stop=(dc == 7))
        p.inc(mm, "tt")
        p.w(nc.vector, "tt")
        p.inc(nc.vector.tensor_copy(vc[:, uc * D:(uc + 1) * D], p.ps[2]), "vv")
        p.w(nc.tensor, "vv")
    p.w(nc.sync, "vv")
    p.dma("ds", bass.AP(aux.tensor, aux.offset, [[4096, 128], [1, 2048]]), kcT)
    p.dma("ds", bass.AP(aux.tensor, aux.offset + 2048, [[4096, 128], [1, 2048]]), vc)
    p.quiesce()

    if B1_STAGE < 4:
        p.dma("ds", dst, src)
        p.quiesce()
        return
    kT8 = xg8
    wbb = xng8
    A.rewind(3 * 8192)
    v8 = A.take(8192)
    qTt = A.take(1024)
    attT = A.take(1024)
    probs = A.take(1024)
    pT8 = A.take(1024)
    xstg = A.take(1024)
    fstg = A.take(1024)
    xob = A.take(1024)

    p.dma("ds", wbb, bass.AP(Wk.tensor, Wk.offset, [[D, 128], [128 * D, 8], [1, D]]))
    p.wall("ds")
    for dkc in range(8):
        for th in range(2):
            pdst = p.ps[dkc % 2][:, th * 512:(th + 1) * 512]
            if th == 0:
                p.w(nc.tensor, "vv", lag=0 if dkc == 0 else 1)
            for dc in range(8):
                mm = nc.tensor.matmul(
                    pdst, wbb[:, dc * D + dkc * 128: dc * D + dkc * 128 + 128],
                    xnT8[:, dc * D + th * 512: dc * D + (th + 1) * 512],
                    start=(dc == 0), stop=(dc == 7))
        p.inc(mm, "tt")
        p.w(nc.vector, "tt")
        p.inc(nc.vector.tensor_copy(kT8[:, dkc * D:(dkc + 1) * D],
                                    p.ps[dkc % 2]), "vv")
    p.w(nc.sync, "vv")
    p.w(nc.sync, "tt")
    p.dma("ds", wbb, bass.AP(Wv.tensor, Wv.offset, [[D, 128], [128 * D, 8], [1, D]]))
    p.wall("ds")
    p.w(nc.tensor, "vv")
    for t in range(8):
        for dv in range(2):
            pdst = p.ps[t % 2][:, dv * 512:(dv + 1) * 512]
            if dv == 0:
                p.w(nc.tensor, "vv", lag=0 if t == 0 else 1)
            for dc in range(8):
                mm = nc.tensor.matmul(
                    pdst, xnT8[:, dc * D + t * 128: dc * D + t * 128 + 128],
                    wbb[:, dc * D + dv * 512: dc * D + (dv + 1) * 512],
                    start=(dc == 0), stop=(dc == 7))
        p.inc(mm, "tt")
        p.w(nc.vector, "tt")
        p.inc(nc.vector.tensor_copy(v8[:, t * D:(t + 1) * D], p.ps[t % 2]), "vv")
    p.w(nc.sync, "vv")
    p.w(nc.sync, "tt")
    p.dma("ds", wbb, bass.AP(Wq.tensor, Wq.offset, [[D, 128], [128 * D, 8], [1, D]]))
    p.wall("ds")
    for qt in range(8):
        kw = (qt + 1) * 128
        p.w(nc.tensor, "vv")
        for dqc in range(8):
            pdst = p.ps[2][:, dqc * 128:(dqc + 1) * 128]
            for dc in range(8):
                mm = nc.tensor.matmul(
                    pdst, wbb[:, dc * D + dqc * 128: dc * D + dqc * 128 + 128],
                    xnT8[:, dc * D + qt * 128: dc * D + qt * 128 + 128],
                    start=(dc == 0), stop=(dc == 7))
        p.inc(mm, "tt")
        p.w(nc.vector, "tt")
        p.inc(nc.vector.tensor_copy(qTt, p.ps[2]), "vv")
        p.w(nc.tensor, "vv")
        for h in range(16):
            s_, hc2 = h % 2, h // 2
            lq = qTt[64 * s_:64 * s_ + 64, hc2 * 128:(hc2 + 1) * 128]
            psc = p.ps[0]
            nmm = min(kw, 512)
            mm = nc.tensor.matmul(
                psc[:, 0:nmm], lq,
                kT8[64 * s_:64 * s_ + 64, hc2 * D: hc2 * D + nmm],
                start=True, stop=True)
            if kw > 512:
                mm = nc.tensor.matmul(
                    psc[:, 512:kw], lq,
                    kT8[64 * s_:64 * s_ + 64, hc2 * D + 512: hc2 * D + kw],
                    start=True, stop=True)
            p.inc(mm, "tt")
            p.w(nc.vector, "tt")
            p.inc(nc.vector.tensor_add(psc[:, qt * 128:kw], psc[:, qt * 128:kw],
                                       p.tri), "vv")
            p.w(nc.vector, "vv")
            nm = p.stats[:, 0:1]
            p.inc(nc.vector.reduce_max(nm, psc[:, 0:kw], axis=AXX, negate=True), "vv")
            p.w(nc.vector, "vv")
            p.inc(nc.vector.tensor_scalar_mul(nm, nm, 0.125), "vv")
            p.w(nc.scalar, "vv")
            p.inc(nc.scalar.activation(probs[:, 0:kw], psc[:, 0:kw], ACTF.Exp,
                                       scale=0.125, bias=nm,
                                       accum_out=p.stats[:, 1:2]), "ss")
            p.w(nc.vector, "ss")
            p.inc(nc.vector.reciprocal(p.stats[:, 2:3], p.stats[:, 1:2]), "vv")
            p.w(nc.vector, "vv")
            p.inc(nc.vector.tensor_scalar(probs[:, 0:kw], probs[:, 0:kw],
                                          p.stats[:, 2:3], None, op0=ALU.mult), "vv")
            p.w(nc.tensor, "vv")
            prs = [(probs[:, kt * 128:(kt + 1) * 128], pT8[:, kt * 128:(kt + 1) * 128])
                   for kt in range(qt + 1)]
            p.transposes(prs)
            p.w(nc.tensor, "vv")
            for kt in range(qt + 1):
                mm = nc.tensor.matmul(
                    p.ps[1][64 * s_:64 * s_ + 64, 0:128],
                    v8[:, kt * D + h * 64: kt * D + h * 64 + 64],
                    pT8[:, kt * 128:(kt + 1) * 128],
                    start=(kt == 0), stop=(kt == qt))
            p.inc(mm, "tt")
            if s_ == 1:
                p.w(nc.vector, "tt")
                p.inc(nc.vector.tensor_copy(attT[:, hc2 * 128:(hc2 + 1) * 128],
                                            p.ps[1][:, 0:128]), "vv")
                p.w(nc.tensor, "vv")
        p.w(nc.sync, "vv")
        p.dma("ds", bass.AP(aux.tensor,
                            aux.offset + (128 + 128 * (qt // 4)) * 4096 +
                            (qt % 4) * 1024,
                            [[4096, 128], [1, 1024]]), attT)
    p.quiesce()
    p.dma("ds", wbb, bass.AP(Wo.tensor, Wo.offset, [[D, 128], [128 * D, 8], [1, D]]))
    p.wall("ds")
    for qt in range(8):
        p.w(nc.sync, "vv")
        p.w(nc.sync, "tt")
        p.dma("ds", attT, bass.AP(aux.tensor,
                                  aux.offset + (128 + 128 * (qt // 4)) * 4096 +
                                  (qt % 4) * 1024,
                                  [[4096, 128], [1, 1024]]))
        p.dma("ds", xstg, hbm_tiles(src, qt))
        p.dma("ds", fstg, hbm_tiles(spill, qt))
        p.w(nc.tensor, "ds")
        p.w(nc.tensor, "vv")
        for dh in range(2):
            pdst = p.ps[2][:, dh * 512:(dh + 1) * 512]
            for dvc in range(8):
                mm = nc.tensor.matmul(
                    pdst, attT[:, dvc * 128:(dvc + 1) * 128],
                    wbb[:, dvc * D + dh * 512: dvc * D + (dh + 1) * 512],
                    start=(dvc == 0), stop=(dvc == 7))
        p.inc(mm, "tt")
        p.w(nc.vector, "tt")
        p.w(nc.vector, "ds")
        p.inc(nc.vector.tensor_add(xob, xstg, fstg), "vv")
        p.w(nc.vector, "vv")
        p.inc(nc.vector.tensor_add(xob, xob, p.ps[2]), "vv")
        p.w(nc.sync, "vv")
        p.dma("ds", hbm_tiles(dst, qt), xob)
    p.quiesce()

    if B1_STAGE < 5:
        p.quiesce()
        return
    A.reset()
    wqb = A.take(8192)
    wob = A.take(8192)
    xgc = A.take(4096)
    xngc = A.take(4096)
    xnTc = A.take(4096)
    kcT2 = A.take(2048)
    vc2 = A.take(2048)
    qTg = A.take(4096)
    attTc = A.take(1024)
    probsc = A.take(256)
    pTc = A.take(256)
    fstg2 = A.take(1024)
    xob2 = A.take(1024)
    p.dma("ds", wqb, bass.AP(Wq.tensor, Wq.offset, [[D, 128], [128 * D, 8], [1, D]]))
    p.dma("ds", wob, bass.AP(Wo.tensor, Wo.offset, [[D, 128], [128 * D, 8], [1, D]]))
    p.dma("ds", kcT2, bass.AP(aux.tensor, aux.offset, [[4096, 128], [1, 2048]]))
    p.dma("ds", vc2, bass.AP(aux.tensor, aux.offset + 2048, [[4096, 128], [1, 2048]]))
    p.wall("ds")
    for g in range(2, 8):
        p.w(nc.sync, "vv")
        p.w(nc.sync, "ss")
        p.w(nc.sync, "tt")
        p.dma("ds", xgc, hbm_tiles(src, g * 4, 4))
        p.w(nc.vector, "ds")
        p.w(nc.scalar, "ds")
        p.ln(xgc, xngc, xnTc, 4)
        p.w(nc.tensor, "vv")
        pairs = []
        for t in range(4):
            for dc in range(8):
                pairs.append(
                    (xngc[:, t * D + dc * 128: t * D + dc * 128 + 128],
                     xnTc[:, dc * 512 + t * 128: dc * 512 + t * 128 + 128]))
        p.transposes(pairs)
        p.w(nc.tensor, "vv")
        for dqc in range(8):
            pdst = p.ps[dqc % 2][:, 0:512]
            p.w(nc.tensor, "vv", lag=0 if dqc == 0 else 1)
            for dc in range(8):
                mm = nc.tensor.matmul(
                    pdst, wqb[:, dc * D + dqc * 128: dc * D + dqc * 128 + 128],
                    xnTc[:, dc * 512:(dc + 1) * 512],
                    start=(dc == 0), stop=(dc == 7))
            p.inc(mm, "tt")
            p.w(nc.vector, "tt")
            p.inc(nc.vector.tensor_copy(qTg[:, dqc * 512:(dqc + 1) * 512],
                                        p.ps[dqc % 2][:, 0:512]), "vv")
        p.w(nc.tensor, "vv")
        for t in range(4):
            for h in range(16):
                s_, hc2 = h % 2, h // 2
                lq = qTg[64 * s_:64 * s_ + 64,
                         hc2 * 512 + t * 128: hc2 * 512 + (t + 1) * 128]
                psc = p.ps[0][:, 0:256]
                mm = nc.tensor.matmul(
                    psc, lq, kcT2[64 * s_:64 * s_ + 64, hc2 * 256:(hc2 + 1) * 256],
                    start=True, stop=True)
                p.inc(mm, "tt")
                p.w(nc.vector, "tt")
                nm = p.stats[:, 0:1]
                p.inc(nc.vector.reduce_max(nm, psc, axis=AXX, negate=True), "vv")
                p.w(nc.vector, "vv")
                p.inc(nc.vector.tensor_scalar_mul(nm, nm, 0.125), "vv")
                p.w(nc.scalar, "vv")
                p.inc(nc.scalar.activation(probsc, psc, ACTF.Exp,
                                           scale=0.125, bias=nm,
                                           accum_out=p.stats[:, 1:2]), "ss")
                p.w(nc.vector, "ss")
                p.inc(nc.vector.reciprocal(p.stats[:, 2:3], p.stats[:, 1:2]), "vv")
                p.w(nc.vector, "vv")
                p.inc(nc.vector.tensor_scalar(probsc, probsc, p.stats[:, 2:3],
                                              None, op0=ALU.mult), "vv")
                p.w(nc.tensor, "vv")
                prs = [(probsc[:, u * 128:(u + 1) * 128],
                        pTc[:, u * 128:(u + 1) * 128]) for u in range(2)]
                p.transposes(prs)
                p.w(nc.tensor, "vv")
                for u in range(2):
                    mm = nc.tensor.matmul(
                        p.ps[1][64 * s_:64 * s_ + 64, 0:128],
                        vc2[:, u * D + h * 64: u * D + h * 64 + 64],
                        pTc[:, u * 128:(u + 1) * 128],
                        start=(u == 0), stop=(u == 1))
                p.inc(mm, "tt")
                if s_ == 1:
                    p.w(nc.vector, "tt")
                    p.inc(nc.vector.tensor_copy(
                        attTc[:, hc2 * 128:(hc2 + 1) * 128],
                        p.ps[1][:, 0:128]), "vv")
                    p.w(nc.tensor, "vv")
            p.w(nc.tensor, "vv")
            for dh in range(2):
                pdst = p.ps[2][:, dh * 512:(dh + 1) * 512]
                for dvc in range(8):
                    mm = nc.tensor.matmul(
                        pdst, attTc[:, dvc * 128:(dvc + 1) * 128],
                        wob[:, dvc * D + dh * 512: dvc * D + (dh + 1) * 512],
                        start=(dvc == 0), stop=(dvc == 7))
            p.inc(mm, "tt")
            p.w(nc.sync, "vv")
            p.dma("ds", fstg2, hbm_tiles(spill, g * 4 + t))
            p.w(nc.vector, "tt")
            p.w(nc.vector, "ds")
            p.inc(nc.vector.tensor_add(xob2, xgc[:, t * D:(t + 1) * D], fstg2), "vv")
            p.w(nc.vector, "vv")
            p.inc(nc.vector.tensor_add(xob2, xob2, p.ps[2]), "vv")
            p.w(nc.sync, "vv")
            p.w(nc.tensor, "vv")
            p.dma("ds", hbm_tiles(dst, g * 4 + t), xob2)
    p.quiesce()





def make_toeplitz(filt):
    pad = np.zeros(127 + 2048 + 128, dtype=np.float32)
    pad[127:127 + 2048] = filt
    o = np.arange(17)[:, None, None]
    pq = np.arange(128)
    idx = 127 + 128 * o + pq[None, :, None] - pq[None, None, :]
    return np.ascontiguousarray(pad[idx])


_CACHE = {}


def kernel(**inp):
    key = tuple(ACTIVE)
    if _CACHE.get("key") != key:
        _CACHE["prog"] = build()
        _CACHE["key"] = key
    nc, es, names = _CACHE["prog"]
    x = np.ascontiguousarray(np.asarray(inp["L0_attn_x"], np.float32))
    base = {}
    for n in names:
        if n == "x0":
            continue
        if n == "toep":
            base[n] = make_toeplitz(np.asarray(inp["L0_attn_filt"], np.float32))
        else:
            base[n] = np.ascontiguousarray(np.asarray(inp[n], np.float32))
    in_maps = []
    for c in range(8):
        m = dict(base)
        m["x0"] = x[c]
        in_maps.append(m)
    res = bass_utils.run_bass_kernel_spmd(nc, in_maps, core_ids=list(range(8)))
    return np.stack([res.results[c]["out"] for c in range(8)], axis=0)


# version 04 (final)
# speedup vs baseline: 1.1563x; 1.0171x over previous
import numpy as np
from contextlib import ExitStack, contextmanager
import concourse.bass as bass
import concourse.mybir as mybir
import concourse.bass_utils as bass_utils

F32 = mybir.dt.float32
AXX = mybir.AxisListType.X
ALU = mybir.AluOpType
ACTF = mybir.ActivationFunctionType
TS = bass.ts

D = 1024
S = 4096
NT = 32
TILE_B = 128 * D

ACTIVE = (0, 1, 2, 3)
B3_STAGE = 4
B1_STAGE = 5

INPUT_SPECS = {
    "L0_attn_filt": (2048,), "L0_ffn_expert_sel": (16, 1024),
    "L0_ffn_keys": (16, 1024, 256), "L0_ffn_values": (16, 256, 1024),
    "L0_ffn_route_scale": (1,),
    "L1_attn_Wq": (1024, 1024), "L1_attn_Wk": (1024, 1024),
    "L1_attn_Wv": (1024, 1024), "L1_attn_Wo": (1024, 1024),
    "L1_attn_W1": (1024, 512), "L1_attn_b1": (512,),
    "L1_attn_W2": (512, 1024), "L1_attn_b2": (1024,),
    "L1_ffn_gate_w": (2, 1024), "L1_ffn_w1": (2, 1024, 4096),
    "L1_ffn_b1": (2, 4096), "L1_ffn_w2": (2, 4096, 1024),
    "L1_ffn_b2": (2, 1024), "L1_ffn_sw1": (1024, 4096),
    "L1_ffn_sb1": (4096,), "L1_ffn_sw2": (4096, 1024), "L1_ffn_sb2": (1024,),
    "L2_attn_Wq": (1024, 1024), "L2_attn_bq": (1024,),
    "L2_attn_Wk": (1024, 1024), "L2_attn_bk": (1024,),
    "L2_attn_Wv": (1024, 1024), "L2_attn_bv": (1024,),
    "L2_attn_Wo": (1024, 1024), "L2_attn_bo": (1024,),
    "L2_attn_beta": (1,),
    "L2_ffn_gate_w": (16, 1024), "L2_ffn_expert_bias": (16,),
    "L2_ffn_w_gate": (16, 1024, 1024), "L2_ffn_w_up": (16, 1024, 1024),
    "L2_ffn_w_down": (16, 1024, 1024), "L2_ffn_shared_gate_w": (1024, 1024),
    "L2_ffn_shared_up_w": (1024, 1024), "L2_ffn_shared_down_w": (1024, 1024),
    "L3_attn_K": (4096, 1024), "L3_attn_V": (4096, 1024),
    "L3_ffn_weight": (1024, 1024), "L3_ffn_alpha": (1,), "L3_ffn_bias": (1024,),
    "toep": (17, 128, 128),
}


class Arena:
    def __init__(self, ap, ncols):
        self.base_ap = ap
        self.ncols = ncols
        self.off = 0

    def take(self, cols):
        assert self.off + cols <= self.ncols, (self.off, cols, self.ncols)
        v = self.base_ap[:, self.off:self.off + cols]
        self.off += cols
        return v

    def reset(self):
        self.off = 0

    def mark(self):
        return self.off

    def rewind(self, m):
        self.off = m


class P:
    def __init__(self, nc, es):
        self.nc = nc
        self.c = {}
        self.sem = {}
        for name in ("ds", "dg", "tt", "vv", "ss"):
            self.sem[name] = es.enter_context(nc.semaphore(name))
            self.c[name] = 0
        self.lp = None
        self.inputs = {}

    def inp(self, name):
        if name not in self.inputs:
            self.inputs[name] = self.nc.dram_tensor(
                name, list(INPUT_SPECS[name]), F32, kind="ExternalInput")
        return self.inputs[name].ap()

    def dma(self, q, dst, src, slow=False):
        eng = self.nc.sync if q == "ds" else self.nc.gpsimd
        eng.dma_start(dst, src,
                      allow_slow_non_contiguous=slow).then_inc(self.sem[q], 16)
        self.c[q] += 16

    def inc(self, inst, s, n=1):
        inst.then_inc(self.sem[s], n)
        self.c[s] += n
        return inst

    def tgt(self, s, lag=0):
        return self.c[s] - lag

    def w(self, eng, s, lag=0):
        eng.wait_ge(self.sem[s], self.tgt(s, lag))

    def wall(self, s):
        nc = self.nc
        for eng in (nc.tensor, nc.vector, nc.scalar, nc.sync, nc.gpsimd):
            self.w(eng, s)

    def quiesce(self):
        self.wall("ds")
        self.wall("dg")
        self.nc.all_engine_barrier()

    def bar(self):
        self.nc.all_engine_barrier()

    def clear_all(self):
        for s in self.sem:
            self.nc.sync.sem_clear(self.sem[s])
        for s in self.c:
            self.c[s] = 0

    @contextmanager
    def loop(self, n):
        assert self.lp is None
        self.quiesce()
        self.clear_all()
        self.bar()
        ctx = self.nc.Fori(0, n)
        i = ctx.__enter__()
        self.lp = {"i": i}
        yield i
        for s in self.sem:
            self.wall(s)
        self.bar()
        self.clear_all()
        self.bar()
        ctx.__exit__(None, None, None)
        self.lp = None

    def transposes(self, pairs):
        nc = self.nc
        for j, (srcap, dstap) in enumerate(pairs):
            sl = self.psT_slots[j % 2]
            self.w(nc.tensor, "vv", lag=0 if j == 0 else 1)
            self.inc(nc.tensor.transpose(sl, srcap, self.ident), "tt")
            self.w(nc.vector, "tt")
            self.inc(nc.vector.tensor_copy(dstap, sl), "vv")

    def ln(self, xg, xng, sq, n):
        nc = self.nc
        st = self.stats
        x3 = bass.AP(xg.tensor, xg.offset, [xg.ap[0], [D, n], [1, D]])
        xn3 = bass.AP(xng.tensor, xng.offset, [xng.ap[0], [D, n], [1, D]])
        s_sum, s_var = st[:, 0:n], st[:, 8:8 + n]
        s_rs, s_t = st[:, 16:16 + n], st[:, 24:24 + n]
        self.inc(nc.vector.reduce_sum(s_sum, x3, axis=AXX), "vv")
        self.w(nc.vector, "vv")
        self.inc(nc.vector.tensor_scalar_mul(s_t, s_sum, -1.0 / D), "vv")
        self.w(nc.vector, "vv")
        nm_b = bass.AP(st.tensor, s_t.offset, [s_t.ap[0], [1, n], [0, D]])
        self.inc(nc.vector.tensor_tensor(xn3, x3, nm_b, op=ALU.add), "vv")
        self.w(nc.scalar, "vv")
        for t in range(n):
            self.inc(nc.scalar.activation(
                sq[:, t * D:(t + 1) * D], xng[:, t * D:(t + 1) * D],
                ACTF.Square, accum_out=s_var[:, t:t + 1]), "ss")
        self.w(nc.vector, "ss")
        self.inc(nc.vector.tensor_scalar(s_t, s_var, 1.0 / D, 1e-5,
                                         op0=ALU.mult, op1=ALU.add), "vv")
        self.w(nc.scalar, "vv")
        self.inc(nc.scalar.activation(s_var, s_t, ACTF.Ln), "ss")
        self.w(nc.scalar, "ss")
        self.inc(nc.scalar.activation(s_rs, s_var, ACTF.Exp, scale=-0.5), "ss")
        self.w(nc.vector, "ss")
        rs_b = bass.AP(st.tensor, s_rs.offset, [s_rs.ap[0], [1, n], [0, D]])
        self.inc(nc.vector.tensor_tensor(xn3, xn3, rs_b, op=ALU.mult), "vv")


def ap2d(ap, n):
    return bass.AP(ap.tensor, ap.offset, [[n, 1], [1, n]])


def hbm_tiles(t, i, ntile=1):
    if ntile == 1:
        return bass.AP(t.tensor, t.offset + i * TILE_B, [[D, 128], [1, D]])
    return bass.AP(t.tensor, t.offset + i * TILE_B,
                   [[D, 128], [128 * D, ntile], [1, D]])


def fm_cols(t, pc, width, X):
    return bass.AP(t.tensor, t.offset + pc * width,
                   [[X, 128], [128 * X, 8], [1, width]])


def build():
    es = ExitStack()
    nc = bass.Bass("TRN2", target_bir_lowering=False,
                   detect_race_conditions=False)
    p = P(nc, es)

    x0 = nc.dram_tensor("x0", [S, D], F32, kind="ExternalInput").ap()
    out = nc.dram_tensor("out", [S, D], F32, kind="ExternalOutput").ap()
    xa = nc.dram_tensor("xa", [S, D], F32, kind="Internal").ap()
    xb = nc.dram_tensor("xb", [S, D], F32, kind="Internal").ap()
    spill = nc.dram_tensor("spill", [S, D], F32, kind="Internal").ap()
    aux = nc.dram_tensor("aux", [1024, 4096], F32, kind="Internal").ap()
    kT_hbm = nc.dram_tensor("kT_sc", [D, S], F32, kind="Internal").ap()

    ident_h = nc.inline_tensor(np.eye(128, dtype=np.float32), "ident_c")
    onesr_h = nc.inline_tensor(np.ones((1, 128), dtype=np.float32), "onesr_c")
    onesc_h = nc.inline_tensor(np.ones((128, 1), dtype=np.float32), "onesc_c")
    q_ = np.arange(128)
    tri_h = nc.inline_tensor(
        np.where(q_[:, None] >= q_[None, :], 0.0, -3e38).astype(np.float32), "tri_c")
    sm_h = nc.inline_tensor(
        (q_[:, None] <= q_[None, :]).astype(np.float32), "smask_c")

    p.ident = nc.alloc_sbuf_tensor("ident", [128, 128], F32).ap()
    p.onesr = nc.alloc_sbuf_tensor("onesr", [1, 128], F32).ap()
    p.onesc = nc.alloc_sbuf_tensor("onesc", [128, 1], F32).ap()
    p.tri = nc.alloc_sbuf_tensor("tri", [128, 128], F32).ap()
    p.smask = nc.alloc_sbuf_tensor("smask", [128, 128], F32).ap()
    p.stats = nc.alloc_sbuf_tensor("stats", [128, 128], F32).ap()
    p.srow = nc.alloc_sbuf_tensor("srow", [1, 8192], F32).ap()
    p.scol = nc.alloc_sbuf_tensor("scol", [128, 32], F32).ap()
    p.bcols = nc.alloc_sbuf_tensor("bcols", [128, 128], F32).ap()
    big = nc.alloc_sbuf_tensor("big", [128, 44160], F32).ap()
    p.A = Arena(big, 44160)

    p.ps = [nc.alloc_psum_tensor("ps%d" % i, [128, 1024], F32).ap()
            for i in range(4)]
    p.psT_slots = [p.ps[3][:, 0:128], p.ps[3][:, 512:640]]

    p.dma("ds", p.ident, ident_h.ap())
    p.dma("ds", p.onesr, onesr_h.ap())
    p.dma("ds", p.onesc, onesc_h.ap())
    p.dma("ds", p.tri, tri_h.ap())
    p.dma("ds", p.smask, sm_h.ap())
    p.quiesce()

    l2fT = None
    if 2 in ACTIVE:
        l2fT = tuple(
            nc.dram_tensor(nm, shp, F32, kind="Internal").ap()
            for nm, shp in (("l2f_gT", [16 * D, D]), ("l2f_uT", [16 * D, D]),
                            ("l2f_dT", [16 * D, D]), ("l2f_sgT", [D, D]),
                            ("l2f_suT", [D, D]), ("l2f_sdT", [D, D])))
    cur = x0
    free = [xa, xb]
    if 3 in ACTIVE:
        emit_setup_kT(p, kT_hbm)
    if 2 in ACTIVE:
        emit_setup_l2f(p, *l2fT)
    for b in ACTIVE:
        dstb = out if b == max(ACTIVE) else free[0]
        if b == 0:
            emit_b0(p, cur, dstb, spill)
        elif b == 1:
            emit_b1(p, cur, dstb, spill, aux)
        elif b == 2:
            emit_b2(p, cur, dstb, spill, l2fT)
        else:
            emit_b3(p, cur, dstb, kT_hbm)
        if dstb is not out:
            old = cur
            cur = dstb
            free = [free[1], old] if old is not x0 else [free[1], free[0]]
        else:
            cur = dstb
    if cur is not out:
        p.dma("ds", out, cur)
    p.w(nc.sync, "ds")
    p.w(nc.sync, "dg")
    p.bar()
    return nc, es, sorted(p.inputs.keys())


def emit_setup_kT(p, kT_hbm):
    nc = p.nc
    K = p.inp("L3_attn_K")
    A = p.A
    A.reset()
    stage = A.take(1024)
    tbuf = A.take(1024)
    with p.loop(NT):
        i = p.lp["i"]
        p.w(nc.sync, "tt")
        src = bass.AP(K.tensor, K.offset + i * TILE_B, [[D, 128], [1, D]])
        p.dma("ds", stage, src)
        p.w(nc.vector, "ds")
        p.w(nc.tensor, "ds")
        p.transposes([(stage[:, 128 * dc:128 * dc + 128],
                       tbuf[:, 128 * dc:128 * dc + 128]) for dc in range(8)])
        p.w(nc.sync, "vv")
        dst = bass.AP(kT_hbm.tensor, kT_hbm.offset + i * 128,
                      [[S, 128], [128 * S, 8], [1, 128]])
        p.dma("ds", dst, bass.AP(tbuf.tensor, tbuf.offset,
                                 [tbuf.ap[0], [128, 8], [1, 128]]))
    p.quiesce()


def emit_setup_l2f(p, gT, uT, dT, sgT, suT, sdT):
    nc = p.nc
    A = p.A
    A.reset()
    stage = A.take(8192)
    tbuf = A.take(8192)
    wg, wu, wd = p.inp("L2_ffn_w_gate"), p.inp("L2_ffn_w_up"), p.inp("L2_ffn_w_down")
    sg, su, sd = (p.inp("L2_ffn_shared_gate_w"), p.inp("L2_ffn_shared_up_w"),
                  p.inp("L2_ffn_shared_down_w"))

    def one(w_in, w_out, eoff):
        p.w(nc.sync, "tt")
        p.dma("ds", stage, bass.AP(w_in.tensor, eoff,
                                   [[D, 128], [128 * D, 8], [1, D]]))
        p.w(nc.tensor, "ds")
        p.w(nc.vector, "ds")
        pairs = []
        for rc in range(8):
            for cc in range(8):
                pairs.append(
                    (stage[:, rc * D + cc * 128: rc * D + cc * 128 + 128],
                     tbuf[:, cc * D + rc * 128: cc * D + rc * 128 + 128]))
        p.transposes(pairs)
        p.w(nc.sync, "vv")
        p.dma("ds", bass.AP(w_out.tensor, eoff,
                            [[D, 128], [128 * D, 8], [1, D]]), tbuf)

    with p.loop(16):
        eoff = p.lp["i"] * (D * D)
        for w_in, w_out in ((wg, gT), (wu, uT), (wd, dT)):
            one(w_in, w_out, eoff)
    for w_in, w_out in ((sg, sgT), (su, suT), (sd, sdT)):
        one(w_in, w_out, 0)
    p.quiesce()


def emit_b2(p, src, dst, spill, l2fT):
    nc = p.nc
    A = p.A
    A.reset()
    gT, uT, dT, sgT, suT, sdT = l2fT
    Wq, Wk, Wv, Wo = (p.inp("L2_attn_W%s" % s) for s in "qkvo")
    bq, bk, bv, bo = (p.inp("L2_attn_b%s" % s) for s in "qkvo")
    beta = p.inp("L2_attn_beta")
    gw = p.inp("L2_ffn_gate_w")
    ebias = p.inp("L2_ffn_expert_bias")

    wT = {}
    for nm in ("q", "k", "v", "o"):
        wT[nm] = A.take(8192)
    _mk = A.mark()
    stage = A.take(8192)
    for nm, W in (("q", Wq), ("k", Wk), ("v", Wv), ("o", Wo)):
        p.w(nc.sync, "tt")
        p.dma("ds", stage, bass.AP(W.tensor, W.offset, [[D, 128], [128 * D, 8], [1, D]]))
        p.wall("ds")
        pairs = []
        for rc in range(8):
            for cc in range(8):
                pairs.append(
                    (stage[:, rc * D + cc * 128: rc * D + cc * 128 + 128],
                     wT[nm][:, cc * D + rc * 128: cc * D + rc * 128 + 128]))
        p.transposes(pairs)
    p.dma("ds", p.srow[0:1, 0:D], ap2d(bq, D))
    p.dma("ds", p.srow[0:1, D:2 * D], ap2d(bk, D))
    p.dma("ds", p.srow[0:1, 2 * D:3 * D], ap2d(bv, D))
    p.dma("ds", p.srow[0:1, 3 * D:4 * D], ap2d(bo, D))
    p.dma("ds", p.scol[0:1, 6:7], ap2d(beta, 1))
    p.dma("ds", p.srow[0:1, 4 * D:4 * D + 16], ap2d(ebias, 16))
    p.wall("ds")
    p.inc(nc.scalar.activation(p.scol[0:1, 7:8], p.scol[0:1, 6:7], ACTF.Exp), "ss")
    p.w(nc.tensor, "ss")
    p.w(nc.tensor, "vv")
    p.inc(nc.tensor.matmul(p.ps[2][:, 0:1], p.onesr, p.scol[0:1, 7:8],
                           start=True, stop=True), "tt")
    p.inc(nc.tensor.matmul(p.ps[2][:, 4:20], p.onesr, p.srow[0:1, 4 * D:4 * D + 16],
                           start=True, stop=True), "tt")
    p.w(nc.vector, "tt")
    be_col = p.scol[:, 7:8]
    p.inc(nc.vector.tensor_copy(be_col, p.ps[2][:, 0:1]), "vv")
    eb_bc = p.bcols[:, 0:16]
    p.inc(nc.vector.tensor_copy(eb_bc, p.ps[2][:, 4:20]), "vv")
    p.bar()

    A.rewind(_mk)
    xstage = A.take(1024)
    xn = xstage
    xnT = A.take(1024)
    apT = xnT
    pads = A.take(2048)
    qT_pad = A.take(2048)
    kT_pad = A.take(2048)
    v_aug = A.take(1088)
    S_pad = A.take(1088)
    work = A.take(1024)
    num = work
    ap_ = work
    masked = xstage[:, 0:512]
    p.inc(nc.vector.memset(pads, 0.0), "vv")
    p.inc(nc.vector.memset(S_pad, 0.0), "vv")
    p.inc(nc.vector.memset(v_aug, 1.0), "vv")
    p.bar()

    ps_state = p.ps[3][:, 512 + 128:512 + 128 + 272]

    with p.loop(NT):
        c = p.lp["i"]
        coff = c * TILE_B
        p.dma("ds", xstage, bass.AP(src.tensor, src.offset + coff,
                                    [[D, 128], [1, D]]))
        p.w(nc.vector, "ds")
        p.w(nc.scalar, "ds")
        p.ln(xstage, xn, xnT, 1)
        p.w(nc.tensor, "vv")
        p.transposes([(xn[:, dc * 128:(dc + 1) * 128], xnT[:, dc * 128:(dc + 1) * 128])
                      for dc in range(8)])
        pad3 = bass.AP(pads.tensor, pads.offset, [pads.ap[0], [32, 64], [1, 16]])
        for pi, (wname, soff, padT) in enumerate(
                (("q", 0, qT_pad), ("k", D, kT_pad), ("v", 2 * D, None))):
            pd = p.ps[pi % 2]
            p.w(nc.tensor, "ss")
            p.w(nc.tensor, "vv")
            for dh in range(2):
                for dc in range(8):
                    nc.tensor.matmul(
                        pd[:, dh * 512:(dh + 1) * 512],
                        xnT[:, dc * 128:(dc + 1) * 128],
                        wT[wname][:, dc * D + dh * 512: dc * D + (dh + 1) * 512],
                        start=(dc == 0), stop=False)
                mm = nc.tensor.matmul(pd[:, dh * 512:(dh + 1) * 512], p.onesr,
                                      p.srow[0:1, soff + dh * 512: soff + (dh + 1) * 512],
                                      start=False, stop=True)
            p.inc(mm, "tt")
            if padT is not None:
                p.w(nc.scalar, "tt")
                p.w(nc.scalar, "vv")
                p.w(nc.scalar, "tt")
                psrc = bass.AP(pd.tensor, pd.offset, [pd.ap[0], [16, 64], [1, 16]])
                p.inc(nc.scalar.activation(pad3, psrc, ACTF.Exp, scale=be_col), "ss")
                p.w(nc.vector, "ss")
                p.inc(nc.vector.reduce_sum(p.stats[:, 0:64], pad3, axis=AXX), "vv")
                p.w(nc.vector, "vv")
                p.inc(nc.vector.reciprocal(p.stats[:, 0:64], p.stats[:, 0:64]), "vv")
                p.w(nc.vector, "vv")
                r3 = bass.AP(p.stats.tensor, p.stats.offset,
                             [p.stats.ap[0], [1, 64], [0, 16]])
                p.inc(nc.vector.tensor_tensor(pad3, pad3, r3, op=ALU.mult), "vv")
                p.w(nc.tensor, "vv")
                p.transposes([(pads[:, g_ * 128:(g_ + 1) * 128],
                               padT[:, g_ * 128:(g_ + 1) * 128]) for g_ in range(16)])
            else:
                p.w(nc.vector, "tt")
                vsrc = bass.AP(pd.tensor, pd.offset, [pd.ap[0], [16, 64], [1, 16]])
                vdst = bass.AP(v_aug.tensor, v_aug.offset,
                               [v_aug.ap[0], [17, 64], [1, 16]])
                p.inc(nc.vector.tensor_copy(vdst, vsrc), "vv")
                p.w(nc.vector, "vv")
        p.w(nc.tensor, "vv")

        sc_banks = [p.ps[0][:, 0:512], p.ps[0][:, 512:1024],
                    p.ps[1][:, 0:512], p.ps[1][:, 512:1024]]
        for g in range(16):
            p.w(nc.tensor, "vv")
            for s_ in range(4):
                p.inc(nc.tensor.matmul(
                    sc_banks[s_][:, 0:128],
                    kT_pad[32 * s_:32 * s_ + 32, g * 128:(g + 1) * 128],
                    qT_pad[32 * s_:32 * s_ + 32, g * 128:(g + 1) * 128],
                    start=True, stop=True, tile_position=(32 * s_, 0)), "tt")
            p.w(nc.vector, "tt")
            for s_ in range(4):
                p.inc(nc.vector.tensor_tensor(
                    masked[:, s_ * 128:(s_ + 1) * 128], sc_banks[s_][:, 0:128],
                    p.smask, op=ALU.mult), "vv")
            p.w(nc.tensor, "vv")
            pnum = p.ps[2][:, (g % 2) * 512:(g % 2) * 512 + 68]
            for s_ in range(4):
                h = 4 * g + s_
                nc.tensor.matmul(pnum[:, s_ * 17:s_ * 17 + 17],
                                 masked[:, s_ * 128:(s_ + 1) * 128],
                                 v_aug[:, 17 * h:17 * h + 17],
                                 start=True, stop=False)
                mm = nc.tensor.matmul(pnum[:, s_ * 17:s_ * 17 + 17],
                                      qT_pad[:, g * 128:(g + 1) * 128],
                                      S_pad[:, 17 * h:17 * h + 17],
                                      start=False, stop=True)
            p.inc(mm, "tt")
            p.w(nc.vector, "tt")
            nsrc = bass.AP(pnum.tensor, pnum.offset, [pnum.ap[0], [17, 4], [1, 16]])
            ndst = bass.AP(num.tensor, num.offset + g * 64, [num.ap[0], [16, 4], [1, 16]])
            p.inc(nc.vector.tensor_copy(ndst, nsrc), "vv")
            dsrc = bass.AP(pnum.tensor, pnum.offset + 16, [pnum.ap[0], [17, 4], [1, 1]])
            ddst = bass.AP(p.stats.tensor, p.stats.offset + 4 * g,
                           [p.stats.ap[0], [1, 4], [1, 1]])
            p.inc(nc.vector.tensor_copy(ddst, dsrc), "vv")
            for s_ in range(4):
                h = 4 * g + s_
                p.w(nc.tensor, "tt")
                p.inc(nc.tensor.matmul(
                    ps_state[32 * s_:32 * s_ + 16, g * 17:(g + 1) * 17],
                    pads[:, 32 * h:32 * h + 16], v_aug[:, 17 * h:17 * h + 17],
                    start=True, stop=True,
                    tile_position=(0, 32 * s_)), "tt")
        p.w(nc.vector, "tt")
        p.inc(nc.vector.reciprocal(p.stats[:, 64:128], p.stats[:, 0:64]), "vv")
        p.w(nc.vector, "vv")
        n3 = bass.AP(num.tensor, num.offset, [num.ap[0], [16, 64], [1, 16]])
        rb = bass.AP(p.stats.tensor, p.stats.offset + 64,
                     [p.stats.ap[0], [1, 64], [0, 16]])
        p.inc(nc.vector.tensor_tensor(n3, n3, rb, op=ALU.mult), "vv")
        for s_ in range(4):
            sA = ps_state[32 * s_:32 * s_ + 16, :]
            ssrc = bass.AP(sA.tensor, sA.offset, [sA.ap[0], [17, 16], [1, 17]])
            sB = S_pad[32 * s_:32 * s_ + 16, :]
            sdst = bass.AP(sB.tensor, sB.offset + 17 * s_, [sB.ap[0], [68, 16], [1, 17]])
            p.inc(nc.vector.tensor_tensor(sdst, sdst, ssrc, op=ALU.add), "vv")
        p.w(nc.vector, "vv")
        p.w(nc.tensor, "vv")
        p.transposes([(ap_[:, dc * 128:(dc + 1) * 128], apT[:, dc * 128:(dc + 1) * 128])
                      for dc in range(8)])
        p.w(nc.tensor, "vv")
        for dh in range(2):
            pd = p.ps[0][:, dh * 512:(dh + 1) * 512]
            for dc in range(8):
                nc.tensor.matmul(pd, apT[:, dc * 128:(dc + 1) * 128],
                                 wT["o"][:, dc * D + dh * 512: dc * D + (dh + 1) * 512],
                                 start=(dc == 0), stop=False)
            mm = nc.tensor.matmul(pd, p.onesr,
                                  p.srow[0:1, 3 * D + dh * 512: 3 * D + (dh + 1) * 512],
                                  start=False, stop=True)
        p.inc(mm, "tt")
        p.w(nc.vector, "tt")
        p.w(nc.vector, "ds")
        p.inc(nc.vector.tensor_copy(xstage, p.ps[0]), "vv")
        p.w(nc.sync, "vv")
        p.dma("ds", bass.AP(spill.tensor, spill.offset + coff,
                            [[D, 128], [1, D]]), xstage)
    p.quiesce()

    A.rewind(0)
    xg = A.take(4096)
    xng = A.take(4096)
    xnT4 = A.take(4096)
    hT4 = A.take(4096)
    facc = A.take(4096)
    wbuf_pair = [A.take(8192), A.take(8192)]
    hgb = xng
    prb = A.take(128)
    rnk = A.take(128)
    wcmb = A.take(128)
    tmp512 = A.take(512)
    astg = A.take(1024)
    xob = A.take(1024)
    gwT16 = A.take(128)
    st16b = A.take(1024)
    p.dma("ds", bass.AP(st16b.tensor, st16b.offset, [[st16b.ap[0][0], 16], [1, 1024]]),
          bass.AP(gw.tensor, gw.offset, [[1024, 16], [1, 1024]]))
    p.wall("ds")
    for dc in range(8):
        sl = p.psT_slots[dc % 2]
        p.w(nc.tensor, "vv", lag=0 if dc == 0 else 1)
        p.inc(nc.tensor.transpose(
            sl[:, 0:16], bass.AP(st16b.tensor, st16b.offset + dc * 128,
                                 [[st16b.ap[0][0], 16], [1, 128]]),
            p.ident[0:16, 0:16]), "tt")
        p.w(nc.vector, "tt")
        p.inc(nc.vector.tensor_copy(gwT16[:, dc * 16:(dc + 1) * 16], sl[:, 0:16]), "vv")
    p.bar()

    with p.loop(8):
        g = p.lp["i"]
        goff = g * (4 * TILE_B)
        toff = [goff if t == 0 else goff + t * TILE_B for t in range(4)]
        p.dma("ds", xg, bass.AP(src.tensor, src.offset + goff,
                                [[D, 128], [128 * D, 4], [1, D]]))
        p.w(nc.vector, "ds")
        p.w(nc.scalar, "ds")
        p.ln(xg, xng, xnT4, 4)
        p.w(nc.tensor, "vv")
        pairs = []
        for t in range(4):
            for dc in range(8):
                pairs.append(
                    (xng[:, t * D + dc * 128: t * D + dc * 128 + 128],
                     xnT4[:, dc * 512 + t * 128: dc * 512 + t * 128 + 128]))
        p.transposes(pairs)
        p.w(nc.tensor, "vv")
        p.w(nc.tensor, "ss")
        for t in range(4):
            for dc in range(8):
                mm = nc.tensor.matmul(
                    p.ps[2][:, t * 16:(t + 1) * 16],
                    xnT4[:, dc * 512 + t * 128: dc * 512 + t * 128 + 128],
                    gwT16[:, dc * 16:(dc + 1) * 16],
                    start=(dc == 0), stop=(dc == 7))
        p.inc(mm, "tt")
        p.w(nc.scalar, "tt")
        p.inc(nc.scalar.activation(prb[:, 0:64], p.ps[2][:, 0:64],
                                   ACTF.Exp, scale=-1.0), "ss")
        p.w(nc.vector, "ss")
        p.inc(nc.vector.tensor_scalar_add(prb[:, 0:64], prb[:, 0:64], 1.0), "vv")
        p.w(nc.vector, "vv")
        p.inc(nc.vector.reciprocal(prb[:, 0:64], prb[:, 0:64]), "vv")
        p.w(nc.vector, "vv")
        p.inc(nc.vector.tensor_tensor(
            bass.AP(rnk.tensor, rnk.offset + 64, [rnk.ap[0], [16, 4], [1, 16]]),
            bass.AP(prb.tensor, prb.offset, [prb.ap[0], [16, 4], [1, 16]]),
            bass.AP(eb_bc.tensor, eb_bc.offset, [eb_bc.ap[0], [0, 4], [1, 16]]),
            op=ALU.add), "vv")
        p.w(nc.vector, "vv")
        sel = rnk[:, 64:128]
        for t in range(4):
            st_ = sel[:, t * 16:(t + 1) * 16]
            rt = rnk[:, t * 16:(t + 1) * 16]
            p.inc(nc.vector.tensor_scalar(rt, st_, st_[:, 0:1], None, op0=ALU.is_lt), "vv")
            p.w(nc.vector, "vv")
            for e in range(1, 16):
                p.inc(nc.vector.scalar_tensor_tensor(
                    rt, st_, st_[:, e:e + 1], rt, op0=ALU.is_lt, op1=ALU.add), "vv")
                p.w(nc.vector, "vv")
        p.inc(nc.vector.tensor_scalar(rnk[:, 0:64], rnk[:, 0:64], 7.5, None,
                                      op0=ALU.is_lt), "vv")
        p.w(nc.vector, "vv")
        p.inc(nc.vector.tensor_mul(wcmb[:, 0:64], prb[:, 0:64], rnk[:, 0:64]), "vv")
        p.w(nc.vector, "vv")
        w3 = bass.AP(wcmb.tensor, wcmb.offset, [wcmb.ap[0], [16, 4], [1, 16]])
        p.inc(nc.vector.reduce_sum(p.stats[:, 0:4], w3, axis=AXX), "vv")
        p.w(nc.vector, "vv")
        p.inc(nc.vector.reciprocal(p.stats[:, 4:8], p.stats[:, 0:4]), "vv")
        p.w(nc.vector, "vv")
        p.inc(nc.vector.tensor_scalar_mul(p.stats[:, 4:8], p.stats[:, 4:8], 2.826), "vv")
        p.w(nc.vector, "vv")
        rr = bass.AP(p.stats.tensor, p.stats.offset + 4, [p.stats.ap[0], [1, 4], [0, 16]])
        p.inc(nc.vector.tensor_tensor(w3, w3, rr, op=ALU.mult), "vv")
        p.w(nc.scalar, "vv")
        p.w(nc.tensor, "vv")
        p.inc(nc.vector.memset(facc, 0.0), "vv")
        p.w(nc.vector, "vv")

        marks = []
        mi = 0
        for e in range(17):
            if e < 16:
                srcs = ((gT, e), (uT, e), (dT, e))
            else:
                srcs = ((sgT, 0), (suT, 0), (sdT, 0))
            (g_w, g_e), (u_w, u_e), (d_w, d_e) = srcs
            wb = wbuf_pair[mi % 2]
            if mi >= 2:
                nc.sync.wait_ge(p.sem["tt"], marks[mi - 2])
            p.dma("ds", wb, bass.AP(g_w.tensor, g_w.offset + g_e * D * D,
                                    [[D, 128], [128 * D, 8], [1, D]]))
            p.w(nc.tensor, "ds")
            for ic in range(8):
                pd = p.ps[ic % 2][:, 0:512]
                p.w(nc.tensor, "vv", lag=0 if ic == 0 else 1)
                for dc in range(8):
                    mm = nc.tensor.matmul(
                        pd, wb[:, dc * D + ic * 128: dc * D + ic * 128 + 128],
                        xnT4[:, dc * 512:(dc + 1) * 512],
                        start=(dc == 0), stop=(dc == 7))
                p.inc(mm, "tt")
                p.w(nc.vector, "tt")
                p.inc(nc.vector.tensor_copy(hgb[:, ic * 512:(ic + 1) * 512], pd), "vv")
            marks.append(p.c["tt"])
            mi += 1
            wb = wbuf_pair[mi % 2]
            if mi >= 2:
                nc.sync.wait_ge(p.sem["tt"], marks[mi - 2])
            p.dma("ds", wb, bass.AP(u_w.tensor, u_w.offset + u_e * D * D,
                                    [[D, 128], [128 * D, 8], [1, D]]))
            p.w(nc.tensor, "ds")
            for ic in range(8):
                pd = p.ps[ic % 2][:, 0:512]
                p.w(nc.tensor, "ss", lag=0 if ic == 0 else 1)
                p.w(nc.tensor, "vv", lag=0 if ic == 0 else 1)
                for dc in range(8):
                    mm = nc.tensor.matmul(
                        pd, wb[:, dc * D + ic * 128: dc * D + ic * 128 + 128],
                        xnT4[:, dc * 512:(dc + 1) * 512],
                        start=(dc == 0), stop=(dc == 7))
                p.inc(mm, "tt")
                p.w(nc.scalar, "tt")
                p.w(nc.scalar, "vv")
                p.inc(nc.scalar.activation(tmp512, hgb[:, ic * 512:(ic + 1) * 512],
                                           ACTF.Silu), "ss")
                p.w(nc.vector, "ss")
                p.inc(nc.vector.tensor_tensor(hT4[:, ic * 512:(ic + 1) * 512],
                                              tmp512, pd, op=ALU.mult), "vv")
            marks.append(p.c["tt"])
            mi += 1
            wb = wbuf_pair[mi % 2]
            if mi >= 2:
                nc.sync.wait_ge(p.sem["tt"], marks[mi - 2])
            p.dma("ds", wb, bass.AP(d_w.tensor, d_w.offset + d_e * D * D,
                                    [[D, 128], [128 * D, 8], [1, D]]))
            p.w(nc.tensor, "ds")
            p.w(nc.tensor, "vv")
            for t in range(4):
                pd2 = p.ps[2]
                for ic in range(8):
                    for dh in range(2):
                        mm = nc.tensor.matmul(
                            pd2[:, dh * 512:(dh + 1) * 512],
                            hT4[:, ic * 512 + t * 128: ic * 512 + t * 128 + 128],
                            wb[:, ic * D + dh * 512: ic * D + (dh + 1) * 512],
                            start=(ic == 0), stop=(ic == 7))
                p.inc(mm, "tt")
                p.w(nc.vector, "tt")
                if e < 16:
                    p.inc(nc.vector.scalar_tensor_tensor(
                        facc[:, t * D:(t + 1) * D], pd2,
                        wcmb[:, t * 16 + e: t * 16 + e + 1],
                        facc[:, t * D:(t + 1) * D],
                        op0=ALU.mult, op1=ALU.add), "vv")
                else:
                    p.inc(nc.vector.tensor_add(
                        facc[:, t * D:(t + 1) * D],
                        facc[:, t * D:(t + 1) * D], pd2), "vv")
                p.w(nc.tensor, "vv")
            marks.append(p.c["tt"])
            mi += 1
        for t in range(4):
            p.w(nc.sync, "vv")
            p.dma("ds", astg, bass.AP(spill.tensor, spill.offset + toff[t],
                                      [[D, 128], [1, D]]))
            p.w(nc.vector, "ds")
            p.inc(nc.vector.tensor_add(xob, xg[:, t * D:(t + 1) * D],
                                       facc[:, t * D:(t + 1) * D]), "vv")
            p.w(nc.vector, "vv")
            p.inc(nc.vector.tensor_add(xob, xob, astg), "vv")
            p.w(nc.sync, "vv")
            p.dma("ds", bass.AP(dst.tensor, dst.offset + toff[t],
                                [[D, 128], [1, D]]), xob)
    p.quiesce()


def emit_b3(p, src, dst, kT_hbm):
    nc = p.nc
    V = p.inp("L3_attn_V")
    Wt = p.inp("L3_ffn_weight")
    alpha = p.inp("L3_ffn_alpha")
    bias = p.inp("L3_ffn_bias")
    A = p.A
    A.reset()

    if B3_STAGE < 1:
        p.dma("ds", dst, src)
        p.quiesce()
        return
    wefT = A.take(8192)
    _mk = A.mark()
    wq = A.take(8192)
    p.dma("ds", wq, bass.AP(Wt.tensor, Wt.offset, [[D, 128], [128 * D, 8], [1, D]]))
    p.dma("ds", p.srow[:, 0:D], ap2d(bias, D))
    p.dma("ds", p.scol[0:1, 0:1], ap2d(alpha, 1))
    p.wall("ds")
    st = p.stats
    p.inc(nc.vector.reduce_sum(
        st[:, 0:8], bass.AP(wq.tensor, wq.offset, [wq.ap[0], [D, 8], [1, D]]),
        axis=AXX, apply_absolute_value=True), "vv")
    p.w(nc.vector, "vv")
    p.inc(nc.vector.reduce_sum(st[:, 8:9], st[:, 0:8], axis=AXX), "vv")
    p.w(nc.tensor, "vv")
    p.inc(nc.tensor.matmul(p.ps[0][0:1, 0:1], st[:, 8:9], p.onesc,
                           start=True, stop=True), "tt")
    p.w(nc.vector, "tt")
    p.inc(nc.vector.tensor_scalar_mul(st[0:1, 12:13], p.ps[0][0:1, 0:1],
                                      0.7 / (D * D)), "vv")
    p.w(nc.tensor, "vv")
    p.inc(nc.tensor.matmul(p.ps[0][:, 4:5], p.onesr, st[0:1, 12:13],
                           start=True, stop=True), "tt")
    p.inc(nc.tensor.matmul(p.ps[0][:, 5:6], p.onesr, p.scol[0:1, 0:1],
                           start=True, stop=True), "tt")
    p.w(nc.vector, "tt")
    dcol, acol, ndcol = p.scol[:, 1:2], p.scol[:, 2:3], p.scol[:, 3:4]
    p.inc(nc.vector.tensor_copy(dcol, p.ps[0][:, 4:5]), "vv")
    p.inc(nc.vector.tensor_copy(acol, p.ps[0][:, 5:6]), "vv")
    p.w(nc.vector, "vv")
    p.inc(nc.vector.tensor_scalar_mul(ndcol, dcol, -1.0), "vv")
    p.w(nc.vector, "vv")
    tq = A.take(1024)
    for dc in range(8):
        w_ = wq[:, dc * D:(dc + 1) * D]
        p.inc(nc.vector.tensor_scalar(tq, w_, dcol, None, op0=ALU.is_gt), "vv")
        p.inc(nc.vector.tensor_scalar(w_, w_, ndcol, None, op0=ALU.is_lt), "vv")
        p.w(nc.vector, "vv")
        p.inc(nc.vector.tensor_sub(w_, tq, w_), "vv")
        p.w(nc.vector, "vv")
        p.inc(nc.vector.tensor_scalar(w_, w_, acol, None, op0=ALU.mult), "vv")
        p.w(nc.vector, "vv")
    p.w(nc.tensor, "vv")
    pairs = []
    for oc in range(8):
        for dc in range(8):
            pairs.append((wq[:, oc * D + dc * 128: oc * D + dc * 128 + 128],
                          wefT[:, dc * D + oc * 128: dc * D + oc * 128 + 128]))
    p.transposes(pairs)
    p.quiesce()
    A.rewind(_mk)
    if B3_STAGE < 2:
        p.dma("ds", dst, src)
        p.quiesce()
        return

    xg, xng = A.take(4096), A.take(4096)
    xnT = A.take(4096)
    oacc = A.take(4096)
    sq = oacc
    ktb = [A.take(4096), A.take(4096)]
    vtb = [A.take(4096), A.take(4096)]
    ebuf, eT = A.take(512), A.take(512)
    den8 = A.take(32)
    xo = A.take(1024)
    psA = [p.ps[0][:, 0:512], p.ps[0][:, 512:1024]]
    psB = [p.ps[1], p.ps[2]]
    psC = p.ps[1]

    with p.loop(8):
        g = p.lp["i"]
        goff = g * (4 * TILE_B)
        toff = [goff if t == 0 else goff + t * TILE_B for t in range(4)]
        p.w(nc.sync, "vv")
        p.w(nc.sync, "tt")
        p.dma("ds", xg, bass.AP(src.tensor, src.offset + goff,
                                [[D, 128], [128 * D, 4], [1, D]]))
        p.w(nc.vector, "ds")
        p.w(nc.scalar, "ds")
        p.ln(xg, xng, sq, 4)
        p.w(nc.tensor, "vv")
        pairs = []
        for t in range(4):
            for dc in range(8):
                pairs.append(
                    (xng[:, t * D + dc * 128: t * D + dc * 128 + 128],
                     xnT[:, dc * 512 + t * 128: dc * 512 + t * 128 + 128]))
        p.transposes(pairs)
        if B3_STAGE == 2:
            for t in range(4):
                p.w(nc.sync, "vv")
                p.dma("ds", bass.AP(dst.tensor, dst.offset + toff[t],
                                    [[D, 128], [1, D]]), xng[:, t * D:(t + 1) * D])
        if B3_STAGE >= 3:
          p.inc(nc.vector.memset(den8, 0.0), "vv")
          p.w(nc.scalar, "vv")

          for pc in range(8):
            par = pc % 2
            p.w(nc.sync, "tt", lag=24)
            p.dma("ds", ktb[par], fm_cols(kT_hbm, pc, 512, S))
            p.dma("ds", vtb[par],
                  bass.AP(V.tensor, V.offset + pc * 4 * TILE_B,
                          [[D, 128], [128 * D, 4], [1, D]]))
            p.w(nc.tensor, "ds")
            for t in range(4):
                pa = psA[t % 2]
                p.w(nc.tensor, "ss", lag=1)
                for dc in range(8):
                    mm = nc.tensor.matmul(
                        pa, xnT[:, dc * 512 + t * 128: dc * 512 + t * 128 + 128],
                        ktb[par][:, dc * 512:(dc + 1) * 512],
                        start=(dc == 0), stop=(dc == 7))
                p.inc(mm, "tt")
                p.w(nc.scalar, "tt")
                p.inc(nc.scalar.activation(
                    ebuf, pa, ACTF.Exp, scale=1.0 / 32.0,
                    accum_out=den8[:, t * 8 + pc: t * 8 + pc + 1]), "ss")
                p.w(nc.tensor, "ss")
                p.transposes([(ebuf[:, j * 128:(j + 1) * 128],
                               eT[:, j * 128:(j + 1) * 128]) for j in range(4)])
                p.w(nc.tensor, "vv")
                pb = psB[t % 2]
                for j in range(4):
                    first = (j == 0)
                    last = (j == 3)
                    nc.tensor.matmul(pb[:, 0:512], eT[:, j * 128:(j + 1) * 128],
                                     vtb[par][:, j * D: j * D + 512],
                                     start=first, stop=last)
                    mm = nc.tensor.matmul(pb[:, 512:1024],
                                          eT[:, j * 128:(j + 1) * 128],
                                          vtb[par][:, j * D + 512: (j + 1) * D],
                                          start=first, stop=last)
                p.inc(mm, "tt")
                p.w(nc.vector, "tt")
                if pc == 0:
                    p.inc(nc.vector.tensor_copy(oacc[:, t * D:(t + 1) * D], pb), "vv")
                else:
                    p.inc(nc.vector.tensor_add(
                        oacc[:, t * D:(t + 1) * D],
                        oacc[:, t * D:(t + 1) * D], pb), "vv")

        if B3_STAGE < 3:
            tail_range = ()
        else:
            tail_range = range(4)
            p.w(nc.vector, "ss")
            d3 = bass.AP(den8.tensor, den8.offset, [den8.ap[0], [8, 4], [1, 8]])
            p.inc(nc.vector.reduce_sum(p.stats[:, 32:36], d3, axis=AXX), "vv")
            p.w(nc.vector, "vv")
            p.inc(nc.vector.reciprocal(p.stats[:, 36:40], p.stats[:, 32:36]), "vv")

        for t in tail_range:
            if B3_STAGE == 3:
                p.w(nc.vector, "vv")
                p.inc(nc.vector.tensor_scalar(xo, oacc[:, t * D:(t + 1) * D],
                                              p.stats[:, 36 + t:37 + t], None,
                                              op0=ALU.mult), "vv")
                p.w(nc.sync, "vv")
                p.dma("ds", bass.AP(dst.tensor, dst.offset + toff[t],
                                    [[D, 128], [1, D]]), xo)
                continue
            p.w(nc.tensor, "vv")
            for dc in range(8):
                o_ = dc * 512 + t * 128
                nc.tensor.matmul(psC[:, 0:512], xnT[:, o_: o_ + 128],
                                 wefT[:, dc * D: dc * D + 512],
                                 start=(dc == 0), stop=False)
                nc.tensor.matmul(psC[:, 512:1024], xnT[:, o_: o_ + 128],
                                 wefT[:, dc * D + 512: (dc + 1) * D],
                                 start=(dc == 0), stop=False)
            nc.tensor.matmul(psC[:, 0:512], p.onesr, p.srow[0:1, 0:512],
                             start=False, stop=True)
            p.inc(nc.tensor.matmul(psC[:, 512:1024], p.onesr,
                                   p.srow[0:1, 512:1024],
                                   start=False, stop=True), "tt")
            p.w(nc.vector, "tt")
            p.w(nc.vector, "dg")
            p.inc(nc.vector.scalar_tensor_tensor(
                xo, oacc[:, t * D:(t + 1) * D], p.stats[:, 36 + t:37 + t],
                xg[:, t * D:(t + 1) * D], op0=ALU.mult, op1=ALU.add), "vv")
            p.w(nc.vector, "vv")
            p.inc(nc.vector.tensor_add(xo, xo, psC), "vv")
            p.w(nc.sync, "vv")
            p.dma("ds", bass.AP(dst.tensor, dst.offset + toff[t],
                                [[D, 128], [1, D]]), xo)
    p.quiesce()


def emit_b0(p, src, dst, spill):
    nc = p.nc
    toep = p.inp("toep")
    sel = p.inp("L0_ffn_expert_sel")
    keys = p.inp("L0_ffn_keys")
    values = p.inp("L0_ffn_values")
    rsc = p.inp("L0_ffn_route_scale")
    A = p.A
    A.reset()

    M = A.take(17 * 128)
    _mk1 = A.mark()
    W17 = A.take(17 * 1024)
    xstage = A.take(1024)
    sq1 = A.take(1024)
    cbuf = A.take(1024)
    p.dma("ds", M, bass.AP(toep.tensor, toep.offset,
                           [[128, 128], [128 * 128, 17], [1, 128]]))
    p.wall("ds")

    def load_ln_to_slot(tile_idx, slot):
        p.w(nc.sync, "vv")
        p.w(nc.sync, "ss")
        p.dma("ds", xstage, hbm_tiles(src, tile_idx))
        p.w(nc.vector, "ds")
        p.w(nc.scalar, "ds")
        p.w(nc.vector, "tt")
        p.ln(xstage, W17[:, slot * D:(slot + 1) * D], sq1, 1)

    for w_ in range(17):
        load_ln_to_slot(w_, w_)
    for bt in range(NT):
        omax = min(16, NT - 1 - bt)
        p.w(nc.tensor, "vv")
        for o in range(omax + 1):
            slot = (bt + o) % 17
            for h in range(2):
                mm = nc.tensor.matmul(
                    p.ps[0][:, h * 512:(h + 1) * 512],
                    M[:, o * 128:(o + 1) * 128],
                    W17[:, slot * D + h * 512: slot * D + (h + 1) * 512],
                    start=(o == 0), stop=(o == omax))
        p.inc(mm, "tt")
        p.w(nc.vector, "tt")
        p.w(nc.vector, "ds", lag=0)
        p.inc(nc.vector.tensor_copy(cbuf, p.ps[0]), "vv")
        p.w(nc.sync, "vv")
        p.dma("ds", hbm_tiles(spill, bt), cbuf)
        if bt <= 14:
            load_ln_to_slot(bt + 17, bt % 17)
    p.quiesce()

    A.rewind(_mk1)
    selT = A.take(128)
    st16 = A.take(1024)
    p.dma("ds", bass.AP(st16.tensor, st16.offset, [[st16.ap[0][0], 16], [1, 1024]]),
          bass.AP(sel.tensor, sel.offset, [[1024, 16], [1, 1024]]))
    p.dma("ds", p.scol[0:1, 4:5], ap2d(rsc, 1))
    p.wall("ds")
    for dc in range(8):
        sl = p.psT_slots[dc % 2]
        p.w(nc.tensor, "vv", lag=0 if dc == 0 else 1)
        p.inc(nc.tensor.transpose(
            sl[:, 0:16], bass.AP(st16.tensor, st16.offset + dc * 128,
                                 [[st16.ap[0][0], 16], [1, 128]]),
            p.ident[0:16, 0:16]), "tt")
        p.w(nc.vector, "tt")
        p.inc(nc.vector.tensor_copy(selT[:, dc * 16:(dc + 1) * 16], sl[:, 0:16]), "vv")
    p.w(nc.tensor, "vv")
    p.inc(nc.tensor.matmul(p.ps[0][:, 0:1], p.onesr, p.scol[0:1, 4:5],
                           start=True, stop=True), "tt")
    p.w(nc.vector, "tt")
    rs_col = p.scol[:, 5:6]
    p.inc(nc.vector.tensor_copy(rs_col, p.ps[0][:, 0:1]), "vv")
    p.bar()

    xg8 = A.take(8192)
    xng8 = A.take(8192)
    facc = xng8
    xnT8 = A.take(8192)
    kb = [A.take(2048), A.take(2048)]
    vb = [A.take(2048), A.take(2048)]
    hT = A.take(2048)
    prb = A.take(128)
    rnk = A.take(128)
    wbuf = A.take(128)
    sstage = A.take(1024)
    xob = A.take(1024)

    for g in range(4):
        p.w(nc.sync, "vv")
        p.w(nc.sync, "ss")
        p.w(nc.sync, "tt")
        p.dma("ds", xg8, hbm_tiles(src, g * 8, 8))
        p.w(nc.vector, "ds")
        p.w(nc.scalar, "ds")
        p.ln(xg8, xng8, xnT8, 8)
        p.w(nc.tensor, "vv")
        pairs = []
        for t in range(8):
            for dc in range(8):
                pairs.append(
                    (xng8[:, t * D + dc * 128: t * D + dc * 128 + 128],
                     xnT8[:, dc * D + t * 128: dc * D + t * 128 + 128]))
        p.transposes(pairs)
        p.w(nc.tensor, "ss")
        p.w(nc.tensor, "vv")
        for t in range(8):
            for dc in range(8):
                mm = nc.tensor.matmul(
                    p.ps[2][:, t * 16:(t + 1) * 16],
                    xnT8[:, dc * D + t * 128: dc * D + t * 128 + 128],
                    selT[:, dc * 16:(dc + 1) * 16],
                    start=(dc == 0), stop=(dc == 7))
        p.inc(mm, "tt")
        p.w(nc.scalar, "tt")
        p.inc(nc.scalar.activation(prb, p.ps[2][:, 0:128], ACTF.Exp, scale=-1.0), "ss")
        p.w(nc.vector, "ss")
        p.inc(nc.vector.tensor_scalar_add(prb, prb, 1.0), "vv")
        p.w(nc.vector, "vv")
        p.inc(nc.vector.reciprocal(prb, prb), "vv")
        p.w(nc.vector, "vv")
        for t in range(8):
            pt = prb[:, t * 16:(t + 1) * 16]
            rt = rnk[:, t * 16:(t + 1) * 16]
            p.inc(nc.vector.tensor_scalar(rt, pt, pt[:, 0:1], None, op0=ALU.is_lt), "vv")
            p.w(nc.vector, "vv")
            for e in range(1, 16):
                p.inc(nc.vector.scalar_tensor_tensor(
                    rt, pt, pt[:, e:e + 1], rt, op0=ALU.is_lt, op1=ALU.add), "vv")
                p.w(nc.vector, "vv")
        p.inc(nc.vector.tensor_scalar(rnk, rnk, 3.5, None, op0=ALU.is_lt), "vv")
        p.w(nc.vector, "vv")
        p.inc(nc.vector.tensor_mul(wbuf, prb, rnk), "vv")
        p.w(nc.vector, "vv")
        wv3 = bass.AP(wbuf.tensor, wbuf.offset, [wbuf.ap[0], [16, 8], [1, 16]])
        p.inc(nc.vector.reduce_sum(p.stats[:, 40:48], wv3, axis=AXX), "vv")
        p.w(nc.vector, "vv")
        p.inc(nc.vector.tensor_scalar_max(p.stats[:, 40:48], p.stats[:, 40:48],
                                          1e-9), "vv")
        p.w(nc.vector, "vv")
        p.inc(nc.vector.reciprocal(p.stats[:, 48:56], p.stats[:, 40:48]), "vv")
        p.w(nc.vector, "vv")
        p.inc(nc.vector.tensor_scalar(p.stats[:, 48:56], p.stats[:, 48:56],
                                      rs_col, None, op0=ALU.mult), "vv")
        p.w(nc.vector, "vv")
        rr_b = bass.AP(p.stats.tensor, p.stats[:, 48:56].offset,
                       [p.stats.ap[0], [1, 8], [0, 16]])
        wb3 = bass.AP(wbuf.tensor, wbuf.offset, [wbuf.ap[0], [16, 8], [1, 16]])
        p.inc(nc.vector.tensor_tensor(wb3, wb3, rr_b, op=ALU.mult), "vv")
        p.w(nc.scalar, "vv")
        p.w(nc.tensor, "vv")

        for e in range(16):
            par = e % 2
            p.w(nc.sync, "tt", lag=9)
            p.dma("ds", kb[par], bass.AP(keys.tensor, keys.offset + e * D * 256,
                                         [[256, 128], [128 * 256, 8], [1, 256]]))
            p.dma("ds", vb[par], bass.AP(values.tensor, values.offset + e * 256 * D,
                                         [[D, 128], [128 * D, 2], [1, D]]))
            p.w(nc.tensor, "ds")
            p.w(nc.tensor, "ss")
            for hc in range(2):
                for th in range(2):
                    pdst = p.ps[hc][:, th * 512:(th + 1) * 512]
                    for dc in range(8):
                        mm = nc.tensor.matmul(
                            pdst, kb[par][:, dc * 256 + hc * 128: dc * 256 + hc * 128 + 128],
                            xnT8[:, dc * D + th * 512: dc * D + (th + 1) * 512],
                            start=(dc == 0), stop=(dc == 7))
            p.inc(mm, "tt")
            p.w(nc.scalar, "tt")
            p.w(nc.scalar, "vv")
            for hc in range(2):
                ii = nc.scalar.activation(hT[:, hc * D:(hc + 1) * D],
                                          p.ps[hc], ACTF.Relu)
            p.inc(ii, "ss")
            p.w(nc.tensor, "ss")
            for t in range(8):
                for dh in range(2):
                    for hc in range(2):
                        mm = nc.tensor.matmul(
                            p.ps[2][:, dh * 512:(dh + 1) * 512],
                            hT[:, hc * D + t * 128: hc * D + t * 128 + 128],
                            vb[par][:, hc * D + dh * 512: hc * D + (dh + 1) * 512],
                            start=(hc == 0), stop=(hc == 1))
                p.inc(mm, "tt")
                p.w(nc.vector, "tt")
                if e == 0:
                    p.inc(nc.vector.tensor_scalar(
                        facc[:, t * D:(t + 1) * D], p.ps[2],
                        wbuf[:, t * 16 + e: t * 16 + e + 1], None,
                        op0=ALU.mult), "vv")
                else:
                    p.inc(nc.vector.scalar_tensor_tensor(
                        facc[:, t * D:(t + 1) * D], p.ps[2],
                        wbuf[:, t * 16 + e: t * 16 + e + 1],
                        facc[:, t * D:(t + 1) * D],
                        op0=ALU.mult, op1=ALU.add), "vv")
                p.w(nc.tensor, "vv")
        for t in range(8):
            p.w(nc.sync, "vv")
            p.w(nc.sync, "ss")
            p.dma("ds", sstage, hbm_tiles(spill, g * 8 + t))
            p.w(nc.vector, "ds")
            p.inc(nc.vector.tensor_add(xob, xg8[:, t * D:(t + 1) * D],
                                       facc[:, t * D:(t + 1) * D]), "vv")
            p.w(nc.vector, "vv")
            p.inc(nc.vector.tensor_add(xob, xob, sstage), "vv")
            p.w(nc.sync, "vv")
            p.dma("ds", hbm_tiles(dst, g * 8 + t), xob)
    p.quiesce()


def emit_b1(p, src, dst, spill, aux):
    nc = p.nc
    A = p.A
    A.reset()
    Wq, Wk, Wv, Wo = (p.inp("L1_attn_W%s" % s) for s in "qkvo")
    W1, W2 = p.inp("L1_attn_W1"), p.inp("L1_attn_W2")
    b1a, b2a = p.inp("L1_attn_b1"), p.inp("L1_attn_b2")
    gw = p.inp("L1_ffn_gate_w")
    fw1, fb1 = p.inp("L1_ffn_w1"), p.inp("L1_ffn_b1")
    fw2, fb2 = p.inp("L1_ffn_w2"), p.inp("L1_ffn_b2")
    sw1, sb1 = p.inp("L1_ffn_sw1"), p.inp("L1_ffn_sb1")
    sw2, sb2 = p.inp("L1_ffn_sw2"), p.inp("L1_ffn_sb2")
    bcols = p.bcols
    gwT = p.scol[:, 8:24]

    st16 = A.take(1024)
    p.dma("ds", bcols[:, 0:8], bass.AP(b2a.tensor, b2a.offset, [[1, 128], [128, 8]]), slow=True)
    p.dma("ds", bcols[:, 8:12], bass.AP(b1a.tensor, b1a.offset, [[1, 128], [128, 4]]), slow=True)
    p.dma("ds", bcols[:, 16:48], bass.AP(fb1.tensor, fb1.offset, [[1, 128], [128, 32]]), slow=True)
    p.dma("ds", bcols[:, 48:80],
          bass.AP(fb1.tensor, fb1.offset + 4096, [[1, 128], [128, 32]]), slow=True)
    p.dma("ds", bcols[:, 80:112], bass.AP(sb1.tensor, sb1.offset, [[1, 128], [128, 32]]), slow=True)
    p.dma("ds", p.srow[0:1, 0:D], bass.AP(fb2.tensor, fb2.offset, [[D, 1], [1, D]]))
    p.dma("ds", p.srow[0:1, D:2 * D],
          bass.AP(fb2.tensor, fb2.offset + D, [[D, 1], [1, D]]))
    p.dma("ds", p.srow[0:1, 2 * D:3 * D], ap2d(sb2, D))
    p.dma("ds", bass.AP(st16.tensor, st16.offset, [[st16.ap[0][0], 2], [1, 1024]]),
          bass.AP(gw.tensor, gw.offset, [[1024, 2], [1, 1024]]))
    p.wall("ds")
    for dc in range(8):
        sl = p.psT_slots[dc % 2]
        p.w(nc.tensor, "vv", lag=0 if dc == 0 else 1)
        p.inc(nc.tensor.transpose(
            sl[:, 0:2], bass.AP(st16.tensor, st16.offset + dc * 128,
                                [[st16.ap[0][0], 2], [1, 128]]),
            p.ident[0:2, 0:2]), "tt")
        p.w(nc.vector, "tt")
        p.inc(nc.vector.tensor_copy(gwT[:, dc * 2:dc * 2 + 2], sl[:, 0:2]), "vv")
    p.bar()
    if B1_STAGE < 2:
        p.dma("ds", dst, src)
        p.quiesce()
        return

    xg = A.take(4096)
    xng = A.take(4096)
    xnT = A.take(4096)
    xnT0 = A.take(4096)
    xnT1 = A.take(4096)
    hT = A.take(16384)
    w1c = [A.take(1024), A.take(1024)]
    w2c = [A.take(1024), A.take(1024)]
    facc = xnT0
    xnm = A.take(1024)

    mlps = [
        (fw1.tensor, fw1.offset, fw2.tensor, fw2.offset, 16, 0),
        (fw1.tensor, fw1.offset + D * 4096, fw2.tensor, fw2.offset + 4096 * D, 48, 1),
        (sw1.tensor, sw1.offset, sw2.tensor, sw2.offset, 80, 2),
    ]

    for g in range(8):
        p.w(nc.sync, "vv")
        p.w(nc.sync, "ss")
        p.w(nc.sync, "tt")
        p.dma("ds", xg, hbm_tiles(src, g * 4, 4))
        p.w(nc.vector, "ds")
        p.w(nc.scalar, "ds")
        p.ln(xg, xng, xnT, 4)
        p.w(nc.tensor, "vv")
        pairs = []
        for t in range(4):
            for dc in range(8):
                pairs.append(
                    (xng[:, t * D + dc * 128: t * D + dc * 128 + 128],
                     xnT[:, dc * 512 + t * 128: dc * 512 + t * 128 + 128]))
        p.transposes(pairs)
        p.w(nc.tensor, "vv")
        for t in range(4):
            for dc in range(8):
                mm = nc.tensor.matmul(
                    p.ps[2][:, t * 2:t * 2 + 2],
                    xnT[:, dc * 512 + t * 128: dc * 512 + t * 128 + 128],
                    gwT[:, dc * 2:dc * 2 + 2], start=(dc == 0), stop=(dc == 7))
        p.inc(mm, "tt")
        p.w(nc.vector, "tt")
        m01 = p.stats[:, 48:56]
        gsb = p.stats[:, 40:48]
        p.inc(nc.vector.tensor_copy(gsb, p.ps[2][:, 0:8]), "vv")
        p.w(nc.vector, "vv")
        for t in range(4):
            p.inc(nc.vector.tensor_tensor(
                m01[:, 2 * t:2 * t + 1], gsb[:, 2 * t:2 * t + 1],
                gsb[:, 2 * t + 1:2 * t + 2], op=ALU.is_ge), "vv")
            p.w(nc.vector, "vv")
            p.inc(nc.vector.tensor_scalar(
                m01[:, 2 * t + 1:2 * t + 2], m01[:, 2 * t:2 * t + 1],
                -1.0, 1.0, op0=ALU.mult, op1=ALU.add), "vv")
            p.w(nc.vector, "vv")
        for e, xnTe in ((0, xnT0), (1, xnT1)):
            for t in range(4):
                p.inc(nc.vector.tensor_scalar(
                    xnm, xng[:, t * D:(t + 1) * D],
                    m01[:, 2 * t + e:2 * t + e + 1], None, op0=ALU.mult), "vv")
                p.w(nc.tensor, "vv")
                pairs = [(xnm[:, dc * 128:(dc + 1) * 128],
                          xnTe[:, dc * 512 + t * 128: dc * 512 + t * 128 + 128])
                         for dc in range(8)]
                p.transposes(pairs)
                p.w(nc.vector, "vv")

        for w1t, w1o, w2t, w2o, bcol, mi in mlps:
            xin = (xnT0, xnT1, xnT)[mi]
            for hc in range(32):
                par = hc % 2
                p.w(nc.sync, "tt", lag=1)
                p.dma("ds", w1c[par],
                      bass.AP(w1t, w1o + hc * 128,
                              [[4096, 128], [128 * 4096, 8], [1, 128]]))
                p.w(nc.tensor, "ds")
                p.w(nc.tensor, "ss", lag=1)
                pdst = p.ps[par][:, 0:512]
                for dc in range(8):
                    mm = nc.tensor.matmul(
                        pdst, w1c[par][:, dc * 128:(dc + 1) * 128],
                        xin[:, dc * 512:(dc + 1) * 512],
                        start=(dc == 0), stop=(dc == 7))
                p.inc(mm, "tt")
                p.w(nc.scalar, "tt")
                p.inc(nc.scalar.activation(
                    hT[:, hc * 512:(hc + 1) * 512], pdst, ACTF.Relu,
                    bias=bcols[:, bcol + hc:bcol + hc + 1]), "ss")
            p.w(nc.tensor, "ss")
            for hc in range(32):
                par = hc % 2
                p.w(nc.sync, "tt", lag=1)
                p.dma("ds", w2c[par],
                      bass.AP(w2t, w2o + hc * 128 * D, [[D, 128], [1, D]]))
                p.w(nc.tensor, "ds")
                for t in range(4):
                    for dh in range(2):
                        mm = nc.tensor.matmul(
                            p.ps[t][:, dh * 512:(dh + 1) * 512],
                            hT[:, hc * 512 + t * 128: hc * 512 + t * 128 + 128],
                            w2c[par][:, dh * 512:(dh + 1) * 512],
                            start=(hc == 0), stop=False)
                p.inc(mm, "tt")
            srow_off = mi * D
            for t in range(4):
                nc.tensor.matmul(p.ps[t][:, 0:512], p.onesr,
                                 p.srow[0:1, srow_off:srow_off + 512],
                                 start=False, stop=True)
                mm = nc.tensor.matmul(p.ps[t][:, 512:1024], p.onesr,
                                      p.srow[0:1, srow_off + 512:srow_off + 1024],
                                      start=False, stop=True)
            p.inc(mm, "tt")
            p.w(nc.vector, "tt")
            for t in range(4):
                if mi == 0:
                    p.inc(nc.vector.tensor_copy(
                        facc[:, t * D:(t + 1) * D], p.ps[t]), "vv")
                else:
                    p.inc(nc.vector.tensor_add(
                        facc[:, t * D:(t + 1) * D],
                        facc[:, t * D:(t + 1) * D], p.ps[t]), "vv")
            p.w(nc.tensor, "vv")
        for t in range(4):
            p.w(nc.sync, "vv")
            p.dma("ds", hbm_tiles(spill, g * 4 + t), facc[:, t * D:(t + 1) * D])
    p.quiesce()
    if B1_STAGE < 3:
        p.dma("ds", dst, src)
        p.quiesce()
        return

    A.reset()
    xg8 = A.take(8192)
    xng8 = A.take(8192)
    xnT8 = A.take(8192)
    compT = xg8
    w1b = xng8[:, 0:4096]
    h1T = xng8[:, 4096:8192]
    compTp = A.take(2048)
    kcT = A.take(2048)
    vc = A.take(2048)
    tmp256 = A.take(256)

    p.dma("ds", xg8, hbm_tiles(src, 0, 8))
    p.w(nc.vector, "ds")
    p.w(nc.scalar, "ds")
    p.ln(xg8, xng8, xnT8, 8)
    p.w(nc.tensor, "vv")
    pairs = []
    for t in range(8):
        for dc in range(8):
            pairs.append(
                (xng8[:, t * D + dc * 128: t * D + dc * 128 + 128],
                 xnT8[:, dc * D + t * 128: dc * D + t * 128 + 128]))
    p.transposes(pairs)
    p.quiesce()
    p.dma("ds", w1b, bass.AP(W1.tensor, W1.offset, [[512, 128], [128 * 512, 8], [1, 512]]))
    p.wall("ds")
    for jc in range(4):
        for th in range(2):
            pdst = p.ps[jc % 2][:, th * 512:(th + 1) * 512]
            p.w(nc.tensor, "ss")
            for dc in range(8):
                mm = nc.tensor.matmul(
                    pdst, w1b[:, dc * 512 + jc * 128: dc * 512 + jc * 128 + 128],
                    xnT8[:, dc * D + th * 512: dc * D + (th + 1) * 512],
                    start=(dc == 0), stop=(dc == 7))
            p.inc(mm, "tt")
            p.w(nc.scalar, "tt")
            p.inc(nc.scalar.activation(
                h1T[:, jc * D + th * 512: jc * D + (th + 1) * 512], pdst,
                ACTF.Relu, bias=bcols[:, 8 + jc:9 + jc]), "ss")
    p.w(nc.sync, "ss")
    p.w(nc.sync, "tt")
    p.dma("ds", w1b, bass.AP(W2.tensor, W2.offset, [[D, 128], [128 * D, 4], [1, D]]))
    p.wall("ds")
    p.w(nc.tensor, "ss")
    for dc in range(8):
        for th in range(2):
            pdst = p.ps[2][:, th * 512:(th + 1) * 512]
            for jc in range(4):
                mm = nc.tensor.matmul(
                    pdst, w1b[:, jc * D + dc * 128: jc * D + dc * 128 + 128],
                    h1T[:, jc * D + th * 512: jc * D + (th + 1) * 512],
                    start=(jc == 0), stop=(jc == 3))
        p.inc(mm, "tt")
        p.w(nc.vector, "tt")
        p.inc(nc.vector.tensor_scalar(
            compT[:, dc * D: dc * D + 512], p.ps[2][:, 0:512],
            bcols[:, dc:dc + 1], None, op0=ALU.add), "vv")
        p.w(nc.vector, "vv")
        p.inc(nc.vector.tensor_scalar(
            compT[:, dc * D + 512: (dc + 1) * D], p.ps[2][:, 512:1024],
            bcols[:, dc:dc + 1], None, op0=ALU.add), "vv")
        p.w(nc.tensor, "vv")
    for dc in range(8):
        a_ = [bass.AP(compT.tensor, compT.offset + dc * D + k, [compT.ap[0], [4, 256]])
              for k in range(4)]
        o_ = compTp[:, dc * 256:(dc + 1) * 256]
        p.inc(nc.vector.tensor_add(o_, a_[0], a_[1]), "vv")
        p.inc(nc.vector.tensor_add(tmp256, a_[2], a_[3]), "vv")
        p.w(nc.vector, "vv")
        p.inc(nc.vector.tensor_add(o_, o_, tmp256), "vv")
        p.w(nc.vector, "vv")
        p.inc(nc.vector.tensor_scalar_mul(o_, o_, 0.25), "vv")
        p.w(nc.vector, "vv")
    p.w(nc.sync, "vv")
    p.w(nc.sync, "ss")
    p.dma("ds", w1b, bass.AP(Wk.tensor, Wk.offset, [[D, 128], [128 * D, 8], [1, 512]]))
    p.dma("ds", h1T, bass.AP(Wk.tensor, Wk.offset + 512,
                             [[D, 128], [128 * D, 8], [1, 512]]))
    p.wall("ds")
    p.w(nc.tensor, "vv")
    for dkc in range(8):
        buf = w1b if dkc < 4 else h1T
        dko = (dkc % 4) * 128
        pdst = p.ps[dkc % 2][:, 0:256]
        p.w(nc.tensor, "vv", lag=0 if dkc == 0 else 1)
        for dc in range(8):
            mm = nc.tensor.matmul(
                pdst, buf[:, dc * 512 + dko: dc * 512 + dko + 128],
                compTp[:, dc * 256:(dc + 1) * 256],
                start=(dc == 0), stop=(dc == 7))
        p.inc(mm, "tt")
        p.w(nc.vector, "tt")
        p.inc(nc.vector.tensor_copy(kcT[:, dkc * 256:(dkc + 1) * 256], pdst), "vv")
    p.w(nc.sync, "vv")
    p.w(nc.sync, "tt")
    p.dma("ds", w1b, bass.AP(Wv.tensor, Wv.offset, [[D, 128], [128 * D, 8], [1, 512]]))
    p.dma("ds", h1T, bass.AP(Wv.tensor, Wv.offset + 512,
                             [[D, 128], [128 * D, 8], [1, 512]]))
    p.wall("ds")
    p.w(nc.tensor, "vv")
    for uc in range(2):
        for dv in range(2):
            buf = w1b if dv == 0 else h1T
            pdst = p.ps[2][:, dv * 512:(dv + 1) * 512]
            for dc in range(8):
                mm = nc.tensor.matmul(
                    pdst, compTp[:, dc * 256 + uc * 128: dc * 256 + uc * 128 + 128],
                    buf[:, dc * 512:(dc + 1) * 512],
                    start=(dc == 0), stop=(dc == 7))
        p.inc(mm, "tt")
        p.w(nc.vector, "tt")
        p.inc(nc.vector.tensor_copy(vc[:, uc * D:(uc + 1) * D], p.ps[2]), "vv")
        p.w(nc.tensor, "vv")
    p.w(nc.sync, "vv")
    p.dma("ds", bass.AP(aux.tensor, aux.offset, [[4096, 128], [1, 2048]]), kcT)
    p.dma("ds", bass.AP(aux.tensor, aux.offset + 2048, [[4096, 128], [1, 2048]]), vc)
    p.quiesce()

    if B1_STAGE < 4:
        p.dma("ds", dst, src)
        p.quiesce()
        return
    kT8 = xg8
    wbb = xng8
    A.rewind(3 * 8192)
    v8 = A.take(8192)
    qTt = A.take(1024)
    attT = A.take(1024)
    probs = A.take(1024)
    pT8 = A.take(1024)
    xstg = A.take(1024)
    fstg = A.take(1024)
    xob = A.take(1024)

    p.dma("ds", wbb, bass.AP(Wk.tensor, Wk.offset, [[D, 128], [128 * D, 8], [1, D]]))
    p.wall("ds")
    for dkc in range(8):
        for th in range(2):
            pdst = p.ps[dkc % 2][:, th * 512:(th + 1) * 512]
            if th == 0:
                p.w(nc.tensor, "vv", lag=0 if dkc == 0 else 1)
            for dc in range(8):
                mm = nc.tensor.matmul(
                    pdst, wbb[:, dc * D + dkc * 128: dc * D + dkc * 128 + 128],
                    xnT8[:, dc * D + th * 512: dc * D + (th + 1) * 512],
                    start=(dc == 0), stop=(dc == 7))
        p.inc(mm, "tt")
        p.w(nc.vector, "tt")
        p.inc(nc.vector.tensor_copy(kT8[:, dkc * D:(dkc + 1) * D],
                                    p.ps[dkc % 2]), "vv")
    p.w(nc.sync, "vv")
    p.w(nc.sync, "tt")
    p.dma("ds", wbb, bass.AP(Wv.tensor, Wv.offset, [[D, 128], [128 * D, 8], [1, D]]))
    p.wall("ds")
    p.w(nc.tensor, "vv")
    for t in range(8):
        for dv in range(2):
            pdst = p.ps[t % 2][:, dv * 512:(dv + 1) * 512]
            if dv == 0:
                p.w(nc.tensor, "vv", lag=0 if t == 0 else 1)
            for dc in range(8):
                mm = nc.tensor.matmul(
                    pdst, xnT8[:, dc * D + t * 128: dc * D + t * 128 + 128],
                    wbb[:, dc * D + dv * 512: dc * D + (dv + 1) * 512],
                    start=(dc == 0), stop=(dc == 7))
        p.inc(mm, "tt")
        p.w(nc.vector, "tt")
        p.inc(nc.vector.tensor_copy(v8[:, t * D:(t + 1) * D], p.ps[t % 2]), "vv")
    p.w(nc.sync, "vv")
    p.w(nc.sync, "tt")
    p.dma("ds", wbb, bass.AP(Wq.tensor, Wq.offset, [[D, 128], [128 * D, 8], [1, D]]))
    p.wall("ds")
    for qt in range(8):
        kw = (qt + 1) * 128
        p.w(nc.tensor, "vv")
        for dqc in range(8):
            pdst = p.ps[2][:, dqc * 128:(dqc + 1) * 128]
            for dc in range(8):
                mm = nc.tensor.matmul(
                    pdst, wbb[:, dc * D + dqc * 128: dc * D + dqc * 128 + 128],
                    xnT8[:, dc * D + qt * 128: dc * D + qt * 128 + 128],
                    start=(dc == 0), stop=(dc == 7))
        p.inc(mm, "tt")
        p.w(nc.vector, "tt")
        p.inc(nc.vector.tensor_copy(qTt, p.ps[2]), "vv")
        p.w(nc.tensor, "vv")
        for h in range(16):
            s_, hc2 = h % 2, h // 2
            lq = qTt[64 * s_:64 * s_ + 64, hc2 * 128:(hc2 + 1) * 128]
            psc = p.ps[0]
            nmm = min(kw, 512)
            mm = nc.tensor.matmul(
                psc[:, 0:nmm], lq,
                kT8[64 * s_:64 * s_ + 64, hc2 * D: hc2 * D + nmm],
                start=True, stop=True)
            if kw > 512:
                mm = nc.tensor.matmul(
                    psc[:, 512:kw], lq,
                    kT8[64 * s_:64 * s_ + 64, hc2 * D + 512: hc2 * D + kw],
                    start=True, stop=True)
            p.inc(mm, "tt")
            p.w(nc.vector, "tt")
            p.inc(nc.vector.tensor_add(psc[:, qt * 128:kw], psc[:, qt * 128:kw],
                                       p.tri), "vv")
            p.w(nc.vector, "vv")
            nm = p.stats[:, 0:1]
            p.inc(nc.vector.reduce_max(nm, psc[:, 0:kw], axis=AXX, negate=True), "vv")
            p.w(nc.vector, "vv")
            p.inc(nc.vector.tensor_scalar_mul(nm, nm, 0.125), "vv")
            p.w(nc.scalar, "vv")
            p.inc(nc.scalar.activation(probs[:, 0:kw], psc[:, 0:kw], ACTF.Exp,
                                       scale=0.125, bias=nm,
                                       accum_out=p.stats[:, 1:2]), "ss")
            p.w(nc.vector, "ss")
            p.inc(nc.vector.reciprocal(p.stats[:, 2:3], p.stats[:, 1:2]), "vv")
            p.w(nc.vector, "vv")
            p.inc(nc.vector.tensor_scalar(probs[:, 0:kw], probs[:, 0:kw],
                                          p.stats[:, 2:3], None, op0=ALU.mult), "vv")
            p.w(nc.tensor, "vv")
            prs = [(probs[:, kt * 128:(kt + 1) * 128], pT8[:, kt * 128:(kt + 1) * 128])
                   for kt in range(qt + 1)]
            p.transposes(prs)
            p.w(nc.tensor, "vv")
            for kt in range(qt + 1):
                mm = nc.tensor.matmul(
                    p.ps[1][64 * s_:64 * s_ + 64, 0:128],
                    v8[:, kt * D + h * 64: kt * D + h * 64 + 64],
                    pT8[:, kt * 128:(kt + 1) * 128],
                    start=(kt == 0), stop=(kt == qt))
            p.inc(mm, "tt")
            if s_ == 1:
                p.w(nc.vector, "tt")
                p.inc(nc.vector.tensor_copy(attT[:, hc2 * 128:(hc2 + 1) * 128],
                                            p.ps[1][:, 0:128]), "vv")
                p.w(nc.tensor, "vv")
        p.w(nc.sync, "vv")
        p.dma("ds", bass.AP(aux.tensor,
                            aux.offset + (128 + 128 * (qt // 4)) * 4096 +
                            (qt % 4) * 1024,
                            [[4096, 128], [1, 1024]]), attT)
    p.quiesce()
    p.dma("ds", wbb, bass.AP(Wo.tensor, Wo.offset, [[D, 128], [128 * D, 8], [1, D]]))
    p.wall("ds")
    for qt in range(8):
        p.w(nc.sync, "vv")
        p.w(nc.sync, "tt")
        p.dma("ds", attT, bass.AP(aux.tensor,
                                  aux.offset + (128 + 128 * (qt // 4)) * 4096 +
                                  (qt % 4) * 1024,
                                  [[4096, 128], [1, 1024]]))
        p.dma("ds", xstg, hbm_tiles(src, qt))
        p.dma("ds", fstg, hbm_tiles(spill, qt))
        p.w(nc.tensor, "ds")
        p.w(nc.tensor, "vv")
        for dh in range(2):
            pdst = p.ps[2][:, dh * 512:(dh + 1) * 512]
            for dvc in range(8):
                mm = nc.tensor.matmul(
                    pdst, attT[:, dvc * 128:(dvc + 1) * 128],
                    wbb[:, dvc * D + dh * 512: dvc * D + (dh + 1) * 512],
                    start=(dvc == 0), stop=(dvc == 7))
        p.inc(mm, "tt")
        p.w(nc.vector, "tt")
        p.w(nc.vector, "ds")
        p.inc(nc.vector.tensor_add(xob, xstg, fstg), "vv")
        p.w(nc.vector, "vv")
        p.inc(nc.vector.tensor_add(xob, xob, p.ps[2]), "vv")
        p.w(nc.sync, "vv")
        p.dma("ds", hbm_tiles(dst, qt), xob)
    p.quiesce()

    if B1_STAGE < 5:
        p.quiesce()
        return
    A.reset()
    wqb = A.take(8192)
    wob = A.take(8192)
    xgc = A.take(4096)
    xngc = A.take(4096)
    xnTc = A.take(4096)
    kcT2 = A.take(2048)
    vc2 = A.take(2048)
    qTg = A.take(4096)
    attTc = A.take(1024)
    probsc = A.take(256)
    pTc = A.take(256)
    fstg2 = A.take(1024)
    xob2 = A.take(1024)
    p.dma("ds", wqb, bass.AP(Wq.tensor, Wq.offset, [[D, 128], [128 * D, 8], [1, D]]))
    p.dma("ds", wob, bass.AP(Wo.tensor, Wo.offset, [[D, 128], [128 * D, 8], [1, D]]))
    p.dma("ds", kcT2, bass.AP(aux.tensor, aux.offset, [[4096, 128], [1, 2048]]))
    p.dma("ds", vc2, bass.AP(aux.tensor, aux.offset + 2048, [[4096, 128], [1, 2048]]))
    p.wall("ds")
    for g in range(2, 8):
        p.w(nc.sync, "vv")
        p.w(nc.sync, "ss")
        p.w(nc.sync, "tt")
        p.dma("ds", xgc, hbm_tiles(src, g * 4, 4))
        p.w(nc.vector, "ds")
        p.w(nc.scalar, "ds")
        p.ln(xgc, xngc, xnTc, 4)
        p.w(nc.tensor, "vv")
        pairs = []
        for t in range(4):
            for dc in range(8):
                pairs.append(
                    (xngc[:, t * D + dc * 128: t * D + dc * 128 + 128],
                     xnTc[:, dc * 512 + t * 128: dc * 512 + t * 128 + 128]))
        p.transposes(pairs)
        p.w(nc.tensor, "vv")
        for dqc in range(8):
            pdst = p.ps[dqc % 2][:, 0:512]
            p.w(nc.tensor, "vv", lag=0 if dqc == 0 else 1)
            for dc in range(8):
                mm = nc.tensor.matmul(
                    pdst, wqb[:, dc * D + dqc * 128: dc * D + dqc * 128 + 128],
                    xnTc[:, dc * 512:(dc + 1) * 512],
                    start=(dc == 0), stop=(dc == 7))
            p.inc(mm, "tt")
            p.w(nc.vector, "tt")
            p.inc(nc.vector.tensor_copy(qTg[:, dqc * 512:(dqc + 1) * 512],
                                        p.ps[dqc % 2][:, 0:512]), "vv")
        p.w(nc.tensor, "vv")
        for t in range(4):
            for h in range(16):
                s_, hc2 = h % 2, h // 2
                lq = qTg[64 * s_:64 * s_ + 64,
                         hc2 * 512 + t * 128: hc2 * 512 + (t + 1) * 128]
                psc = p.ps[0][:, 0:256]
                mm = nc.tensor.matmul(
                    psc, lq, kcT2[64 * s_:64 * s_ + 64, hc2 * 256:(hc2 + 1) * 256],
                    start=True, stop=True)
                p.inc(mm, "tt")
                p.w(nc.vector, "tt")
                nm = p.stats[:, 0:1]
                p.inc(nc.vector.reduce_max(nm, psc, axis=AXX, negate=True), "vv")
                p.w(nc.vector, "vv")
                p.inc(nc.vector.tensor_scalar_mul(nm, nm, 0.125), "vv")
                p.w(nc.scalar, "vv")
                p.inc(nc.scalar.activation(probsc, psc, ACTF.Exp,
                                           scale=0.125, bias=nm,
                                           accum_out=p.stats[:, 1:2]), "ss")
                p.w(nc.vector, "ss")
                p.inc(nc.vector.reciprocal(p.stats[:, 2:3], p.stats[:, 1:2]), "vv")
                p.w(nc.vector, "vv")
                p.inc(nc.vector.tensor_scalar(probsc, probsc, p.stats[:, 2:3],
                                              None, op0=ALU.mult), "vv")
                p.w(nc.tensor, "vv")
                prs = [(probsc[:, u * 128:(u + 1) * 128],
                        pTc[:, u * 128:(u + 1) * 128]) for u in range(2)]
                p.transposes(prs)
                p.w(nc.tensor, "vv")
                for u in range(2):
                    mm = nc.tensor.matmul(
                        p.ps[1][64 * s_:64 * s_ + 64, 0:128],
                        vc2[:, u * D + h * 64: u * D + h * 64 + 64],
                        pTc[:, u * 128:(u + 1) * 128],
                        start=(u == 0), stop=(u == 1))
                p.inc(mm, "tt")
                if s_ == 1:
                    p.w(nc.vector, "tt")
                    p.inc(nc.vector.tensor_copy(
                        attTc[:, hc2 * 128:(hc2 + 1) * 128],
                        p.ps[1][:, 0:128]), "vv")
                    p.w(nc.tensor, "vv")
            p.w(nc.tensor, "vv")
            for dh in range(2):
                pdst = p.ps[2][:, dh * 512:(dh + 1) * 512]
                for dvc in range(8):
                    mm = nc.tensor.matmul(
                        pdst, attTc[:, dvc * 128:(dvc + 1) * 128],
                        wob[:, dvc * D + dh * 512: dvc * D + (dh + 1) * 512],
                        start=(dvc == 0), stop=(dvc == 7))
            p.inc(mm, "tt")
            p.w(nc.sync, "vv")
            p.dma("ds", fstg2, hbm_tiles(spill, g * 4 + t))
            p.w(nc.vector, "tt")
            p.w(nc.vector, "ds")
            p.inc(nc.vector.tensor_add(xob2, xgc[:, t * D:(t + 1) * D], fstg2), "vv")
            p.w(nc.vector, "vv")
            p.inc(nc.vector.tensor_add(xob2, xob2, p.ps[2]), "vv")
            p.w(nc.sync, "vv")
            p.w(nc.tensor, "vv")
            p.dma("ds", hbm_tiles(dst, g * 4 + t), xob2)
    p.quiesce()





def make_toeplitz(filt):
    pad = np.zeros(127 + 2048 + 128, dtype=np.float32)
    pad[127:127 + 2048] = filt
    o = np.arange(17)[:, None, None]
    pq = np.arange(128)
    idx = 127 + 128 * o + pq[None, :, None] - pq[None, None, :]
    return np.ascontiguousarray(pad[idx])


_CACHE = {}


def kernel(**inp):
    key = tuple(ACTIVE)
    if _CACHE.get("key") != key:
        _CACHE["prog"] = build()
        _CACHE["key"] = key
    nc, es, names = _CACHE["prog"]
    x = np.ascontiguousarray(np.asarray(inp["L0_attn_x"], np.float32))
    base = {}
    for n in names:
        if n == "x0":
            continue
        if n == "toep":
            base[n] = make_toeplitz(np.asarray(inp["L0_attn_filt"], np.float32))
        else:
            base[n] = np.ascontiguousarray(np.asarray(inp[n], np.float32))
    in_maps = []
    for c in range(8):
        m = dict(base)
        m["x0"] = x[c]
        in_maps.append(m)
    res = bass_utils.run_bass_kernel_spmd(nc, in_maps, core_ids=list(range(8)))
    return np.stack([res.results[c]["out"] for c in range(8)], axis=0)
